# version 07
# baseline (speedup 1.0000x reference)
from contextlib import ExitStack
import numpy as np
import concourse.bass as bass
import concourse.mybir as mybir
from concourse.bass import IndirectOffsetOnAxis
from concourse.bass_utils import run_bass_kernel_spmd

AF = mybir.ActivationFunctionType
ALU = mybir.AluOpType
F32 = mybir.dt.float32
BF16 = mybir.dt.bfloat16
U32 = mybir.dt.uint32

DM, DI, DS, NB, T = 1024, 2048, 16, 4, 8192
TH = T // 2
TF = 1024
TFP = TF + 3
NFC = TH // TF
TC = 2048
NSC = T // TC
CTF = DI // 128
CTS = DI // 2 // 128
DT = DM // 128
NE, FF = 8, 4096
FTK = FF // 128
TFN = 256
NFN = TH // TFN
LNEPS = 1e-5
PAIRS = [[0, 1], [2, 3], [4, 5], [6, 7]]


def build_program(debug=False, detect_races=False, stop_after=None):
    nc = bass.Bass("TRN2", target_bir_lowering=False, debug=False,
                   detect_race_conditions=detect_races)

    def din(name, shape, dt=F32):
        return nc.dram_tensor(name, shape, dt, kind="ExternalInput").ap()

    def scr(name, shape, dt=F32):
        return nc.dram_tensor(name, shape, dt, kind="Internal").ap()

    x_pad = din("x_pad", [128, NFC * DT * TFP])
    inw = din("inw", [128, 8 * 8 * 512])
    dtw = din("dtw", [128, CTF * CTF * 128])
    outw = din("outw", [128, DT * CTF * 128])
    xpw = din("xpw", [DI, DS])
    dtb = din("dtb", [DI])
    convw = din("convw", [DI, 4])
    convb = din("convb", [DI])
    lng = din("lng", [DM])
    lnb = din("lnb", [DM])
    alog_h = din("alog_h", [DI // 2, DS])
    d_h = din("d_h", [DI // 2])
    wr = din("wr", [DM, NE])
    w1 = din("w1", [NE * DM, FF])
    w2 = din("w2", [NE * FF, DM])
    b1 = din("b1", [NE, FF])
    b2 = din("b2", [NE, DM])
    ident_in = din("ident_in", [128, 128])
    iota_in = din("iota_in", [128, 33])
    ygidx_in = din("ygidx_in", [128, 2 * CTF], U32)
    scidx_in = din("scidx_in", [128, NSC * 2 * CTS], U32)
    hmask = din("hmask", [128, 1])
    out_fm = nc.dram_tensor("out_fm", [DM, TH], F32, kind="ExternalOutput").ap()

    inw_b = scr("inw_b", [128, 8 * 8 * 512], BF16)
    dtw_b = scr("dtw_b", [128, CTF * CTF * 128], BF16)
    outw_b = scr("outw_b", [128, DT * CTF * 128], BF16)
    dx_in = scr("dx_in", [NFC * DI, TF])
    ux_in = scr("ux_in", [NFC * DI, TF], BF16)
    zx_in = scr("zx_in", [NFC * DI, TF], BF16)
    bmx_in = scr("bmx_in", [NFC * DS, TF])
    dx_out = scr("dx_out", [NFC * 2 * DI, TF])
    ux_out = scr("ux_out", [NFC * 2 * DI, TF], BF16)
    zx_out = scr("zx_out", [NFC * 2 * DI, TF], BF16)
    bmx_out = scr("bmx_out", [NFC * 2 * DS, TF])
    yg_in = scr("yg_in", [NSC * (DI // 2), TC], BF16)
    ygath = scr("ygath", [NSC * DI, TC], BF16)
    pool_in = scr("pool_in", [128, DT])
    pool_out = scr("pool_out", [128, DT])
    y0_scr = scr("y0_scr", [DM, TH])
    x1b_scr = scr("x1b_scr", [DM, TH], BF16)
    x1f_scr = scr("x1f_scr", [DM, TH])
    dbg = {}
    if debug:
        dbg["delta"] = nc.dram_tensor("dbg_delta", [NFC * DI, TF], F32, kind="ExternalOutput").ap()
        dbg["u"] = nc.dram_tensor("dbg_u", [NFC * DI, TF], BF16, kind="ExternalOutput").ap()
        dbg["bm"] = nc.dram_tensor("dbg_bm", [NFC * DS, TF], F32, kind="ExternalOutput").ap()
        dbg["yg"] = nc.dram_tensor("dbg_yg", [NSC * (DI // 2), TC], BF16, kind="ExternalOutput").ap()
        dbg["x1"] = nc.dram_tensor("dbg_x1", [DM, TH], F32, kind="ExternalOutput").ap()
        dbg["rout"] = nc.dram_tensor("dbg_rout", [1, 16], F32, kind="ExternalOutput").ap()
        dbg["scan0"] = nc.dram_tensor("dbg_scan0", [128, 8 * TC], F32, kind="ExternalOutput").ap()
        dbg["bmr"] = nc.dram_tensor("dbg_bmr", [128, TC + 64], F32, kind="ExternalOutput").ap()

    st = ExitStack()

    def sbuf(name, shape, dt, stk=None):
        return (stk or st).enter_context(nc.sbuf_tensor(name, shape, dt))

    def psum(name, shape, dt=F32, stk=None):
        return (stk or st).enter_context(nc.psum_tensor(name, shape, dt))

    class Sem:
        def __init__(self, name):
            self.h = st.enter_context(nc.semaphore(name))
            self.n = 0

        def inc(self, instr, k=1):
            instr.then_inc(self.h, k)
            self.n += k
            return self.n

    a_sb = sbuf("a_sb", [128, CTS * DS], F32)
    d_sb = sbuf("d_sb", [128, CTS], F32)
    cw_sb = sbuf("cw_sb", [128, CTF * 4], F32)
    cb_sb = sbuf("cb_sb", [128, CTF], F32)
    dtb_sb = sbuf("dtb_sb", [128, CTF], F32)
    lng_sb = sbuf("lng_sb", [128, DT], F32)
    lnb_sb = sbuf("lnb_sb", [128, DT], F32)
    wr_sb = sbuf("wr_sb", [128, DT * NE], F32)
    xpw_sb = sbuf("xpw_sb", [128, CTF * DS], BF16)
    ident_b = sbuf("ident_b", [128, 128], BF16)
    ident_f = sbuf("ident_f", [128, 128], F32)
    ones_row_b = sbuf("ones_row_b", [1, 128], BF16)
    ones_row_f = sbuf("ones_row_f", [1, 128], F32)
    ones_col_f = sbuf("ones_col_f", [128, 1], F32)
    ones_tfn_b = sbuf("ones_tfn_b", [1, TFN], BF16)
    iota_sb = sbuf("iota_sb", [128, 33], F32)
    ygidx_sb = sbuf("ygidx_sb", [128, 2 * CTF], U32)
    scidx_sb = sbuf("scidx_sb", [128, NSC * 2 * CTS], U32)
    hmask_sb = sbuf("hmask_sb", [128, 1], F32)
    pool_cols = sbuf("pool_cols", [128, 4 * DT], F32)
    epsc = sbuf("epsc", [1, 2], F32)
    jnk_v = sbuf("jnk_v", [128, 1024], F32)
    jnk_a = sbuf("jnk_a", [128, 1024], F32)
    jnk_g = sbuf("jnk_g", [128, 512], F32)

    s_x = Sem("s_x")
    s_iw = [Sem("s_iw0"), Sem("s_iw1")]
    s_dw = [Sem("s_dw0"), Sem("s_dw1")]
    s_ow = [Sem("s_ow0"), Sem("s_ow1")]
    s_sp = Sem("s_sp")
    s_cc = Sem("s_cc")
    s_pe = Sem("s_pe")
    s_act = Sem("s_act")
    s_dve = Sem("s_dve")
    s_bar = Sem("s_bar")

    blk = st.enter_context(nc.Block())

    def vfence(v):
        v.wait_ge(s_dve.h, s_dve.n)
        ii = v.tensor_copy(jnk_v[:, 512:1024], jnk_v[:, 0:512])
        s_dve.inc(ii)
        return ii

    def afence(a):
        a.wait_ge(s_act.h, s_act.n)
        ii = a.activation(out=jnk_a[:, 512:1024], in_=jnk_a[:, 0:512], func=AF.Copy)
        s_act.inc(ii)
        return ii

    M = {}
    ln_last_rsev = {}

    def barrier():
        s_bar.n += 4
        tgt = s_bar.n
        pe_tgt = s_pe.n
        for dec in (blk.sync, blk.scalar, blk.vector, blk.gpsimd):
            @dec
            def _(e):
                e.sem_inc(s_bar.h, 1)
                e.wait_ge(s_bar.h, tgt)
                e.wait_ge(s_pe.h, pe_tgt)

        @blk.tensor
        def _(t):
            t.wait_ge(s_bar.h, tgt)


    class StopEmission(Exception):
        pass

    def maybe_stop(tag):
        if stop_after == tag:
            raise StopEmission()

    def emit_stop():
        @blk.vector
        def _(v):
            s_dve.inc(v.memset(jnk_v[:, 0:512], 3.0))
        done = s_dve.n

        @blk.gpsimd
        def _(gp):
            gp.wait_ge(s_dve.h, done)
            for dt in range(DT):
                for q in range(TH // 512):
                    s_sp.inc(gp.dma_start(out=out_fm[dt * 128:(dt + 1) * 128, q * 512:(q + 1) * 512],
                                          in_=jnk_v[:, 0:512]), 16)
            gp.wait_ge(s_sp.h, s_sp.n)
        barrier()

    @blk.sync
    def _(sp):
        with nc.allow_non_contiguous_dma("tiny constants"):
            for ct in range(CTS):
                s_x.inc(sp.dma_start(out=a_sb[:, ct * DS:(ct + 1) * DS],
                                     in_=alog_h[ct * 128:(ct + 1) * 128, :]), 16)
            s_x.inc(sp.dma_start(out=d_sb[:], in_=d_h.rearrange("(a p) -> p a", p=128)), 16)
            for ct in range(CTF):
                s_x.inc(sp.dma_start(out=cw_sb[:, ct * 4:(ct + 1) * 4],
                                     in_=convw[ct * 128:(ct + 1) * 128, :]), 16)
            s_x.inc(sp.dma_start(out=cb_sb[:], in_=convb.rearrange("(a p) -> p a", p=128)), 16)
            s_x.inc(sp.dma_start(out=dtb_sb[:], in_=dtb.rearrange("(a p) -> p a", p=128)), 16)
            s_x.inc(sp.dma_start(out=lng_sb[:], in_=lng.rearrange("(a p) -> p a", p=128)), 16)
            s_x.inc(sp.dma_start(out=lnb_sb[:], in_=lnb.rearrange("(a p) -> p a", p=128)), 16)
            for kt in range(DT):
                s_x.inc(sp.dma_start(out=wr_sb[:, kt * NE:(kt + 1) * NE],
                                     in_=wr[kt * 128:(kt + 1) * 128, :]), 16)
            s_x.inc(sp.dma_start(out=ident_f[:], in_=ident_in[:]), 16)
            s_x.inc(sp.dma_start(out=iota_sb[:], in_=iota_in[:]), 16)
            s_x.inc(sp.dma_start(out=ygidx_sb[:], in_=ygidx_in[:]), 16)
            s_x.inc(sp.dma_start(out=scidx_sb[:], in_=scidx_in[:]), 16)
            s_x.inc(sp.dma_start(out=hmask_sb[:], in_=hmask[:]), 16)
    p0_x = s_x.n

    @blk.gpsimd
    def _(g):
        g.memset(jnk_g[:], 0.0)
        for i in range(4):
            s_sp.inc(g.dma_start(out=inw_b[:, i * 8192:(i + 1) * 8192],
                                 in_=inw[:, i * 8192:(i + 1) * 8192]), 16)
        for i in range(4):
            s_sp.inc(g.dma_start(out=dtw_b[:, i * 8192:(i + 1) * 8192],
                                 in_=dtw[:, i * 8192:(i + 1) * 8192]), 16)
        for i in range(2):
            s_sp.inc(g.dma_start(out=outw_b[:, i * 8192:(i + 1) * 8192],
                                 in_=outw[:, i * 8192:(i + 1) * 8192]), 16)
        for kt in range(CTF):
            s_sp.inc(g.dma_start(out=xpw_sb[:, kt * DS:(kt + 1) * DS],
                                 in_=xpw[kt * 128:(kt + 1) * 128, :]), 16)
        g.wait_ge(s_sp.h, s_sp.n)

    @blk.scalar
    def _(a):
        s_act.inc(a.memzero(jnk_a[:]))
        a.wait_ge(s_x.h, p0_x)
        a.activation(out=a_sb[:], in_=a_sb[:], func=AF.Exp)
        afence(a)

    @blk.vector
    def _(v):
        s_dve.inc(v.memset(jnk_v[:], 0.0))
        v.wait_ge(s_act.h, s_act.n)
        s_dve.inc(v.tensor_scalar_mul(a_sb[:], a_sb[:], -1.0))
        s_dve.inc(v.tensor_copy(ident_b[:], ident_f[:]))
        s_dve.inc(v.memset(ones_row_b[:], 1.0))
        s_dve.inc(v.memset(ones_row_f[:], 1.0))
        s_dve.inc(v.memset(ones_col_f[:], 1.0))
        s_dve.inc(v.memset(ones_tfn_b[:], 1.0))
        s_dve.inc(v.memset(epsc[0:1, 0:1], LNEPS))
        s_dve.inc(v.memset(epsc[0:1, 1:2], -0.5))


    barrier()
    if stop_after == "p0":
        emit_stop()
        st.close()
        return nc

    def emit_ln_stats(tag, W, tiles, ps_stat, sq2, stat4, murep, rsrep,
                      gate_act=None):
        slabs = [(i, min(i + 512, W)) for i in range(0, W, 512)]
        key = id(ps_stat)
        prev_rsev = ln_last_rsev.get(key)

        def sq_slot(dt):
            return sq2[:, (dt % 2) * W:(dt % 2) * W + W]

        for dt in range(DT):
            @blk.scalar
            def _(a, dt=dt):
                if dt == 0 and gate_act is not None:
                    a.wait_ge(*gate_act)
                if (tag, "st", dt - 2) in M:
                    a.wait_ge(s_pe.h, M[(tag, "st", dt - 2)])
                a.activation(out=sq_slot(dt), in_=tiles(dt), func=AF.Square)
                afence(a)
                M[(tag, "sq", dt)] = s_act.n

            @blk.tensor
            def _(t, dt=dt):
                if dt == 0 and prev_rsev is not None:
                    t.wait_ge(s_dve.h, prev_rsev)
                t.wait_ge(s_act.h, M[(tag, "sq", dt)])
                for (lo, hi) in slabs:
                    t.matmul(ps_stat[0:1, lo:hi], ones_col_f[:], tiles(dt)[:, lo:hi],
                             start=(dt == 0), stop=(dt == DT - 1))
                for (lo, hi) in slabs:
                    mi_ = t.matmul(ps_stat[32:33, lo:hi], ones_col_f[:], sq_slot(dt)[:, lo:hi],
                             start=(dt == 0), stop=(dt == DT - 1))
                s_pe.inc(mi_)
                M[(tag, "st", dt)] = s_pe.n

        @blk.vector
        def _(v):
            v.wait_ge(s_pe.h, M[(tag, "st", DT - 1)])
            v.tensor_scalar_mul(stat4[0:1, 0:W], ps_stat[0:1, 0:W], 1.0 / DM)
            v.tensor_scalar_mul(stat4[0:1, W:2 * W], ps_stat[32:33, 0:W], 1.0 / DM)
            vfence(v)
            M[(tag, "muev")] = s_dve.n

        @blk.scalar
        def _(a):
            a.wait_ge(s_dve.h, M[(tag, "muev")])
            a.activation(out=stat4[0:1, 2 * W:3 * W], in_=stat4[0:1, 0:W],
                         func=AF.Square)
            afence(a)
            M[(tag, "mu2")] = s_act.n

        @blk.vector
        def _(v):
            v.wait_ge(s_act.h, M[(tag, "mu2")])
            v.tensor_sub(stat4[0:1, W:2 * W], stat4[0:1, W:2 * W],
                         stat4[0:1, 2 * W:3 * W])
            vfence(v)
            M[(tag, "var")] = s_dve.n

        @blk.scalar
        def _(a):
            a.wait_ge(s_dve.h, M[(tag, "var")])
            a.activation(out=stat4[0:1, 2 * W:3 * W], in_=stat4[0:1, W:2 * W],
                         func=AF.Ln, bias=epsc[0:1, 0:1])
            afence(a)
            a.activation(out=stat4[0:1, 3 * W:4 * W], in_=stat4[0:1, 2 * W:3 * W],
                         func=AF.Exp, scale=epsc[0:1, 1:2])
            afence(a)
            M[(tag, "rstd")] = s_act.n

        @blk.tensor
        def _(t):
            t.wait_ge(s_act.h, M[(tag, "rstd")])
            for (lo, hi) in slabs:
                mi_ = t.matmul(ps_stat[:, lo:hi], ones_row_f[:], stat4[0:1, lo:hi],
                         start=True, stop=True)
            s_pe.inc(mi_)
            M[(tag, "mubc")] = s_pe.n

        @blk.vector
        def _(v):
            v.wait_ge(s_pe.h, M[(tag, "mubc")])
            s_dve.inc(v.tensor_copy(murep[:, 0:W], ps_stat[:, 0:W]))
            M[(tag, "muget")] = s_dve.n

        @blk.tensor
        def _(t):
            t.wait_ge(s_dve.h, M[(tag, "muget")])
            for (lo, hi) in slabs:
                mi_ = t.matmul(ps_stat[:, lo:hi], ones_row_f[:], stat4[0:1, 3 * W + lo:3 * W + hi],
                         start=True, stop=True)
            s_pe.inc(mi_)
            M[(tag, "rsbc")] = s_pe.n

        @blk.vector
        def _(v):
            v.wait_ge(s_pe.h, M[(tag, "rsbc")])
            s_dve.inc(v.tensor_copy(rsrep[:, 0:W], ps_stat[:, 0:W]))
            M[(tag, "rsev")] = s_dve.n
            ln_last_rsev[key] = s_dve.n

    p1 = ExitStack()
    xt = sbuf("xt", [128, DT * TFP], F32, p1)
    sq2 = sbuf("sq2", [128, 2 * TFP], F32, p1)
    stat4 = sbuf("stat4", [1, 4 * TFP], F32, p1)
    murep = sbuf("murep", [128, TFP], F32, p1)
    rsrep = sbuf("rsrep", [128, TFP], F32, p1)
    xnb = sbuf("xnb", [128, DT * TFP], BF16, p1)
    wstr = sbuf("wstr", [128, 2 * 8 * 512], BF16, p1)
    uraw = sbuf("uraw", [128, 2 * TFP], F32, p1)
    ucv = sbuf("ucv", [128, 2 * TF], F32, p1)
    ub = sbuf("ub", [128, CTF * TF], BF16, p1)
    zb = sbuf("zb", [128, 2 * TF], BF16, p1)
    dwstr = sbuf("dwstr", [128, 2 * 16 * 128], BF16, p1)
    e1t = sbuf("e1t", [128, 2 * TF], F32, p1)
    delt = sbuf("delt", [128, 2 * TF], F32, p1)
    bmev = sbuf("bmev", [16, TF], F32, p1)
    ps_pool = [psum(f"ps_p{i}", [128, 512], stk=p1) for i in range(4)]
    ps_stat = psum("ps_stat", [128, 1536], stk=p1)
    ps_bm = psum("ps_bm", [16, 512], stk=p1)
    ps_evac = {}
    SLABS3 = [(0, 512), (512, 1024), (1024, TFP)]
    USLABS = [(3, 515), (515, 1027)]
    pcnt = [0]

    def ps_next():
        i = pcnt[0] % 4
        pcnt[0] += 1
        return i

    import contextlib as _ctl
    with _ctl.suppress(StopEmission):
      for j in range(NFC):
        xo = 0

        def XT(dt, xo=xo):
            return xt[:, xo + dt * TFP:xo + (dt + 1) * TFP]

        @blk.sync
        def _(sp, j=j, xo=xo):
            if j >= 1:
                sp.wait_ge(s_dve.h, M[("xn", j - 1)])
            M[("x", j)] = s_x.inc(sp.dma_start(
                out=xt[:, xo:xo + DT * TFP],
                in_=x_pad[:, j * DT * TFP:(j + 1) * DT * TFP]), 16)

        @blk.scalar
        def _(a, j=j):
            a.wait_ge(s_x.h, M[("x", j)])
            if j >= 1:
                a.wait_ge(s_dve.h, M[("xn", j - 1)])
        emit_ln_stats(("ln1", j), TFP, XT, ps_stat, sq2, stat4, murep, rsrep)
        maybe_stop(f"p1ln{j}")

        @blk.vector
        def _(v, j=j):
            v.wait_ge(s_dve.h, M[(("ln1", j), "rsev")])
            if j >= 1:
                v.wait_ge(s_pe.h, M[("inproj", j - 1, 7)])
            ii = None
            for dt in range(DT):
                v.tensor_sub(sq2[:, 0:TFP], XT(dt), murep[:, 0:TFP])
                v.scalar_tensor_tensor(sq2[:, TFP:2 * TFP], sq2[:, 0:TFP],
                                       lng_sb[:, dt:dt + 1], rsrep[:, 0:TFP], ALU.mult, ALU.mult)
                ii = v.tensor_scalar_add(xnb[:, dt * TFP:(dt + 1) * TFP],
                                         sq2[:, TFP:2 * TFP], lnb_sb[:, dt:dt + 1])
            if j == 0:
                for dt in range(DT):
                    v.tensor_scalar_mul(xnb[:, dt * TFP:dt * TFP + 3],
                                        xnb[:, dt * TFP:dt * TFP + 3], hmask_sb[:, 0:1])
                vfence(v)
                M[("xn", j)] = s_dve.n
            else:
                s_dve.inc(ii)
                M[("xn", j)] = s_dve.n

        maybe_stop(f"p1xn{j}")
        for grp in range(8):
            wb = (grp % 2) * 8 * 512

            @blk.sync
            def _(sp, j=j, grp=grp, wb=wb):
                gidx = j * 8 + grp
                if gidx >= 2:
                    sp.wait_ge(s_pe.h, M[("inproj", (gidx - 2) // 8, (gidx - 2) % 8)])
                sw = s_iw[gidx % 2]
                M[("inw", gidx)] = (sw.h, sw.inc(sp.dma_start(
                    out=wstr[:, wb:wb + 8 * 512],
                    in_=inw_b[:, grp * 4096:(grp + 1) * 4096]), 16))

            for mi in range(4):
                mt = grp * 4 + mi
                slabs_mt = SLABS3 if mt < CTF else USLABS
                for (lo, hi) in slabs_mt:
                    pi = ps_next()

                    @blk.tensor
                    def _(t, j=j, grp=grp, wb=wb, mi=mi, mt=mt, lo=lo, hi=hi, pi=pi):
                        gidx = j * 8 + grp
                        if mi == 0 and lo == SLABS3[0][0] or (mi == 0 and lo == slabs_mt[0][0]):
                            t.wait_ge(*M[("inw", gidx)])
                            t.wait_ge(s_dve.h, M[("xn", j)])
                        if pi in ps_evac:
                            t.wait_ge(*ps_evac[pi])
                        for kt in range(DT):
                            mi_ = t.matmul(ps_pool[pi][:, 0:hi - lo],
                                     wstr[:, wb + kt * 512 + mi * 128:wb + kt * 512 + mi * 128 + 128],
                                     xnb[:, kt * TFP + lo:kt * TFP + hi],
                                     start=(kt == 0), stop=(kt == DT - 1))
                        s_pe.inc(mi_)
                        M[("mm", j, mt, lo)] = (s_pe.n, pi)
                        if mi == 3 and (lo, hi) == slabs_mt[-1]:
                            M[("inproj", j, grp)] = s_pe.n

                    @blk.vector
                    def _(v, j=j, mt=mt, lo=lo, hi=hi):
                        cnt, pi2 = M[("mm", j, mt, lo)]
                        if mt < CTF:
                            ct = mt
                            if lo == 0:
                                if ct >= 2:
                                    v.wait_ge(s_act.h, M[("silu", j, ct - 2)])
                                elif j >= 1:
                                    v.wait_ge(s_act.h, M[("silu", j - 1, CTF - 2 + ct)])
                            dst = uraw[:, (ct % 2) * TFP + lo:(ct % 2) * TFP + hi]
                        else:
                            ct = mt - CTF
                            if lo == 3:
                                if ("spz", j, ct - 1) in M:
                                    v.wait_ge(s_sp.h, M[("spz", j, ct - 1)])
                                elif j >= 1 and ct == 0:
                                    v.wait_ge(s_sp.h, M[("spz", j - 1, CTF - 1)])
                            dst = zb[:, (ct % 2) * TF + lo - 3:(ct % 2) * TF + hi - 3]
                        v.wait_ge(s_pe.h, cnt)
                        ii = v.tensor_copy(dst, ps_pool[pi2][:, 0:hi - lo])
                        ps_evac[pi2] = (s_dve.h, s_dve.inc(ii))
                        if mt >= CTF and (lo, hi) == USLABS[-1]:
                            M[("zev", j, ct)] = s_dve.n

                if mt < CTF:
                    ct = mt

                    @blk.vector
                    def _(v, j=j, ct=ct, mt=mt):
                        ur = uraw[:, (ct % 2) * TFP:(ct % 2) * TFP + TFP]
                        uc = ucv[:, (ct % 2) * TF:(ct % 2) * TF + TF]
                        v.tensor_scalar_mul(uc, ur[:, 0:TF], cw_sb[:, ct * 4:ct * 4 + 1])
                        v.scalar_tensor_tensor(uc, ur[:, 1:1 + TF], cw_sb[:, ct * 4 + 1:ct * 4 + 2],
                                               uc, ALU.mult, ALU.add)
                        v.scalar_tensor_tensor(uc, ur[:, 2:2 + TF], cw_sb[:, ct * 4 + 2:ct * 4 + 3],
                                               uc, ALU.mult, ALU.add)
                        ii = v.scalar_tensor_tensor(uc, ur[:, 3:3 + TF], cw_sb[:, ct * 4 + 3:ct * 4 + 4],
                                                    uc, ALU.mult, ALU.add)
                        s_dve.inc(ii)
                        M[("conv", j, ct)] = s_dve.n

                    @blk.scalar
                    def _(a, j=j, ct=ct):
                        if j >= 1 and ct == 0:
                            a.wait_ge(s_pe.h, M[("dtmm", j - 1, CTF - 1)])
                            a.wait_ge(s_pe.h, M[("bm", j - 1, 1)])
                            a.wait_ge(s_sp.h, M[("spu", j - 1, CTF - 1)])
                        a.wait_ge(s_dve.h, M[("conv", j, ct)])
                        a.activation(out=ub[:, ct * TF:(ct + 1) * TF],
                                     in_=ucv[:, (ct % 2) * TF:(ct % 2) * TF + TF],
                                     func=AF.Silu, bias=cb_sb[:, ct:ct + 1])
                        afence(a)
                        M[("silu", j, ct)] = s_act.n

                    @blk.gpsimd
                    def _(g, j=j, ct=ct):
                        g.wait_ge(s_act.h, M[("silu", j, ct)])
                        M[("spu", j, ct)] = s_sp.inc(g.dma_start(
                            out=ux_in[j * DI + ct * 128:j * DI + (ct + 1) * 128, :],
                            in_=ub[:, ct * TF:(ct + 1) * TF]), 16)
                else:
                    ct = mt - CTF

                    @blk.gpsimd
                    def _(g, j=j, ct=ct):
                        g.wait_ge(s_dve.h, M[("zev", j, ct)])
                        M[("spz", j, ct)] = s_sp.inc(g.dma_start(
                            out=zx_in[j * DI + ct * 128:j * DI + (ct + 1) * 128, :],
                            in_=zb[:, (ct % 2) * TF:(ct % 2) * TF + TF]), 16)

        maybe_stop(f"p1ip{j}")
        for si, (lo, hi) in enumerate(USLABS):
            @blk.tensor
            def _(t, j=j, si=si, lo=lo, hi=hi):
                if si == 1:
                    t.wait_ge(s_dve.h, M[("bmev_s", j, 0)])
                elif j >= 1:
                    t.wait_ge(s_dve.h, M[("bmev_s", j - 1, 1)])
                for kt in range(CTF):
                    if si == 0:
                        t.wait_ge(s_act.h, M[("silu", j, kt)])
                    mi_ = t.matmul(ps_bm[:, 0:512], xpw_sb[:, kt * DS:(kt + 1) * DS],
                             ub[:, kt * TF + lo - 3:kt * TF + hi - 3],
                             start=(kt == 0), stop=(kt == CTF - 1))
                s_pe.inc(mi_)
                M[("bm", j, si)] = s_pe.n

            @blk.vector
            def _(v, j=j, si=si, lo=lo, hi=hi):
                if j >= 1 and si == 0:
                    v.wait_ge(s_sp.h, M[("spbm", j - 1)])
                v.wait_ge(s_pe.h, M[("bm", j, si)])
                ii = v.tensor_copy(bmev[:, lo - 3:hi - 3], ps_bm[:, 0:512])
                M[("bmev_s", j, si)] = s_dve.inc(ii)

        @blk.gpsimd
        def _(g, j=j):
            g.wait_ge(s_dve.h, M[("bmev_s", j, 1)])
            M[("spbm", j)] = s_sp.inc(g.dma_start(
                out=bmx_in[j * DS:(j + 1) * DS, :], in_=bmev[:]), 16)

        maybe_stop(f"p1bm{j}")
        for mt in range(CTF):
            wb = (mt % 2) * 16 * 128

            @blk.sync
            def _(sp, j=j, mt=mt, wb=wb):
                gidx = j * CTF + mt
                if gidx >= 2:
                    sp.wait_ge(s_pe.h, M[("dtmm", (gidx - 2) // CTF, (gidx - 2) % CTF)])
                sw = s_dw[gidx % 2]
                M[("dtw", gidx)] = (sw.h, sw.inc(sp.dma_start(
                    out=dwstr[:, wb:wb + 16 * 128],
                    in_=dtw_b[:, mt * 2048:(mt + 1) * 2048]), 16))

            @blk.tensor
            def _(t, j=j, mt=mt, wb=wb):
                t.wait_ge(*M[("dtw", j * CTF + mt)])
                for si, (lo, hi) in enumerate(USLABS):
                    pi = ps_next()
                    if pi in ps_evac:
                        t.wait_ge(*ps_evac[pi])
                    for kt in range(CTF):
                        if si == 0 and mt == 0:
                            t.wait_ge(s_act.h, M[("silu", j, kt)])
                        mi_ = t.matmul(ps_pool[pi][:, 0:512],
                                 dwstr[:, wb + kt * 128:wb + (kt + 1) * 128],
                                 ub[:, kt * TF + lo - 3:kt * TF + hi - 3],
                                 start=(kt == 0), stop=(kt == CTF - 1))
                    s_pe.inc(mi_)
                    M[("dtps", j, mt, si)] = (s_pe.n, pi)
                M[("dtmm", j, mt)] = s_pe.n

            @blk.scalar
            def _(a, j=j, mt=mt):
                dl = delt[:, (mt % 2) * TF:(mt % 2) * TF + TF]
                e1 = e1t[:, (mt % 2) * TF:(mt % 2) * TF + TF]
                if ("spdt", j, mt - 1) in M:
                    a.wait_ge(s_sp.h, M[("spdt", j, mt - 1)])
                elif j >= 1 and mt < 1:
                    a.wait_ge(s_sp.h, M[("spdt", j - 1, CTF - 1)])
                pis = []
                for si, (lo, hi) in enumerate(USLABS):
                    cnt, pi = M[("dtps", j, mt, si)]
                    a.wait_ge(s_pe.h, cnt)
                    ii = a.activation(out=e1[:, lo - 3:hi - 3], in_=ps_pool[pi][:, 0:512],
                                      func=AF.Exp, bias=dtb_sb[:, mt:mt + 1])
                    s_act.inc(ii)
                    pis.append(pi)
                a.activation(out=dl, in_=e1, func=AF.Ln, bias=1.0)
                afence(a)
                M[("dtact", j, mt)] = s_act.n
                for pi in pis:
                    ps_evac[pi] = (s_act.h, M[("dtact", j, mt)])

            @blk.gpsimd
            def _(g, j=j, mt=mt):
                g.wait_ge(s_act.h, M[("dtact", j, mt)])
                M[("spdt", j, mt)] = s_sp.inc(g.dma_start(
                    out=dx_in[j * DI + mt * 128:j * DI + (mt + 1) * 128, :],
                    in_=delt[:, (mt % 2) * TF:(mt % 2) * TF + TF]), 16)

        maybe_stop(f"p1dt{j}")

        @blk.gpsimd
        def _(g, j=j):
            g.wait_ge(s_sp.h, s_sp.n)
            for (ti, to, R) in ((dx_in, dx_out, DI), (ux_in, ux_out, DI),
                                (zx_in, zx_out, DI), (bmx_in, bmx_out, DS)):
                s_cc.inc(g.collective_compute(
                    "AllGather", ALU.bypass, PAIRS,
                    [ti[j * R:(j + 1) * R, :].opt()],
                    [to[j * 2 * R:(j + 1) * 2 * R, :].opt()]))
            M[("a2a", j)] = s_cc.n
        maybe_stop(f"p1cc{j}")

    barrier()
    p1.close()
    if stop_after is not None and stop_after.startswith("p1"):
        emit_stop()
        st.close()
        return nc

    if debug:
        @blk.gpsimd
        def _(g):
            s_sp.inc(g.dma_start(out=dbg["delta"][:], in_=dx_in[:]), 16)
            s_sp.inc(g.dma_start(out=dbg["u"][:], in_=ux_in[:]), 16)
            s_sp.inc(g.dma_start(out=dbg["bm"][:], in_=bmx_in[:]), 16)
        barrier()

    p2 = ExitStack()
    dlt2 = sbuf("dlt2", [128, 2 * TC], F32, p2)
    ut2 = sbuf("ut2", [128, 2 * TC], BF16, p2)
    zt2 = sbuf("zt2", [128, 2 * TC], BF16, p2)
    udl = sbuf("udl", [128, TC], BF16, p2)
    bm2b = sbuf("bm2b", [1, 2 * TC], BF16, p2)
    bmrep = sbuf("bmrep", [128, DS * TC], BF16, p2)
    dat = sbuf("dat", [128, 2 * TC], F32, p2)
    wt2 = sbuf("wt2", [128, 2 * TC], BF16, p2)
    ht2 = sbuf("ht2", [128, 2 * TC], BF16, p2)
    carry = sbuf("carry", [128, CTS * DS], F32, p2)
    y2t = sbuf("y2t", [128, 2 * TC], BF16, p2)
    gst = sbuf("gst", [128, 2 * TC], BF16, p2)
    ygt2 = sbuf("ygt2", [128, 2 * TC], BF16, p2)
    ps_y = psum("ps_y", [128, TC], stk=p2)
    ps_e = psum("ps_e", [128, 1024], stk=p2)

    @blk.vector
    def _(v):
        s_dve.inc(v.memset(carry[:], 0.0))

    for g in range(NSC):
        rh = g // 2
        c0 = (g % 2) * TC

        for s in range(DS):
            @blk.gpsimd
            def _(gp, g=g, rh=rh, c0=c0, s=s):
                if g == 0 and s == 0:
                    gp.wait_ge(s_cc.h, M[("a2a", NFC - 1)])
                if ("bmbc", g, (s - 2) * 4 + 3) in M:
                    gp.wait_ge(s_pe.h, M[("bmbc", g, (s - 2) * 4 + 3)])
                elif ("bmbc", g - 1, (DS - 2 + s) * 4 + 3) in M:
                    gp.wait_ge(s_pe.h, M[("bmbc", g - 1, (DS - 2 + s) * 4 + 3)])
                if g >= 1 and s == 0:
                    gp.wait_ge(s_dve.h, M[("wld", (g - 1) * CTS + CTS - 1, DS - 1)])
                for jsub in range(2):
                    j = (g % 2) * 2 + jsub
                    r = j * 2 * DS + rh * DS + s
                    M[("bm2", g, s)] = s_sp.inc(gp.dma_start(
                        out=bm2b[0:1, (s % 2) * TC + jsub * TF:(s % 2) * TC + (jsub + 1) * TF],
                        in_=bmx_out[r:r + 1, :]), 16)

            for si in range(4):
                @blk.tensor
                def _(t, g=g, s=s, si=si):
                    if si == 0:
                        t.wait_ge(s_sp.h, M[("bm2", g, s)])
                    idx = s * 4 + si
                    half = idx % 2
                    if (("bmrd", g, idx - 1)) in M:
                        t.wait_ge(s_act.h, M[("bmrd", g, idx - 1)])
                    elif ("bmrd", g - 1, DS * 4 - 1 + idx) in M:
                        t.wait_ge(s_act.h, M[("bmrd", g - 1, DS * 4 - 1 + idx)])
                    mi_ = t.matmul(ps_e[:, half * 512:half * 512 + 512], ones_row_b[:],
                             bm2b[0:1, (s % 2) * TC + si * 512:(s % 2) * TC + (si + 1) * 512],
                             start=True, stop=True)
                    s_pe.inc(mi_)
                    M[("bmbc", g, idx)] = s_pe.n

                @blk.scalar
                def _(a, g=g, s=s, si=si):
                    idx = s * 4 + si
                    half = idx % 2
                    a.wait_ge(s_pe.h, M[("bmbc", g, idx)])
                    ii = a.activation(out=bmrep[:, s * TC + si * 512:s * TC + (si + 1) * 512],
                                      in_=ps_e[:, half * 512:half * 512 + 512], func=AF.Copy)
                    s_act.inc(ii)
                    if si == 3:
                        afence(a)
                    M[("bmrd", g, idx)] = s_act.n

            if debug and g == 0 and s == 0:
                @blk.gpsimd
                def _(gp):
                    gp.wait_ge(s_act.h, M[("bmrd", 0, 3)])
                    s_sp.inc(gp.dma_start(out=dbg["bmr"][:, 0:TC], in_=bmrep[:, 0:TC]), 16)
                    M[("dbgbm",)] = s_sp.inc(gp.dma_start(out=dbg["bmr"][0:1, TC:TC + 64],
                                                          in_=bm2b[0:1, 448:512]), 16)

                @blk.tensor
                def _(t):
                    t.wait_ge(s_sp.h, M[("dbgbm",)])

                @blk.scalar
                def _(a):
                    a.wait_ge(s_sp.h, M[("dbgbm",)])

        for ct in range(CTS):
            i = g * CTS + ct
            sl = (i % 2) * TC

            @blk.gpsimd
            def _(gp, g=g, ct=ct, i=i, sl=sl, rh=rh, c0=c0):
                gp.wait_ge(s_cc.h, M[("a2a", NFC - 1)])
                if i >= 2:
                    gp.wait_ge(s_act.h, M[("dA", i - 2, DS - 1)])
                    gp.wait_ge(s_dve.h, M[("yg", i - 2)])
                for jsub in range(2):
                    icol = (g * 2 + jsub) * CTS + ct
                    iox = IndirectOffsetOnAxis(ap=scidx_sb[:, icol:icol + 1], axis=0)
                    s_sp.inc(gp.indirect_dma_start(
                        out=dlt2[:, sl + jsub * TF:sl + (jsub + 1) * TF], out_offset=None,
                        in_=dx_out[:, :], in_offset=iox), 16)
                    s_sp.inc(gp.indirect_dma_start(
                        out=ut2[:, sl + jsub * TF:sl + (jsub + 1) * TF], out_offset=None,
                        in_=ux_out[:, :], in_offset=iox), 16)
                    M[("ld2", i)] = s_sp.inc(gp.indirect_dma_start(
                        out=zt2[:, sl + jsub * TF:sl + (jsub + 1) * TF], out_offset=None,
                        in_=zx_out[:, :], in_offset=iox), 16)

            @blk.vector
            def _(v, g=g, ct=ct, i=i, sl=sl):
                v.wait_ge(s_sp.h, M[("ld2", i)])
                ii = v.tensor_mul(udl[:], ut2[:, sl:sl + TC], dlt2[:, sl:sl + TC])
                s_dve.inc(ii)
                M[("udl", i)] = s_dve.n

            if debug and i == 0:
                @blk.gpsimd
                def _(gp, sl=sl):
                    gp.wait_ge(s_dve.h, M[("udl", 0)])
                    s_sp.inc(gp.dma_start(out=dbg["scan0"][:, 6 * TC:7 * TC], in_=udl[:]), 16)
                    M[("dbgu",)] = s_sp.inc(gp.dma_start(
                        out=dbg["scan0"][:, 7 * TC:8 * TC], in_=dlt2[:, sl:sl + TC]), 16)

                @blk.vector
                def _(v):
                    v.wait_ge(s_sp.h, M[("dbgu",)])

            for s in range(DS):
                hs = (s % 2) * TC

                @blk.scalar
                def _(a, g=g, ct=ct, i=i, s=s, sl=sl, hs=hs):
                    if s == 0:
                        a.wait_ge(s_sp.h, M[("ld2", i)])
                    if ("scan", i, s - 2) in M:
                        a.wait_ge(s_dve.h, M[("scan", i, s - 2)])
                    elif ("scan", i - 1, DS - 2 + s) in M:
                        a.wait_ge(s_dve.h, M[("scan", i - 1, DS - 2 + s)])
                    a.activation(out=dat[:, hs:hs + TC], in_=dlt2[:, sl:sl + TC],
                                 func=AF.Exp, scale=a_sb[:, ct * DS + s:ct * DS + s + 1])
                    afence(a)
                    M[("dA", i, s)] = s_act.n

                @blk.vector
                def _(v, g=g, ct=ct, i=i, s=s, sl=sl, hs=hs):
                    v.wait_ge(s_dve.h, M[("udl", i)])
                    v.wait_ge(s_act.h, M[("bmrd", g, 4 * s + 3)])
                    if ("scan", i, s - 2) in M:
                        v.wait_ge(s_dve.h, M[("scan", i, s - 2)])
                    ii = v.tensor_mul(wt2[:, hs:hs + TC], udl[:],
                                      bmrep[:, s * TC:(s + 1) * TC])
                    M[("wld", i, s)] = s_dve.inc(ii)
                    if s >= 1:
                        v.tensor_copy(carry[:, ct * DS + s - 1:ct * DS + s],
                                      ht2[:, ((s - 1) % 2) * TC + TC - 1:((s - 1) % 2) * TC + TC])
                    v.wait_ge(s_act.h, M[("dA", i, s)])
                    if ("yred", i, s - 2) in M:
                        v.wait_ge(s_pe.h, M[("yred", i, s - 2)])
                    elif ("yred", i - 1, DS - 2 + s) in M:
                        v.wait_ge(s_pe.h, M[("yred", i - 1, DS - 2 + s)])
                    ii = v.tensor_tensor_scan(ht2[:, hs:hs + TC], dat[:, hs:hs + TC],
                                              wt2[:, hs:hs + TC],
                                              carry[:, ct * DS + s:ct * DS + s + 1],
                                              ALU.mult, ALU.add)
                    M[("scan", i, s)] = s_dve.inc(ii)

                if debug and i == 0 and s in (0, 1):
                    @blk.gpsimd
                    def _(gp, s=s, hs=hs, sl=sl):
                        gp.wait_ge(s_dve.h, M[("scan", 0, s)])
                        s_sp.inc(gp.dma_start(out=dbg["scan0"][:, (s * 3) * TC:(s * 3 + 1) * TC],
                                              in_=dat[:, hs:hs + TC]), 16)
                        s_sp.inc(gp.dma_start(out=dbg["scan0"][:, (s * 3 + 1) * TC:(s * 3 + 2) * TC],
                                              in_=wt2[:, hs:hs + TC]), 16)
                        M[("dbgs", s)] = s_sp.inc(gp.dma_start(
                            out=dbg["scan0"][:, (s * 3 + 2) * TC:(s * 3 + 3) * TC],
                            in_=ht2[:, hs:hs + TC]), 16)

                    @blk.vector
                    def _(v, s=s):
                        v.wait_ge(s_sp.h, M[("dbgs", s)])

                    @blk.scalar
                    def _(a, s=s):
                        a.wait_ge(s_sp.h, M[("dbgs", s)])

                @blk.tensor
                def _(t, g=g, ct=ct, i=i, s=s, hs=hs):
                    t.wait_ge(s_dve.h, M[("scan", i, s)])
                    if s == 0 and ("y2", i - 1) in M:
                        t.wait_ge(s_dve.h, M[("y2", i - 1)])
                    for si in range(4):
                        mi_ = t.matmul(ps_y[:, si * 512:(si + 1) * 512], ident_b[:],
                                 ht2[:, hs + si * 512:hs + (si + 1) * 512],
                                 start=(s == 0), stop=(s == DS - 1))
                    s_pe.inc(mi_)
                    M[("yred", i, s)] = s_pe.n

            @blk.scalar
            def _(a, g=g, i=i, sl=sl):
                a.wait_ge(s_sp.h, M[("ld2", i)])
                if ("ygsp", i - 1) in M:
                    a.wait_ge(s_sp.h, M[("ygsp", i - 1)])
                a.activation(out=gst[:, sl:sl + TC], in_=zt2[:, sl:sl + TC], func=AF.Silu)
                afence(a)
                M[("gs", i)] = s_act.n

            @blk.vector
            def _(v, g=g, ct=ct, i=i, sl=sl):
                v.wait_ge(s_pe.h, M[("yred", i, DS - 1)])
                v.tensor_copy(carry[:, ct * DS + DS - 1:ct * DS + DS],
                              ht2[:, ((DS - 1) % 2) * TC + TC - 1:((DS - 1) % 2) * TC + TC])
                ii = v.scalar_tensor_tensor(y2t[:, sl:sl + TC], ut2[:, sl:sl + TC],
                                            d_sb[:, ct:ct + 1], ps_y[:, 0:TC],
                                            ALU.mult, ALU.add)
                M[("y2", i)] = s_dve.inc(ii)
                v.wait_ge(s_act.h, M[("gs", i)])
                if ("ygsp", i - 1) in M:
                    v.wait_ge(s_sp.h, M[("ygsp", i - 1)])
                ii = v.tensor_mul(ygt2[:, sl:sl + TC], y2t[:, sl:sl + TC], gst[:, sl:sl + TC])
                M[("yg", i)] = s_dve.inc(ii)

            @blk.gpsimd
            def _(gp, g=g, ct=ct, i=i, sl=sl):
                gp.wait_ge(s_dve.h, M[("yg", i)])
                M[("ygsp", i)] = s_sp.inc(gp.dma_start(
                    out=yg_in[g * (DI // 2) + ct * 128:g * (DI // 2) + (ct + 1) * 128, :],
                    in_=ygt2[:, sl:sl + TC]), 16)

        @blk.gpsimd
        def _(gp, g=g):
            gp.wait_ge(s_sp.h, M[("ygsp", g * CTS + CTS - 1)])
            s_cc.inc(gp.collective_compute(
                "AllGather", ALU.bypass, PAIRS,
                [yg_in[g * (DI // 2):(g + 1) * (DI // 2), :].opt()],
                [ygath[g * DI:(g + 1) * DI, :].opt()]))
            M[("agy", g)] = s_cc.n

    barrier()
    p2.close()
    if stop_after == "p2":
        emit_stop()
        st.close()
        return nc

    TCH = 1024
    p3 = ExitStack()
    ygld = sbuf("ygld", [128, CTF * TCH], BF16, p3)
    owstr = sbuf("owstr", [128, 2 * 16 * 128], BF16, p3)
    xres = sbuf("xres", [128, 2 * TCH], F32, p3)
    r2t = sbuf("r2t", [128, DT * TCH], F32, p3)
    sq3 = sbuf("sq3", [128, 2 * TCH], F32, p3)
    stat3 = sbuf("stat3", [1, 4 * TCH], F32, p3)
    mur3 = sbuf("mur3", [128, TCH], F32, p3)
    rsr3 = sbuf("rsr3", [128, TCH], F32, p3)
    x1ft = sbuf("x1ft", [128, 2 * TCH], F32, p3)
    x1bt = sbuf("x1bt", [128, 2 * TCH], BF16, p3)
    ps_o = [psum(f"ps_o{i}", [128, 512], stk=p3) for i in range(4)]
    ps_st3 = psum("ps_st3", [128, 1536], stk=p3)
    opcnt = [0]
    ops_evac = {}

    for lj in range(4):
        @blk.gpsimd
        def _(gp, lj=lj):
            gp.wait_ge(s_cc.h, M[("agy", NSC - 1)])
            for kt in range(CTF):
                if lj >= 1:
                    gp.wait_ge(s_pe.h, M[("opmmd", lj * DT - 1)])
                M[("ygld", lj, kt)] = s_sp.inc(gp.indirect_dma_start(
                    out=ygld[:, kt * TCH:(kt + 1) * TCH], out_offset=None,
                    in_=ygath[:, :],
                    in_offset=IndirectOffsetOnAxis(
                        ap=ygidx_sb[:, (lj // 2) * CTF + kt:(lj // 2) * CTF + kt + 1], axis=0),
                    element_offset=(lj % 2) * TCH), 16)

        for mt in range(DT):
            wb = (mt % 2) * 16 * 128
            gidx = lj * DT + mt

            @blk.sync
            def _(sp, lj=lj, mt=mt, wb=wb, gidx=gidx):
                if gidx >= 2:
                    sp.wait_ge(s_pe.h, M[("opmmd", gidx - 2)])
                sw = s_ow[gidx % 2]
                M[("ow", gidx)] = (sw.h, sw.inc(sp.dma_start(
                    out=owstr[:, wb:wb + 16 * 128],
                    in_=outw_b[:, mt * 2048:(mt + 1) * 2048]), 16))
                if gidx >= 2:
                    sp.wait_ge(s_dve.h, M[("r2", gidx - 2)])
                M[("xres", gidx)] = s_x.inc(sp.dma_start(
                    out=xres[:, (gidx % 2) * TCH:(gidx % 2) * TCH + TCH],
                    in_=x_pad[:, lj * DT * TFP + mt * TFP + 3:lj * DT * TFP + mt * TFP + 3 + TCH]), 16)

            @blk.tensor
            def _(t, lj=lj, mt=mt, wb=wb, gidx=gidx):
                t.wait_ge(*M[("ow", gidx)])
                t.wait_ge(s_sp.h, M[("ygld", lj, CTF - 1)])
                for si in range(2):
                    pi = opcnt[0] % 4
                    opcnt[0] += 1
                    if pi in ops_evac:
                        t.wait_ge(s_dve.h, ops_evac[pi])
                    for kt in range(CTF):
                        mi_ = t.matmul(ps_o[pi][:, 0:512],
                                 owstr[:, wb + kt * 128:wb + (kt + 1) * 128],
                                 ygld[:, kt * TCH + si * 512:kt * TCH + (si + 1) * 512],
                                 start=(kt == 0), stop=(kt == CTF - 1))
                    s_pe.inc(mi_)
                    M[("opmm", lj, mt, si)] = (s_pe.n, pi)
                M[("opmmd", gidx)] = s_pe.n

            @blk.vector
            def _(v, lj=lj, mt=mt, gidx=gidx):
                v.wait_ge(s_x.h, M[("xres", gidx)])
                for si in range(2):
                    cnt, pi = M[("opmm", lj, mt, si)]
                    v.wait_ge(s_pe.h, cnt)
                    ii = v.scalar_tensor_tensor(
                        r2t[:, mt * TCH + si * 512:mt * TCH + (si + 1) * 512],
                        xres[:, (gidx % 2) * TCH + si * 512:(gidx % 2) * TCH + (si + 1) * 512],
                        2.0, ps_o[pi][:, 0:512], ALU.mult, ALU.add)
                    ops_evac[pi] = s_dve.inc(ii)
                M[("r2", gidx)] = s_dve.n

        emit_ln_stats(("ln2", lj), TCH, lambda dt: r2t[:, dt * TCH:(dt + 1) * TCH],
                      ps_st3, sq3, stat3, mur3, rsr3,
                      gate_act=(s_dve.h, M[("r2", lj * DT + DT - 1)]))

        for dt in range(DT):
            @blk.vector
            def _(v, lj=lj, dt=dt):
                if dt == 0:
                    v.wait_ge(s_dve.h, M[(("ln2", lj), "rsev")])
                if ("x1sp", lj, dt - 1) in M:
                    v.wait_ge(s_sp.h, M[("x1sp", lj, dt - 1)])
                elif lj >= 1 and dt == 0:
                    v.wait_ge(s_sp.h, M[("x1sp", lj - 1, DT - 1)])
                xo = (dt % 2) * TCH
                v.tensor_sub(sq3[:, 0:TCH], r2t[:, dt * TCH:(dt + 1) * TCH], mur3[:])
                ii = v.tensor_mul(x1ft[:, xo:xo + TCH], sq3[:, 0:TCH], rsr3[:])
                s_dve.inc(ii)
                ii = v.tensor_copy(x1bt[:, xo:xo + TCH], x1ft[:, xo:xo + TCH])
                s_dve.inc(ii)
                ii = v.reduce_sum(pool_cols[:, lj * DT + dt:lj * DT + dt + 1],
                                  x1ft[:, xo:xo + TCH], axis=mybir.AxisListType.X)
                s_dve.inc(ii)
                M[("x1", lj, dt)] = s_dve.n

            @blk.gpsimd
            def _(gp, lj=lj, dt=dt):
                gp.wait_ge(s_dve.h, M[("x1", lj, dt)])
                xo = (dt % 2) * TCH
                s_sp.inc(gp.dma_start(
                    out=x1f_scr[dt * 128:(dt + 1) * 128, lj * TCH:(lj + 1) * TCH],
                    in_=x1ft[:, xo:xo + TCH]), 16)
                M[("x1sp", lj, dt)] = s_sp.inc(gp.dma_start(
                    out=x1b_scr[dt * 128:(dt + 1) * 128, lj * TCH:(lj + 1) * TCH],
                    in_=x1bt[:, xo:xo + TCH]), 16)

    barrier()
    p3.close()
    if stop_after == "p25":
        emit_stop()
        st.close()
        return nc

    p4 = ExitStack()
    pooled_sb = sbuf("pooled_sb", [128, DT], F32, p4)
    lg_sb = sbuf("lg_sb", [1, 8], F32, p4)
    m8_sb = sbuf("m8_sb", [1, 8], F32, p4)
    i8_sb = sbuf("i8_sb", [1, 8], U32, p4)
    i8f_sb = sbuf("i8f_sb", [1, 8], F32, p4)
    wpad = sbuf("wpad", [1, 3 * 128], F32, p4)
    ebc = sbuf("ebc", [128, 2], F32, p4)
    wbc = sbuf("wbc", [128, 2], F32, p4)
    ebi = sbuf("ebi", [2, 2], U32, p4)
    w1idx = sbuf("w1idx", [128, 16], U32, p4)
    w2idx = sbuf("w2idx", [128, 64], U32, p4)
    idxf = sbuf("idxf", [128, 80], F32, p4)
    ps_r = psum("ps_r", [128, 512], stk=p4)

    @blk.vector
    def _(v):
        v.wait_ge(s_dve.h, M[("x1", 3, DT - 1)])
        v.tensor_add(pool_cols[:, 0:DT], pool_cols[:, 0:DT], pool_cols[:, DT:2 * DT])
        v.tensor_add(pool_cols[:, 2 * DT:3 * DT], pool_cols[:, 2 * DT:3 * DT],
                     pool_cols[:, 3 * DT:4 * DT])
        vfence(v)
        v.tensor_add(pool_cols[:, 0:DT], pool_cols[:, 0:DT], pool_cols[:, 2 * DT:3 * DT])
        vfence(v)
        v.tensor_scalar_mul(pooled_sb[:], pool_cols[:, 0:DT], 1.0 / float(T))
        vfence(v)
        M["poolloc"] = s_dve.n

    @blk.gpsimd
    def _(gp):
        gp.wait_ge(s_dve.h, M["poolloc"])
        s_sp.inc(gp.dma_start(out=pool_in[:], in_=pooled_sb[:]), 16)
        gp.wait_ge(s_sp.h, s_sp.n)
        s_cc.inc(gp.collective_compute("AllReduce", ALU.add, PAIRS,
                                       [pool_in.opt()], [pool_out.opt()]))
        gp.wait_ge(s_cc.h, s_cc.n)
        M["poolcc"] = s_sp.inc(gp.dma_start(out=pooled_sb[:], in_=pool_out[:]), 16)

    @blk.tensor
    def _(t):
        t.wait_ge(s_sp.h, M["poolcc"])
        for kt in range(DT):
            mi_ = t.matmul(ps_r[0:1, 0:NE], pooled_sb[:, kt:kt + 1], wr_sb[:, kt * NE:(kt + 1) * NE],
                     start=(kt == 0), stop=(kt == DT - 1))
        s_pe.inc(mi_)
        M["lg"] = s_pe.n

    @blk.vector
    def _(v):
        v.wait_ge(s_pe.h, M["lg"])
        v.tensor_copy(lg_sb[:], ps_r[0:1, 0:NE])
        vfence(v)
        v.max(out=m8_sb[:], in_=lg_sb[:])
        vfence(v)
        v.max_index(out=i8_sb[:], in_max=m8_sb[:], in_values=lg_sb[:])
        vfence(v)
        v.tensor_copy(i8f_sb[:], i8_sb[:])
        vfence(v)
        v.tensor_sub(wpad[0:1, 0:128], m8_sb[0:1, 1:2].to_broadcast([1, 128]),
                     m8_sb[0:1, 0:1].to_broadcast([1, 128]))
        vfence(v)
        M["wdiff"] = s_dve.n

    @blk.scalar
    def _(a):
        a.wait_ge(s_dve.h, M["wdiff"])
        a.activation(out=wpad[0:1, 128:256], in_=wpad[0:1, 0:128], func=AF.Exp)
        afence(a)
        M["wexp"] = s_act.n

    @blk.vector
    def _(v):
        v.wait_ge(s_act.h, M["wexp"])
        v.tensor_scalar_add(wpad[0:1, 256:384], wpad[0:1, 128:256], 1.0)
        vfence(v)
        v.reciprocal(wpad[0:1, 0:128], wpad[0:1, 256:384])
        vfence(v)
        v.tensor_scalar(wpad[0:1, 128:256], wpad[0:1, 0:128], -1.0, 1.0,
                        ALU.mult, ALU.add)
        vfence(v)
        M["wgts"] = s_dve.n

    @blk.tensor
    def _(t):
        t.wait_ge(s_dve.h, M["wgts"])
        t.matmul(ps_r[:, 0:2], ones_row_f[:], i8f_sb[0:1, 0:2], start=True, stop=True)
        t.matmul(ps_r[:, 4:5], ones_row_f[:], wpad[0:1, 0:1], start=True, stop=True)
        mi_ = t.matmul(ps_r[:, 5:6], ones_row_f[:], wpad[0:1, 128:129], start=True, stop=True)
        s_pe.inc(mi_)
        M["ebcmm"] = s_pe.n

    @blk.vector
    def _(v):
        v.wait_ge(s_pe.h, M["ebcmm"])
        v.tensor_copy(ebc[:], ps_r[:, 0:2])
        v.tensor_copy(wbc[:], ps_r[:, 4:6])
        vfence(v)
        ii = None
        for slot in range(2):
            for kt in range(DT):
                ii = v.scalar_tensor_tensor(idxf[:, slot * 8 + kt:slot * 8 + kt + 1],
                                            ebc[:, slot:slot + 1], 1024.0,
                                            iota_sb[:, kt:kt + 1], ALU.mult, ALU.add)
        for slot in range(2):
            for kt2 in range(32):
                ii = v.scalar_tensor_tensor(idxf[:, 16 + slot * 32 + kt2:16 + slot * 32 + kt2 + 1],
                                            ebc[:, slot:slot + 1], 4096.0,
                                            iota_sb[:, kt2:kt2 + 1], ALU.mult, ALU.add)
        vfence(v)
        v.tensor_copy(w1idx[:], idxf[:, 0:16])
        v.tensor_copy(w2idx[:], idxf[:, 16:80])
        v.tensor_copy(ebi[:], ebc[0:2, 0:2])
        vfence(v)
        M["idx"] = s_dve.n

    if debug:
        @blk.gpsimd
        def _(gp):
            gp.wait_ge(s_dve.h, M["idx"])
            s_sp.inc(gp.dma_start(out=dbg["rout"][0:1, 0:8], in_=m8_sb[:]), 16)
            s_sp.inc(gp.dma_start(out=dbg["rout"][0:1, 8:16], in_=i8f_sb[:]), 16)
            s_sp.inc(gp.dma_start(out=dbg["yg"][:], in_=yg_in[:]), 16)
            s_sp.inc(gp.dma_start(out=dbg["x1"][:], in_=x1f_scr[:]), 16)

    barrier()
    if stop_after == "p3":
        p4.close()
        emit_stop()
        st.close()
        return nc

    w1sb = sbuf("w1sb", [128, DT * FF], BF16, p4)
    w2sb = sbuf("w2sb", [128, 32 * DM], BF16, p4)
    b1rb = sbuf("b1rb", [2, FF], BF16, p4)
    b2rb = sbuf("b2rb", [2, DM], BF16, p4)
    x1ld = sbuf("x1ld", [128, 2 * DT * TFN], BF16, p4)
    hb4 = sbuf("hb4", [128, FTK * TFN], BF16, p4)
    y0t = sbuf("y0t", [128, 2 * TFN], F32, p4)
    y0ld = sbuf("y0ld", [128, 2 * TFN], F32, p4)
    x1fld = sbuf("x1fld", [128, 2 * TFN], F32, p4)
    r3t = sbuf("r3t", [128, DT * TFN], F32, p4)
    sq4 = sbuf("sq4", [128, 2 * TFN], F32, p4)
    sq5 = sbuf("sq5", [128, 2 * TFN], F32, p4)
    sq6 = sbuf("sq6", [128, TFN], F32, p4)
    stat5 = sbuf("stat5", [1, 4 * TFN], F32, p4)
    mur4 = sbuf("mur4", [128, TFN], F32, p4)
    rsr4 = sbuf("rsr4", [128, TFN], F32, p4)
    outt = sbuf("outt", [128, 2 * TFN], F32, p4)
    ps_f = [psum(f"ps_f{i}", [128, 512], stk=p4) for i in range(4)]
    ps_st5 = psum("ps_st5", [128, 1536], stk=p4)

    for slot in range(2):
        @blk.gpsimd
        def _(gp, slot=slot):
            gp.wait_ge(s_dve.h, M["idx"])
            if slot == 1:
                gp.wait_ge(s_pe.h, M[("mm1d", 0, NFN - 1)])
                gp.wait_ge(s_pe.h, M[("mm2", 0, NFN - 1, DT - 1)])
            for kt in range(DT):
                M[("w1g", slot, kt)] = s_sp.inc(gp.indirect_dma_start(
                    out=w1sb[:, kt * FF:(kt + 1) * FF], out_offset=None,
                    in_=w1[:, :],
                    in_offset=IndirectOffsetOnAxis(
                        ap=w1idx[:, slot * 8 + kt:slot * 8 + kt + 1], axis=0)), 16)
            for kt2 in range(32):
                M[("w2g", slot, kt2)] = s_sp.inc(gp.indirect_dma_start(
                    out=w2sb[:, kt2 * DM:(kt2 + 1) * DM], out_offset=None,
                    in_=w2[:, :],
                    in_offset=IndirectOffsetOnAxis(
                        ap=w2idx[:, slot * 32 + kt2:slot * 32 + kt2 + 1], axis=0)), 16)
            M[("b1g", slot)] = s_sp.inc(gp.indirect_dma_start(
                out=b1rb[:], out_offset=None, in_=b1[:, :],
                in_offset=IndirectOffsetOnAxis(ap=ebi[0:2, slot:slot + 1], axis=0)), 16)
            M[("b2g", slot)] = s_sp.inc(gp.indirect_dma_start(
                out=b2rb[:], out_offset=None, in_=b2[:, :],
                in_offset=IndirectOffsetOnAxis(ap=ebi[0:2, slot:slot + 1], axis=0)), 16)

        for fc in range(NFN):
            gidx = slot * NFN + fc

            @blk.sync
            def _(sp, slot=slot, fc=fc, gidx=gidx):
                sp.wait_ge(s_sp.h, M[("x1sp", 3, DT - 1)])
                if gidx >= 2:
                    sp.wait_ge(s_pe.h, M[("mm1d", (gidx - 2) // NFN, (gidx - 2) % NFN)])
                M[("x1ld", gidx)] = s_x.inc(sp.dma_start(
                    out=x1ld[:, (gidx % 2) * DT * TFN:(gidx % 2) * DT * TFN + DT * TFN].rearrange(
                        "p (a t) -> p a t", a=DT),
                    in_=x1b_scr[:, fc * TFN:(fc + 1) * TFN].rearrange(
                        "(a p) t -> p a t", p=128)), 16)

            for ft in range(FTK):
                @blk.tensor
                def _(t, slot=slot, fc=fc, gidx=gidx, ft=ft):
                    if ft == 0:
                        t.wait_ge(s_x.h, M[("x1ld", gidx)])
                        t.wait_ge(s_sp.h, M[("b2g", slot)])
                    xo = (gidx % 2) * DT * TFN
                    pi = ft % 2
                    if ("gelu", slot, fc, ft - 1) in M:
                        t.wait_ge(s_act.h, M[("gelu", slot, fc, ft - 1)])
                    elif ("gelu", slot, fc - 1, FTK - 1 + ft) in M:
                        t.wait_ge(s_act.h, M[("gelu", slot, fc - 1, FTK - 1 + ft)])
                    elif slot == 1 and ("gelu", 0, NFN - 1, FTK - 1 + ft) in M:
                        t.wait_ge(s_act.h, M[("gelu", 0, NFN - 1, FTK - 1 + ft)])
                    for kt in range(DT):
                        t.matmul(ps_f[pi][:, 0:TFN],
                                 w1sb[:, kt * FF + ft * 128:kt * FF + (ft + 1) * 128],
                                 x1ld[:, xo + kt * TFN:xo + (kt + 1) * TFN],
                                 start=(kt == 0), stop=False)
                    mi_ = t.matmul(ps_f[pi][:, 0:TFN], b1rb[0:1, ft * 128:(ft + 1) * 128],
                             ones_tfn_b[:], start=False, stop=True)
                    s_pe.inc(mi_)
                    M[("mm1", slot, fc, ft)] = (s_pe.n, pi)
                    if ft == FTK - 1:
                        M[("mm1d", slot, fc)] = s_pe.n

                @blk.scalar
                def _(a, slot=slot, fc=fc, ft=ft):
                    cnt, pi = M[("mm1", slot, fc, ft)]
                    a.wait_ge(s_pe.h, cnt)
                    ii = a.activation(out=hb4[:, ft * TFN:(ft + 1) * TFN],
                                      in_=ps_f[pi][:, 0:TFN], func=AF.Gelu)
                    s_act.inc(ii)
                    if ft == FTK - 1:
                        afence(a)
                    M[("gelu", slot, fc, ft)] = s_act.n

            for mt in range(DT):
                @blk.tensor
                def _(t, slot=slot, fc=fc, mt=mt):
                    if mt == 0:
                        t.wait_ge(s_act.h, M[("gelu", slot, fc, FTK - 1)])
                    pi = 2 + (mt % 2)
                    if ("ev3", slot, fc, mt - 2) in M:
                        t.wait_ge(s_dve.h, M[("ev3", slot, fc, mt - 2)])
                    elif ("ev3", slot, fc - 1, DT - 2 + mt) in M:
                        t.wait_ge(s_dve.h, M[("ev3", slot, fc - 1, DT - 2 + mt)])
                    elif slot == 1 and ("ev3", 0, NFN - 1, DT - 2 + mt) in M:
                        t.wait_ge(s_dve.h, M[("ev3", 0, NFN - 1, DT - 2 + mt)])
                    for ft in range(FTK):
                        t.matmul(ps_f[pi][:, 0:TFN],
                                 w2sb[:, ft * DM + mt * 128:ft * DM + (mt + 1) * 128],
                                 hb4[:, ft * TFN:(ft + 1) * TFN],
                                 start=(ft == 0), stop=False)
                    mi_ = t.matmul(ps_f[pi][:, 0:TFN], b2rb[0:1, mt * 128:(mt + 1) * 128],
                             ones_tfn_b[:], start=False, stop=True)
                    s_pe.inc(mi_)
                    M[("mm2", slot, fc, mt)] = s_pe.n

                if slot == 0:
                    @blk.vector
                    def _(v, fc=fc, mt=mt):
                        v.wait_ge(s_pe.h, M[("mm2", 0, fc, mt)])
                        if ("y0sp", fc, mt - 1) in M:
                            v.wait_ge(s_sp.h, M[("y0sp", fc, mt - 1)])
                        elif ("y0sp", fc - 1, DT - 1 + mt) in M:
                            v.wait_ge(s_sp.h, M[("y0sp", fc - 1, DT - 1 + mt)])
                        pi = 2 + (mt % 2)
                        ii = v.tensor_copy(y0t[:, (mt % 2) * TFN:(mt % 2) * TFN + TFN],
                                           ps_f[pi][:, 0:TFN])
                        s_dve.inc(ii)
                        M[("ev3", 0, fc, mt)] = s_dve.n

                    @blk.gpsimd
                    def _(gp, fc=fc, mt=mt):
                        gp.wait_ge(s_dve.h, M[("ev3", 0, fc, mt)])
                        M[("y0sp", fc, mt)] = s_sp.inc(gp.dma_start(
                            out=y0_scr[mt * 128:(mt + 1) * 128, fc * TFN:(fc + 1) * TFN],
                            in_=y0t[:, (mt % 2) * TFN:(mt % 2) * TFN + TFN]), 16)
                else:
                    @blk.sync
                    def _(sp, fc=fc, mt=mt):
                        sp.wait_ge(s_sp.h, M[("y0sp", fc, mt)])
                        if ("r3", fc, mt - 2) in M:
                            sp.wait_ge(s_dve.h, M[("r3", fc, mt - 2)])
                        elif ("r3", fc - 1, DT - 2 + mt) in M:
                            sp.wait_ge(s_dve.h, M[("r3", fc - 1, DT - 2 + mt)])
                        M[("y0l2", fc, mt)] = s_x.inc(sp.dma_start(
                            out=y0ld[:, (mt % 2) * TFN:(mt % 2) * TFN + TFN],
                            in_=y0_scr[mt * 128:(mt + 1) * 128, fc * TFN:(fc + 1) * TFN]), 16)
                        M[("x1fl", fc, mt)] = s_x.inc(sp.dma_start(
                            out=x1fld[:, (mt % 2) * TFN:(mt % 2) * TFN + TFN],
                            in_=x1f_scr[mt * 128:(mt + 1) * 128, fc * TFN:(fc + 1) * TFN]), 16)

                    @blk.vector
                    def _(v, fc=fc, mt=mt):
                        v.wait_ge(s_pe.h, M[("mm2", 1, fc, mt)])
                        v.wait_ge(s_x.h, M[("x1fl", fc, mt)])
                        pi = 2 + (mt % 2)
                        mo = (mt % 2) * TFN
                        v.tensor_scalar_mul(sq5[:, 0:TFN], ps_f[pi][:, 0:TFN], wbc[:, 1:2])
                        v.scalar_tensor_tensor(sq5[:, TFN:2 * TFN], y0ld[:, mo:mo + TFN],
                                               wbc[:, 0:1], sq5[:, 0:TFN], ALU.mult, ALU.add)
                        ii = v.tensor_add(r3t[:, mt * TFN:(mt + 1) * TFN],
                                          sq5[:, TFN:2 * TFN], x1fld[:, mo:mo + TFN])
                        s_dve.inc(ii)
                        M[("ev3", 1, fc, mt)] = s_dve.n
                        M[("r3", fc, mt)] = s_dve.n

            if slot == 1:
                emit_ln_stats(("ln3", fc), TFN, lambda dt: r3t[:, dt * TFN:(dt + 1) * TFN],
                              ps_st5, sq4, stat5, mur4, rsr4,
                              gate_act=(s_dve.h, M[("r3", fc, DT - 1)]))

                for dt in range(DT):
                    @blk.vector
                    def _(v, fc=fc, dt=dt):
                        if dt == 0:
                            v.wait_ge(s_dve.h, M[(("ln3", fc), "rsev")])
                        if ("osp", fc, dt - 1) in M:
                            v.wait_ge(s_sp.h, M[("osp", fc, dt - 1)])
                        elif ("osp", fc - 1, DT - 1 + dt) in M:
                            v.wait_ge(s_sp.h, M[("osp", fc - 1, DT - 1 + dt)])
                        v.tensor_sub(sq6[:, 0:TFN], r3t[:, dt * TFN:(dt + 1) * TFN], mur4[:])
                        ii = v.tensor_mul(outt[:, (dt % 2) * TFN:(dt % 2) * TFN + TFN],
                                          sq6[:, 0:TFN], rsr4[:])
                        s_dve.inc(ii)
                        M[("oev", fc, dt)] = s_dve.n

                    @blk.gpsimd
                    def _(gp, fc=fc, dt=dt):
                        gp.wait_ge(s_dve.h, M[("oev", fc, dt)])
                        M[("osp", fc, dt)] = s_sp.inc(gp.dma_start(
                            out=out_fm[dt * 128:(dt + 1) * 128, fc * TFN:(fc + 1) * TFN],
                            in_=outt[:, (dt % 2) * TFN:(dt % 2) * TFN + TFN]), 16)

    @blk.gpsimd
    def _(gp):
        gp.wait_ge(s_sp.h, s_sp.n)

    barrier()
    p4.close()
    st.close()
    return nc


def _host_inputs(inp):
    x = np.asarray(inp["L0_attn_x"])
    maps = []
    ident = np.eye(128, dtype=np.float32)
    iota = (np.arange(33, dtype=np.float32)[None, :] * 128
            + np.arange(128, dtype=np.float32)[:, None]).astype(np.float32)
    def swz(w, mb):
        K, Mw = w.shape
        a = K // 128
        return np.ascontiguousarray(
            w.reshape(a, 128, Mw // mb, mb).transpose(1, 2, 0, 3).reshape(128, -1))

    shared = {
        "inw": swz(np.asarray(inp["L0_attn_in_proj_w"]), 512),
        "dtw": swz(np.asarray(inp["L0_attn_dt_proj_w"]), 128),
        "outw": swz(np.asarray(inp["L0_attn_out_proj_w"]), 128),
        "xpw": np.ascontiguousarray(np.asarray(inp["L0_attn_x_proj_w"])[:, DS:]),
        "dtb": np.asarray(inp["L0_attn_dt_proj_b"]),
        "convw": np.ascontiguousarray(np.asarray(inp["L0_attn_conv_w"])[:, 0, :]),
        "convb": np.asarray(inp["L0_attn_conv_b"]),
        "lng": np.asarray(inp["L0_attn_ln_gamma"]),
        "lnb": np.asarray(inp["L0_attn_ln_beta"]),
        "wr": np.asarray(inp["L0_ffn_Wr"]),
        "w1": np.asarray(inp["L0_ffn_W1"]).reshape(NE * DM, FF),
        "w2": np.asarray(inp["L0_ffn_W2"]).reshape(NE * FF, DM),
        "b1": np.asarray(inp["L0_ffn_b1"]),
        "b2": np.asarray(inp["L0_ffn_b2"]),
        "ident_in": ident,
        "iota_in": iota,
    }
    for k in range(8):
        b, par = k // 2, k % 2
        lo = par * TH
        xp0 = np.zeros((DM, TH + 3), np.float32)
        xp0[:, 3:] = x[b, lo:lo + TH, :].T
        if par == 1:
            xp0[:, 0:3] = x[b, lo - 3:lo, :].T
        xp = np.stack([xp0[:, j * TF:j * TF + TFP] for j in range(NFC)], axis=0)
        xp = np.ascontiguousarray(
            xp.reshape(NFC, DT, 128, TFP).transpose(2, 0, 1, 3).reshape(128, -1))
        ygidx = np.zeros((128, 2 * CTF), np.uint32)
        for lj in range(2):
            g = par * 2 + lj
            for kt in range(CTF):
                ygidx[:, lj * CTF + kt] = g * DI + kt * 128 + np.arange(128)
        scidx = np.zeros((128, NSC * 2 * CTS), np.uint32)
        for g in range(NSC):
            for jsub in range(2):
                j = (g % 2) * 2 + jsub
                for ct in range(CTS):
                    scidx[:, (g * 2 + jsub) * CTS + ct] = (j * 2 * DI + (g // 2) * DI
                                                           + par * (DI // 2) + ct * 128 + np.arange(128))
        m = dict(shared)
        m.update({
            "x_pad": xp,
            "alog_h": np.ascontiguousarray(np.asarray(inp["L0_attn_A_log"])[par * 1024:(par + 1) * 1024]),
            "d_h": np.ascontiguousarray(np.asarray(inp["L0_attn_D_param"])[par * 1024:(par + 1) * 1024]),
            "ygidx_in": ygidx,
            "scidx_in": scidx,
            "hmask": np.full((128, 1), 1.0 if par == 1 else 0.0, np.float32),
        })
        maps.append(m)
    return maps


def kernel(**inputs):
    nc = build_program(debug=False)
    in_maps = _host_inputs(inputs)
    res = run_bass_kernel_spmd(nc, in_maps, core_ids=list(range(8)))
    out = np.zeros((NB, T, DM), np.float32)
    for k in range(8):
        b, par = k // 2, k % 2
        out[b, par * TH:(par + 1) * TH, :] = res.results[k]["out_fm"].T
    return out


if __name__ == "__main__":
    nc = build_program(debug=False)
    n = sum(1 for _ in nc.all_instructions())
    print("built OK,", n, "instructions")


# version 08
# speedup vs baseline: 1.2457x; 1.2457x over previous
from contextlib import ExitStack
import numpy as np
import concourse.bass as bass
import concourse.mybir as mybir
from concourse.bass import IndirectOffsetOnAxis
from concourse.bass_utils import run_bass_kernel_spmd

AF = mybir.ActivationFunctionType
ALU = mybir.AluOpType
F32 = mybir.dt.float32
BF16 = mybir.dt.bfloat16
U32 = mybir.dt.uint32

DM, DI, DS, NB, T = 1024, 2048, 16, 4, 8192
TH = T // 2
TF = 1024
TFP = TF + 3
NFC = TH // TF
TC = 2048
NSC = T // TC
CTF = DI // 128
CTS = DI // 2 // 128
DT = DM // 128
NE, FF = 8, 4096
FTK = FF // 128
TFN = 256
NFN = TH // TFN
LNEPS = 1e-5
PAIRS = [[0, 1], [2, 3], [4, 5], [6, 7]]


def build_program(debug=False, detect_races=False, stop_after=None):
    nc = bass.Bass("TRN2", target_bir_lowering=False, debug=False,
                   detect_race_conditions=detect_races)

    def din(name, shape, dt=F32):
        return nc.dram_tensor(name, shape, dt, kind="ExternalInput").ap()

    def scr(name, shape, dt=F32):
        return nc.dram_tensor(name, shape, dt, kind="Internal").ap()

    x_pad = din("x_pad", [128, NFC * DT * TFP])
    inw = din("inw", [128, 8 * 8 * 512])
    dtw = din("dtw", [128, CTF * CTF * 128])
    outw = din("outw", [128, DT * CTF * 128])
    xpw = din("xpw", [DI, DS])
    dtb = din("dtb", [DI])
    convw = din("convw", [DI, 4])
    convb = din("convb", [DI])
    lng = din("lng", [DM])
    lnb = din("lnb", [DM])
    alog_h = din("alog_h", [DI // 2, DS])
    d_h = din("d_h", [DI // 2])
    wr = din("wr", [DM, NE])
    w1 = din("w1", [NE * DM, FF])
    w2 = din("w2", [NE * FF, DM])
    b1 = din("b1", [NE, FF])
    b2 = din("b2", [NE, DM])
    ident_in = din("ident_in", [128, 128])
    iota_in = din("iota_in", [128, 33])
    ygidx_in = din("ygidx_in", [128, 2 * CTF], U32)
    scidx_in = din("scidx_in", [128, NSC * 2 * CTS], U32)
    hmask = din("hmask", [128, 1])
    out_fm = nc.dram_tensor("out_fm", [DM, TH], F32, kind="ExternalOutput").ap()

    inw_b = scr("inw_b", [128, 8 * 8 * 512], BF16)
    dtw_b = scr("dtw_b", [128, CTF * CTF * 128], BF16)
    outw_b = scr("outw_b", [128, DT * CTF * 128], BF16)
    dx_in = scr("dx_in", [NFC * DI, TF])
    ux_in = scr("ux_in", [NFC * DI, TF], BF16)
    zx_in = scr("zx_in", [NFC * DI, TF], BF16)
    bmx_in = scr("bmx_in", [NFC * DS, TF])
    dx_out = scr("dx_out", [NFC * 2 * DI, TF])
    ux_out = scr("ux_out", [NFC * 2 * DI, TF], BF16)
    zx_out = scr("zx_out", [NFC * 2 * DI, TF], BF16)
    bmx_out = scr("bmx_out", [NFC * 2 * DS, TF])
    yg_in = scr("yg_in", [NSC * (DI // 2), TC], BF16)
    ygath = scr("ygath", [NSC * DI, TC], BF16)
    pool_in = scr("pool_in", [128, DT])
    pool_out = scr("pool_out", [128, DT])
    y0_scr = scr("y0_scr", [DM, TH])
    x1b_scr = scr("x1b_scr", [DM, TH], BF16)
    x1f_scr = scr("x1f_scr", [DM, TH])
    dbg = {}
    if debug:
        dbg["delta"] = nc.dram_tensor("dbg_delta", [NFC * DI, TF], F32, kind="ExternalOutput").ap()
        dbg["u"] = nc.dram_tensor("dbg_u", [NFC * DI, TF], BF16, kind="ExternalOutput").ap()
        dbg["bm"] = nc.dram_tensor("dbg_bm", [NFC * DS, TF], F32, kind="ExternalOutput").ap()
        dbg["yg"] = nc.dram_tensor("dbg_yg", [NSC * (DI // 2), TC], BF16, kind="ExternalOutput").ap()
        dbg["x1"] = nc.dram_tensor("dbg_x1", [DM, TH], F32, kind="ExternalOutput").ap()
        dbg["rout"] = nc.dram_tensor("dbg_rout", [1, 16], F32, kind="ExternalOutput").ap()
        dbg["scan0"] = nc.dram_tensor("dbg_scan0", [128, 8 * TC], F32, kind="ExternalOutput").ap()
        dbg["bmr"] = nc.dram_tensor("dbg_bmr", [128, TC + 64], F32, kind="ExternalOutput").ap()

    st = ExitStack()

    def sbuf(name, shape, dt, stk=None):
        return (stk or st).enter_context(nc.sbuf_tensor(name, shape, dt))

    def psum(name, shape, dt=F32, stk=None):
        return (stk or st).enter_context(nc.psum_tensor(name, shape, dt))

    class Sem:
        def __init__(self, name):
            self.h = st.enter_context(nc.semaphore(name))
            self.n = 0

        def inc(self, instr, k=1):
            instr.then_inc(self.h, k)
            self.n += k
            return self.n

    a_sb = sbuf("a_sb", [128, CTS * DS], F32)
    d_sb = sbuf("d_sb", [128, CTS], F32)
    cw_sb = sbuf("cw_sb", [128, CTF * 4], F32)
    cb_sb = sbuf("cb_sb", [128, CTF], F32)
    dtb_sb = sbuf("dtb_sb", [128, CTF], F32)
    lng_sb = sbuf("lng_sb", [128, DT], F32)
    lnb_sb = sbuf("lnb_sb", [128, DT], F32)
    wr_sb = sbuf("wr_sb", [128, DT * NE], F32)
    xpw_sb = sbuf("xpw_sb", [128, CTF * DS], BF16)
    ident_b = sbuf("ident_b", [128, 128], BF16)
    ident_f = sbuf("ident_f", [128, 128], F32)
    ones_row_b = sbuf("ones_row_b", [1, 128], BF16)
    ones_row_f = sbuf("ones_row_f", [1, 128], F32)
    ones_col_f = sbuf("ones_col_f", [128, 1], F32)
    ones_tfn_b = sbuf("ones_tfn_b", [1, TFN], BF16)
    iota_sb = sbuf("iota_sb", [128, 33], F32)
    ygidx_sb = sbuf("ygidx_sb", [128, 2 * CTF], U32)
    scidx_sb = sbuf("scidx_sb", [128, NSC * 2 * CTS], U32)
    hmask_sb = sbuf("hmask_sb", [128, 1], F32)
    pool_cols = sbuf("pool_cols", [128, 4 * DT], F32)
    epsc = sbuf("epsc", [1, 2], F32)
    jnk_v = sbuf("jnk_v", [128, 1024], F32)
    jnk_a = sbuf("jnk_a", [128, 1024], F32)
    jnk_g = sbuf("jnk_g", [128, 512], F32)

    s_x = Sem("s_x")
    s_iw = [Sem("s_iw0"), Sem("s_iw1")]
    s_dw = [Sem("s_dw0"), Sem("s_dw1")]
    s_ow = [Sem("s_ow0"), Sem("s_ow1")]
    s_sp = Sem("s_sp")
    s_cc = Sem("s_cc")
    s_pe = Sem("s_pe")
    s_act = Sem("s_act")
    s_dve = Sem("s_dve")
    s_bar = Sem("s_bar")

    blk = st.enter_context(nc.Block())

    def vfence(v):
        ii = v.tensor_copy(jnk_v[0:1, 1:2], jnk_v[0:1, 0:1])
        s_dve.inc(ii)
        return ii

    def afence(a):
        ii = a.activation(out=jnk_a[0:1, 1:2], in_=jnk_a[0:1, 0:1], func=AF.Copy)
        s_act.inc(ii)
        return ii

    M = {}
    ln_last_rsev = {}

    def barrier():
        s_bar.n += 4
        tgt = s_bar.n
        pe_tgt = s_pe.n
        for dec in (blk.sync, blk.scalar, blk.vector, blk.gpsimd):
            @dec
            def _(e):
                e.sem_inc(s_bar.h, 1)
                e.wait_ge(s_bar.h, tgt)
                e.wait_ge(s_pe.h, pe_tgt)

        @blk.tensor
        def _(t):
            t.wait_ge(s_bar.h, tgt)


    class StopEmission(Exception):
        pass

    def maybe_stop(tag):
        if stop_after == tag:
            raise StopEmission()

    def emit_stop():
        @blk.vector
        def _(v):
            s_dve.inc(v.memset(jnk_v[:, 0:512], 3.0))
        done = s_dve.n

        @blk.gpsimd
        def _(gp):
            gp.wait_ge(s_dve.h, done)
            for dt in range(DT):
                for q in range(TH // 512):
                    s_sp.inc(gp.dma_start(out=out_fm[dt * 128:(dt + 1) * 128, q * 512:(q + 1) * 512],
                                          in_=jnk_v[:, 0:512]), 16)
            gp.wait_ge(s_sp.h, s_sp.n)
        barrier()

    @blk.sync
    def _(sp):
        with nc.allow_non_contiguous_dma("tiny constants"):
            for ct in range(CTS):
                s_x.inc(sp.dma_start(out=a_sb[:, ct * DS:(ct + 1) * DS],
                                     in_=alog_h[ct * 128:(ct + 1) * 128, :]), 16)
            s_x.inc(sp.dma_start(out=d_sb[:], in_=d_h.rearrange("(a p) -> p a", p=128)), 16)
            for ct in range(CTF):
                s_x.inc(sp.dma_start(out=cw_sb[:, ct * 4:(ct + 1) * 4],
                                     in_=convw[ct * 128:(ct + 1) * 128, :]), 16)
            s_x.inc(sp.dma_start(out=cb_sb[:], in_=convb.rearrange("(a p) -> p a", p=128)), 16)
            s_x.inc(sp.dma_start(out=dtb_sb[:], in_=dtb.rearrange("(a p) -> p a", p=128)), 16)
            s_x.inc(sp.dma_start(out=lng_sb[:], in_=lng.rearrange("(a p) -> p a", p=128)), 16)
            s_x.inc(sp.dma_start(out=lnb_sb[:], in_=lnb.rearrange("(a p) -> p a", p=128)), 16)
            for kt in range(DT):
                s_x.inc(sp.dma_start(out=wr_sb[:, kt * NE:(kt + 1) * NE],
                                     in_=wr[kt * 128:(kt + 1) * 128, :]), 16)
            s_x.inc(sp.dma_start(out=ident_f[:], in_=ident_in[:]), 16)
            s_x.inc(sp.dma_start(out=iota_sb[:], in_=iota_in[:]), 16)
            s_x.inc(sp.dma_start(out=ygidx_sb[:], in_=ygidx_in[:]), 16)
            s_x.inc(sp.dma_start(out=scidx_sb[:], in_=scidx_in[:]), 16)
            s_x.inc(sp.dma_start(out=hmask_sb[:], in_=hmask[:]), 16)
    p0_x = s_x.n

    @blk.gpsimd
    def _(g):
        g.memset(jnk_g[:], 0.0)
        for i in range(4):
            s_sp.inc(g.dma_start(out=inw_b[:, i * 8192:(i + 1) * 8192],
                                 in_=inw[:, i * 8192:(i + 1) * 8192]), 16)
        for i in range(4):
            s_sp.inc(g.dma_start(out=dtw_b[:, i * 8192:(i + 1) * 8192],
                                 in_=dtw[:, i * 8192:(i + 1) * 8192]), 16)
        for i in range(2):
            s_sp.inc(g.dma_start(out=outw_b[:, i * 8192:(i + 1) * 8192],
                                 in_=outw[:, i * 8192:(i + 1) * 8192]), 16)
        for kt in range(CTF):
            s_sp.inc(g.dma_start(out=xpw_sb[:, kt * DS:(kt + 1) * DS],
                                 in_=xpw[kt * 128:(kt + 1) * 128, :]), 16)
        g.wait_ge(s_sp.h, s_sp.n)

    @blk.scalar
    def _(a):
        s_act.inc(a.memzero(jnk_a[:]))
        a.wait_ge(s_x.h, p0_x)
        a.activation(out=a_sb[:], in_=a_sb[:], func=AF.Exp)
        afence(a)

    @blk.vector
    def _(v):
        s_dve.inc(v.memset(jnk_v[:], 0.0))
        v.wait_ge(s_act.h, s_act.n)
        s_dve.inc(v.tensor_scalar_mul(a_sb[:], a_sb[:], -1.0))
        s_dve.inc(v.tensor_copy(ident_b[:], ident_f[:]))
        s_dve.inc(v.memset(ones_row_b[:], 1.0))
        s_dve.inc(v.memset(ones_row_f[:], 1.0))
        s_dve.inc(v.memset(ones_col_f[:], 1.0))
        s_dve.inc(v.memset(ones_tfn_b[:], 1.0))
        s_dve.inc(v.memset(epsc[0:1, 0:1], LNEPS))
        s_dve.inc(v.memset(epsc[0:1, 1:2], -0.5))


    barrier()
    if stop_after == "p0":
        emit_stop()
        st.close()
        return nc

    def emit_ln_stats(tag, W, tiles, ps_stat, sq2, stat4, murep, rsrep,
                      gate_act=None):
        slabs = [(i, min(i + 512, W)) for i in range(0, W, 512)]
        key = id(ps_stat)
        prev_rsev = ln_last_rsev.get(key)

        def sq_slot(dt):
            return sq2[:, (dt % 2) * W:(dt % 2) * W + W]

        for dt in range(DT):
            @blk.scalar
            def _(a, dt=dt):
                if dt == 0 and gate_act is not None:
                    a.wait_ge(*gate_act)
                if (tag, "st", dt - 2) in M:
                    a.wait_ge(s_pe.h, M[(tag, "st", dt - 2)])
                a.activation(out=sq_slot(dt), in_=tiles(dt), func=AF.Square)
                afence(a)
                M[(tag, "sq", dt)] = s_act.n

            @blk.tensor
            def _(t, dt=dt):
                if dt == 0 and prev_rsev is not None:
                    t.wait_ge(s_dve.h, prev_rsev)
                t.wait_ge(s_act.h, M[(tag, "sq", dt)])
                for (lo, hi) in slabs:
                    t.matmul(ps_stat[0:1, lo:hi], ones_col_f[:], tiles(dt)[:, lo:hi],
                             start=(dt == 0), stop=(dt == DT - 1))
                for (lo, hi) in slabs:
                    mi_ = t.matmul(ps_stat[32:33, lo:hi], ones_col_f[:], sq_slot(dt)[:, lo:hi],
                             start=(dt == 0), stop=(dt == DT - 1))
                s_pe.inc(mi_)
                M[(tag, "st", dt)] = s_pe.n

        @blk.vector
        def _(v):
            v.wait_ge(s_pe.h, M[(tag, "st", DT - 1)])
            v.tensor_scalar_mul(stat4[0:1, 0:W], ps_stat[0:1, 0:W], 1.0 / DM)
            v.tensor_scalar_mul(stat4[0:1, W:2 * W], ps_stat[32:33, 0:W], 1.0 / DM)
            vfence(v)
            M[(tag, "muev")] = s_dve.n

        @blk.scalar
        def _(a):
            a.wait_ge(s_dve.h, M[(tag, "muev")])
            a.activation(out=stat4[0:1, 2 * W:3 * W], in_=stat4[0:1, 0:W],
                         func=AF.Square)
            afence(a)
            M[(tag, "mu2")] = s_act.n

        @blk.vector
        def _(v):
            v.wait_ge(s_act.h, M[(tag, "mu2")])
            v.tensor_sub(stat4[0:1, W:2 * W], stat4[0:1, W:2 * W],
                         stat4[0:1, 2 * W:3 * W])
            vfence(v)
            M[(tag, "var")] = s_dve.n

        @blk.scalar
        def _(a):
            a.wait_ge(s_dve.h, M[(tag, "var")])
            a.activation(out=stat4[0:1, 2 * W:3 * W], in_=stat4[0:1, W:2 * W],
                         func=AF.Ln, bias=epsc[0:1, 0:1])
            afence(a)
            a.activation(out=stat4[0:1, 3 * W:4 * W], in_=stat4[0:1, 2 * W:3 * W],
                         func=AF.Exp, scale=epsc[0:1, 1:2])
            afence(a)
            M[(tag, "rstd")] = s_act.n

        @blk.tensor
        def _(t):
            t.wait_ge(s_act.h, M[(tag, "rstd")])
            for (lo, hi) in slabs:
                mi_ = t.matmul(ps_stat[:, lo:hi], ones_row_f[:], stat4[0:1, lo:hi],
                         start=True, stop=True)
            s_pe.inc(mi_)
            M[(tag, "mubc")] = s_pe.n

        @blk.vector
        def _(v):
            v.wait_ge(s_pe.h, M[(tag, "mubc")])
            s_dve.inc(v.tensor_copy(murep[:, 0:W], ps_stat[:, 0:W]))
            M[(tag, "muget")] = s_dve.n

        @blk.tensor
        def _(t):
            t.wait_ge(s_dve.h, M[(tag, "muget")])
            for (lo, hi) in slabs:
                mi_ = t.matmul(ps_stat[:, lo:hi], ones_row_f[:], stat4[0:1, 3 * W + lo:3 * W + hi],
                         start=True, stop=True)
            s_pe.inc(mi_)
            M[(tag, "rsbc")] = s_pe.n

        @blk.vector
        def _(v):
            v.wait_ge(s_pe.h, M[(tag, "rsbc")])
            s_dve.inc(v.tensor_copy(rsrep[:, 0:W], ps_stat[:, 0:W]))
            M[(tag, "rsev")] = s_dve.n
            ln_last_rsev[key] = s_dve.n

    p1 = ExitStack()
    xt = sbuf("xt", [128, DT * TFP], F32, p1)
    sq2 = sbuf("sq2", [128, 2 * TFP], F32, p1)
    stat4 = sbuf("stat4", [1, 4 * TFP], F32, p1)
    murep = sbuf("murep", [128, TFP], F32, p1)
    rsrep = sbuf("rsrep", [128, TFP], F32, p1)
    xnb = sbuf("xnb", [128, DT * TFP], BF16, p1)
    wstr = sbuf("wstr", [128, 2 * 8 * 512], BF16, p1)
    uraw = sbuf("uraw", [128, 2 * TFP], F32, p1)
    ucv = sbuf("ucv", [128, 2 * TF], F32, p1)
    ub = sbuf("ub", [128, CTF * TF], BF16, p1)
    zb = sbuf("zb", [128, 2 * TF], BF16, p1)
    dwstr = sbuf("dwstr", [128, 2 * 16 * 128], BF16, p1)
    e1t = sbuf("e1t", [128, 2 * TF], F32, p1)
    delt = sbuf("delt", [128, 2 * TF], F32, p1)
    bmev = sbuf("bmev", [16, TF], F32, p1)
    ps_pool = [psum(f"ps_p{i}", [128, 512], stk=p1) for i in range(4)]
    ps_stat = psum("ps_stat", [128, 1536], stk=p1)
    ps_bm = psum("ps_bm", [16, 512], stk=p1)
    ps_evac = {}
    SLABS3 = [(0, 512), (512, 1024), (1024, TFP)]
    USLABS = [(3, 515), (515, 1027)]
    pcnt = [0]

    def ps_next():
        i = pcnt[0] % 4
        pcnt[0] += 1
        return i

    import contextlib as _ctl
    with _ctl.suppress(StopEmission):
      for j in range(NFC):
        xo = 0

        def XT(dt, xo=xo):
            return xt[:, xo + dt * TFP:xo + (dt + 1) * TFP]

        @blk.sync
        def _(sp, j=j, xo=xo):
            if j >= 1:
                sp.wait_ge(s_dve.h, M[("xn", j - 1)])
            M[("x", j)] = s_x.inc(sp.dma_start(
                out=xt[:, xo:xo + DT * TFP],
                in_=x_pad[:, j * DT * TFP:(j + 1) * DT * TFP]), 16)

        @blk.scalar
        def _(a, j=j):
            a.wait_ge(s_x.h, M[("x", j)])
            if j >= 1:
                a.wait_ge(s_dve.h, M[("xn", j - 1)])
        emit_ln_stats(("ln1", j), TFP, XT, ps_stat, sq2, stat4, murep, rsrep)
        maybe_stop(f"p1ln{j}")

        @blk.vector
        def _(v, j=j):
            v.wait_ge(s_dve.h, M[(("ln1", j), "rsev")])
            if j >= 1:
                v.wait_ge(s_pe.h, M[("inproj", j - 1, 7)])
            ii = None
            for dt in range(DT):
                v.tensor_sub(sq2[:, 0:TFP], XT(dt), murep[:, 0:TFP])
                v.scalar_tensor_tensor(sq2[:, TFP:2 * TFP], sq2[:, 0:TFP],
                                       lng_sb[:, dt:dt + 1], rsrep[:, 0:TFP], ALU.mult, ALU.mult)
                ii = v.tensor_scalar_add(xnb[:, dt * TFP:(dt + 1) * TFP],
                                         sq2[:, TFP:2 * TFP], lnb_sb[:, dt:dt + 1])
            if j == 0:
                for dt in range(DT):
                    v.tensor_scalar_mul(xnb[:, dt * TFP:dt * TFP + 3],
                                        xnb[:, dt * TFP:dt * TFP + 3], hmask_sb[:, 0:1])
                vfence(v)
                M[("xn", j)] = s_dve.n
            else:
                s_dve.inc(ii)
                M[("xn", j)] = s_dve.n

        maybe_stop(f"p1xn{j}")
        for grp in range(8):
            wb = (grp % 2) * 8 * 512

            @blk.sync
            def _(sp, j=j, grp=grp, wb=wb):
                gidx = j * 8 + grp
                if gidx >= 2:
                    sp.wait_ge(s_pe.h, M[("inproj", (gidx - 2) // 8, (gidx - 2) % 8)])
                sw = s_iw[gidx % 2]
                M[("inw", gidx)] = (sw.h, sw.inc(sp.dma_start(
                    out=wstr[:, wb:wb + 8 * 512],
                    in_=inw_b[:, grp * 4096:(grp + 1) * 4096]), 16))

            for mi in range(4):
                mt = grp * 4 + mi
                slabs_mt = SLABS3 if mt < CTF else USLABS
                for (lo, hi) in slabs_mt:
                    pi = ps_next()

                    @blk.tensor
                    def _(t, j=j, grp=grp, wb=wb, mi=mi, mt=mt, lo=lo, hi=hi, pi=pi):
                        gidx = j * 8 + grp
                        if mi == 0 and lo == SLABS3[0][0] or (mi == 0 and lo == slabs_mt[0][0]):
                            t.wait_ge(*M[("inw", gidx)])
                            t.wait_ge(s_dve.h, M[("xn", j)])
                        if pi in ps_evac:
                            t.wait_ge(*ps_evac[pi])
                        for kt in range(DT):
                            mi_ = t.matmul(ps_pool[pi][:, 0:hi - lo],
                                     wstr[:, wb + kt * 512 + mi * 128:wb + kt * 512 + mi * 128 + 128],
                                     xnb[:, kt * TFP + lo:kt * TFP + hi],
                                     start=(kt == 0), stop=(kt == DT - 1))
                        s_pe.inc(mi_)
                        M[("mm", j, mt, lo)] = (s_pe.n, pi)
                        if mi == 3 and (lo, hi) == slabs_mt[-1]:
                            M[("inproj", j, grp)] = s_pe.n

                    @blk.vector
                    def _(v, j=j, mt=mt, lo=lo, hi=hi):
                        cnt, pi2 = M[("mm", j, mt, lo)]
                        if mt < CTF:
                            ct = mt
                            if lo == 0:
                                if ct >= 2:
                                    v.wait_ge(s_act.h, M[("silu", j, ct - 2)])
                                elif j >= 1:
                                    v.wait_ge(s_act.h, M[("silu", j - 1, CTF - 2 + ct)])
                            dst = uraw[:, (ct % 2) * TFP + lo:(ct % 2) * TFP + hi]
                        else:
                            ct = mt - CTF
                            if lo == 3:
                                if ("spz", j, ct - 1) in M:
                                    v.wait_ge(s_sp.h, M[("spz", j, ct - 1)])
                                elif j >= 1 and ct == 0:
                                    v.wait_ge(s_sp.h, M[("spz", j - 1, CTF - 1)])
                            dst = zb[:, (ct % 2) * TF + lo - 3:(ct % 2) * TF + hi - 3]
                        v.wait_ge(s_pe.h, cnt)
                        ii = v.tensor_copy(dst, ps_pool[pi2][:, 0:hi - lo])
                        ps_evac[pi2] = (s_dve.h, s_dve.inc(ii))
                        if mt >= CTF and (lo, hi) == USLABS[-1]:
                            M[("zev", j, ct)] = s_dve.n

                if mt < CTF:
                    ct = mt

                    @blk.vector
                    def _(v, j=j, ct=ct, mt=mt):
                        ur = uraw[:, (ct % 2) * TFP:(ct % 2) * TFP + TFP]
                        uc = ucv[:, (ct % 2) * TF:(ct % 2) * TF + TF]
                        v.tensor_scalar_mul(uc, ur[:, 0:TF], cw_sb[:, ct * 4:ct * 4 + 1])
                        v.scalar_tensor_tensor(uc, ur[:, 1:1 + TF], cw_sb[:, ct * 4 + 1:ct * 4 + 2],
                                               uc, ALU.mult, ALU.add)
                        v.scalar_tensor_tensor(uc, ur[:, 2:2 + TF], cw_sb[:, ct * 4 + 2:ct * 4 + 3],
                                               uc, ALU.mult, ALU.add)
                        ii = v.scalar_tensor_tensor(uc, ur[:, 3:3 + TF], cw_sb[:, ct * 4 + 3:ct * 4 + 4],
                                                    uc, ALU.mult, ALU.add)
                        s_dve.inc(ii)
                        M[("conv", j, ct)] = s_dve.n

                    @blk.scalar
                    def _(a, j=j, ct=ct):
                        if j >= 1 and ct == 0:
                            a.wait_ge(s_pe.h, M[("dtmm", j - 1, CTF - 1)])
                            a.wait_ge(s_pe.h, M[("bm", j - 1, 1)])
                            a.wait_ge(s_sp.h, M[("spu", j - 1, CTF - 1)])
                        a.wait_ge(s_dve.h, M[("conv", j, ct)])
                        a.activation(out=ub[:, ct * TF:(ct + 1) * TF],
                                     in_=ucv[:, (ct % 2) * TF:(ct % 2) * TF + TF],
                                     func=AF.Silu, bias=cb_sb[:, ct:ct + 1])
                        afence(a)
                        M[("silu", j, ct)] = s_act.n

                    @blk.gpsimd
                    def _(g, j=j, ct=ct):
                        g.wait_ge(s_act.h, M[("silu", j, ct)])
                        M[("spu", j, ct)] = s_sp.inc(g.dma_start(
                            out=ux_in[j * DI + ct * 128:j * DI + (ct + 1) * 128, :],
                            in_=ub[:, ct * TF:(ct + 1) * TF]), 16)
                else:
                    ct = mt - CTF

                    @blk.gpsimd
                    def _(g, j=j, ct=ct):
                        g.wait_ge(s_dve.h, M[("zev", j, ct)])
                        M[("spz", j, ct)] = s_sp.inc(g.dma_start(
                            out=zx_in[j * DI + ct * 128:j * DI + (ct + 1) * 128, :],
                            in_=zb[:, (ct % 2) * TF:(ct % 2) * TF + TF]), 16)

        maybe_stop(f"p1ip{j}")
        for si, (lo, hi) in enumerate(USLABS):
            @blk.tensor
            def _(t, j=j, si=si, lo=lo, hi=hi):
                if si == 1:
                    t.wait_ge(s_dve.h, M[("bmev_s", j, 0)])
                elif j >= 1:
                    t.wait_ge(s_dve.h, M[("bmev_s", j - 1, 1)])
                for kt in range(CTF):
                    if si == 0:
                        t.wait_ge(s_act.h, M[("silu", j, kt)])
                    mi_ = t.matmul(ps_bm[:, 0:512], xpw_sb[:, kt * DS:(kt + 1) * DS],
                             ub[:, kt * TF + lo - 3:kt * TF + hi - 3],
                             start=(kt == 0), stop=(kt == CTF - 1))
                s_pe.inc(mi_)
                M[("bm", j, si)] = s_pe.n

            @blk.vector
            def _(v, j=j, si=si, lo=lo, hi=hi):
                if j >= 1 and si == 0:
                    v.wait_ge(s_sp.h, M[("spbm", j - 1)])
                v.wait_ge(s_pe.h, M[("bm", j, si)])
                ii = v.tensor_copy(bmev[:, lo - 3:hi - 3], ps_bm[:, 0:512])
                M[("bmev_s", j, si)] = s_dve.inc(ii)

        @blk.gpsimd
        def _(g, j=j):
            g.wait_ge(s_dve.h, M[("bmev_s", j, 1)])
            M[("spbm", j)] = s_sp.inc(g.dma_start(
                out=bmx_in[j * DS:(j + 1) * DS, :], in_=bmev[:]), 16)

        maybe_stop(f"p1bm{j}")
        for mt in range(CTF):
            wb = (mt % 2) * 16 * 128

            @blk.sync
            def _(sp, j=j, mt=mt, wb=wb):
                gidx = j * CTF + mt
                if gidx >= 2:
                    sp.wait_ge(s_pe.h, M[("dtmm", (gidx - 2) // CTF, (gidx - 2) % CTF)])
                sw = s_dw[gidx % 2]
                M[("dtw", gidx)] = (sw.h, sw.inc(sp.dma_start(
                    out=dwstr[:, wb:wb + 16 * 128],
                    in_=dtw_b[:, mt * 2048:(mt + 1) * 2048]), 16))

            @blk.tensor
            def _(t, j=j, mt=mt, wb=wb):
                t.wait_ge(*M[("dtw", j * CTF + mt)])
                for si, (lo, hi) in enumerate(USLABS):
                    pi = ps_next()
                    if pi in ps_evac:
                        t.wait_ge(*ps_evac[pi])
                    for kt in range(CTF):
                        if si == 0 and mt == 0:
                            t.wait_ge(s_act.h, M[("silu", j, kt)])
                        mi_ = t.matmul(ps_pool[pi][:, 0:512],
                                 dwstr[:, wb + kt * 128:wb + (kt + 1) * 128],
                                 ub[:, kt * TF + lo - 3:kt * TF + hi - 3],
                                 start=(kt == 0), stop=(kt == CTF - 1))
                    s_pe.inc(mi_)
                    M[("dtps", j, mt, si)] = (s_pe.n, pi)
                M[("dtmm", j, mt)] = s_pe.n

            @blk.scalar
            def _(a, j=j, mt=mt):
                dl = delt[:, (mt % 2) * TF:(mt % 2) * TF + TF]
                e1 = e1t[:, (mt % 2) * TF:(mt % 2) * TF + TF]
                if ("spdt", j, mt - 1) in M:
                    a.wait_ge(s_sp.h, M[("spdt", j, mt - 1)])
                elif j >= 1 and mt < 1:
                    a.wait_ge(s_sp.h, M[("spdt", j - 1, CTF - 1)])
                for si, (lo, hi) in enumerate(USLABS):
                    cnt, pi = M[("dtps", j, mt, si)]
                    a.wait_ge(s_pe.h, cnt)
                    ii = a.activation(out=e1[:, lo - 3:hi - 3], in_=ps_pool[pi][:, 0:512],
                                      func=AF.Exp, bias=dtb_sb[:, mt:mt + 1])
                    ps_evac[pi] = (s_act.h, s_act.inc(ii))
                a.activation(out=dl, in_=e1, func=AF.Ln, bias=1.0)
                afence(a)
                M[("dtact", j, mt)] = s_act.n

            @blk.gpsimd
            def _(g, j=j, mt=mt):
                g.wait_ge(s_act.h, M[("dtact", j, mt)])
                M[("spdt", j, mt)] = s_sp.inc(g.dma_start(
                    out=dx_in[j * DI + mt * 128:j * DI + (mt + 1) * 128, :],
                    in_=delt[:, (mt % 2) * TF:(mt % 2) * TF + TF]), 16)

        maybe_stop(f"p1dt{j}")

        @blk.gpsimd
        def _(g, j=j):
            g.wait_ge(s_sp.h, s_sp.n)
            for (ti, to, R) in ((dx_in, dx_out, DI), (ux_in, ux_out, DI),
                                (zx_in, zx_out, DI), (bmx_in, bmx_out, DS)):
                s_cc.inc(g.collective_compute(
                    "AllGather", ALU.bypass, PAIRS,
                    [ti[j * R:(j + 1) * R, :].opt()],
                    [to[j * 2 * R:(j + 1) * 2 * R, :].opt()]))
            M[("a2a", j)] = s_cc.n
        maybe_stop(f"p1cc{j}")

    barrier()
    p1.close()
    if stop_after is not None and stop_after.startswith("p1"):
        emit_stop()
        st.close()
        return nc

    if debug:
        @blk.gpsimd
        def _(g):
            s_sp.inc(g.dma_start(out=dbg["delta"][:], in_=dx_in[:]), 16)
            s_sp.inc(g.dma_start(out=dbg["u"][:], in_=ux_in[:]), 16)
            s_sp.inc(g.dma_start(out=dbg["bm"][:], in_=bmx_in[:]), 16)
        barrier()

    p2 = ExitStack()
    dlt2 = sbuf("dlt2", [128, 2 * TC], F32, p2)
    ut2 = sbuf("ut2", [128, 2 * TC], BF16, p2)
    zt2 = sbuf("zt2", [128, 2 * TC], BF16, p2)
    udl = sbuf("udl", [128, TC], BF16, p2)
    bm2b = sbuf("bm2b", [1, 2 * TC], BF16, p2)
    bmrep = sbuf("bmrep", [128, DS * TC], BF16, p2)
    dat = sbuf("dat", [128, 2 * TC], F32, p2)
    wt2 = sbuf("wt2", [128, 2 * TC], BF16, p2)
    ht2 = sbuf("ht2", [128, 2 * TC], BF16, p2)
    carry = sbuf("carry", [128, CTS * DS], F32, p2)
    y2t = sbuf("y2t", [128, 2 * TC], BF16, p2)
    gst = sbuf("gst", [128, 2 * TC], BF16, p2)
    ygt2 = sbuf("ygt2", [128, 2 * TC], BF16, p2)
    ps_y = psum("ps_y", [128, TC], stk=p2)
    ps_e = psum("ps_e", [128, 1024], stk=p2)

    @blk.vector
    def _(v):
        s_dve.inc(v.memset(carry[:], 0.0))

    for g in range(NSC):
        rh = g // 2
        c0 = (g % 2) * TC

        for s in range(DS):
            @blk.gpsimd
            def _(gp, g=g, rh=rh, c0=c0, s=s):
                if g == 0 and s == 0:
                    gp.wait_ge(s_cc.h, M[("a2a", NFC - 1)])
                if ("bmbc", g, (s - 2) * 4 + 3) in M:
                    gp.wait_ge(s_pe.h, M[("bmbc", g, (s - 2) * 4 + 3)])
                elif ("bmbc", g - 1, (DS - 2 + s) * 4 + 3) in M:
                    gp.wait_ge(s_pe.h, M[("bmbc", g - 1, (DS - 2 + s) * 4 + 3)])
                if g >= 1 and s == 0:
                    gp.wait_ge(s_dve.h, M[("wld", (g - 1) * CTS + CTS - 1, DS - 1)])
                for jsub in range(2):
                    j = (g % 2) * 2 + jsub
                    r = j * 2 * DS + rh * DS + s
                    M[("bm2", g, s)] = s_sp.inc(gp.dma_start(
                        out=bm2b[0:1, (s % 2) * TC + jsub * TF:(s % 2) * TC + (jsub + 1) * TF],
                        in_=bmx_out[r:r + 1, :]), 16)

            for si in range(4):
                @blk.tensor
                def _(t, g=g, s=s, si=si):
                    if si == 0:
                        t.wait_ge(s_sp.h, M[("bm2", g, s)])
                    idx = s * 4 + si
                    half = idx % 2
                    if (("bmrd", g, idx - 2)) in M:
                        t.wait_ge(s_act.h, M[("bmrd", g, idx - 2)])
                    elif ("bmrd", g - 1, DS * 4 - 2 + idx) in M:
                        t.wait_ge(s_act.h, M[("bmrd", g - 1, DS * 4 - 2 + idx)])
                    mi_ = t.matmul(ps_e[:, half * 512:half * 512 + 512], ones_row_b[:],
                             bm2b[0:1, (s % 2) * TC + si * 512:(s % 2) * TC + (si + 1) * 512],
                             start=True, stop=True)
                    s_pe.inc(mi_)
                    M[("bmbc", g, idx)] = s_pe.n

                @blk.scalar
                def _(a, g=g, s=s, si=si):
                    idx = s * 4 + si
                    half = idx % 2
                    a.wait_ge(s_pe.h, M[("bmbc", g, idx)])
                    ii = a.activation(out=bmrep[:, s * TC + si * 512:s * TC + (si + 1) * 512],
                                      in_=ps_e[:, half * 512:half * 512 + 512], func=AF.Copy)
                    s_act.inc(ii)
                    if si == 3:
                        afence(a)
                    M[("bmrd", g, idx)] = s_act.n

            if debug and g == 0 and s == 0:
                @blk.gpsimd
                def _(gp):
                    gp.wait_ge(s_act.h, M[("bmrd", 0, 3)])
                    s_sp.inc(gp.dma_start(out=dbg["bmr"][:, 0:TC], in_=bmrep[:, 0:TC]), 16)
                    M[("dbgbm",)] = s_sp.inc(gp.dma_start(out=dbg["bmr"][0:1, TC:TC + 64],
                                                          in_=bm2b[0:1, 448:512]), 16)

                @blk.tensor
                def _(t):
                    t.wait_ge(s_sp.h, M[("dbgbm",)])

                @blk.scalar
                def _(a):
                    a.wait_ge(s_sp.h, M[("dbgbm",)])

        for ct in range(CTS):
            i = g * CTS + ct
            sl = (i % 2) * TC

            @blk.gpsimd
            def _(gp, g=g, ct=ct, i=i, sl=sl, rh=rh, c0=c0):
                gp.wait_ge(s_cc.h, M[("a2a", NFC - 1)])
                if i >= 2:
                    gp.wait_ge(s_act.h, M[("dA", i - 2, DS - 1)])
                    gp.wait_ge(s_dve.h, M[("yg", i - 2)])
                for jsub in range(2):
                    icol = (g * 2 + jsub) * CTS + ct
                    iox = IndirectOffsetOnAxis(ap=scidx_sb[:, icol:icol + 1], axis=0)
                    s_sp.inc(gp.indirect_dma_start(
                        out=dlt2[:, sl + jsub * TF:sl + (jsub + 1) * TF], out_offset=None,
                        in_=dx_out[:, :], in_offset=iox), 16)
                    s_sp.inc(gp.indirect_dma_start(
                        out=ut2[:, sl + jsub * TF:sl + (jsub + 1) * TF], out_offset=None,
                        in_=ux_out[:, :], in_offset=iox), 16)
                    M[("ld2", i)] = s_sp.inc(gp.indirect_dma_start(
                        out=zt2[:, sl + jsub * TF:sl + (jsub + 1) * TF], out_offset=None,
                        in_=zx_out[:, :], in_offset=iox), 16)

            @blk.vector
            def _(v, g=g, ct=ct, i=i, sl=sl):
                v.wait_ge(s_sp.h, M[("ld2", i)])
                ii = v.tensor_mul(udl[:], ut2[:, sl:sl + TC], dlt2[:, sl:sl + TC])
                s_dve.inc(ii)
                M[("udl", i)] = s_dve.n

            if debug and i == 0:
                @blk.gpsimd
                def _(gp, sl=sl):
                    gp.wait_ge(s_dve.h, M[("udl", 0)])
                    s_sp.inc(gp.dma_start(out=dbg["scan0"][:, 6 * TC:7 * TC], in_=udl[:]), 16)
                    M[("dbgu",)] = s_sp.inc(gp.dma_start(
                        out=dbg["scan0"][:, 7 * TC:8 * TC], in_=dlt2[:, sl:sl + TC]), 16)

                @blk.vector
                def _(v):
                    v.wait_ge(s_sp.h, M[("dbgu",)])

            for s in range(DS):
                hs = (s % 2) * TC

                @blk.scalar
                def _(a, g=g, ct=ct, i=i, s=s, sl=sl, hs=hs):
                    if s == 0:
                        a.wait_ge(s_sp.h, M[("ld2", i)])
                    if ("scan", i, s - 2) in M:
                        a.wait_ge(s_dve.h, M[("scan", i, s - 2)])
                    elif ("scan", i - 1, DS - 2 + s) in M:
                        a.wait_ge(s_dve.h, M[("scan", i - 1, DS - 2 + s)])
                    a.activation(out=dat[:, hs:hs + TC], in_=dlt2[:, sl:sl + TC],
                                 func=AF.Exp, scale=a_sb[:, ct * DS + s:ct * DS + s + 1])
                    afence(a)
                    M[("dA", i, s)] = s_act.n

                @blk.vector
                def _(v, g=g, ct=ct, i=i, s=s, sl=sl, hs=hs):
                    v.wait_ge(s_dve.h, M[("udl", i)])
                    v.wait_ge(s_act.h, M[("bmrd", g, 4 * s + 3)])
                    if ("scan", i, s - 2) in M:
                        v.wait_ge(s_dve.h, M[("scan", i, s - 2)])
                    ii = v.tensor_mul(wt2[:, hs:hs + TC], udl[:],
                                      bmrep[:, s * TC:(s + 1) * TC])
                    M[("wld", i, s)] = s_dve.inc(ii)
                    if s >= 1:
                        v.tensor_copy(carry[:, ct * DS + s - 1:ct * DS + s],
                                      ht2[:, ((s - 1) % 2) * TC + TC - 1:((s - 1) % 2) * TC + TC])
                    v.wait_ge(s_act.h, M[("dA", i, s)])
                    if ("yred", i, s - 2) in M:
                        v.wait_ge(s_pe.h, M[("yred", i, s - 2)])
                    elif ("yred", i - 1, DS - 2 + s) in M:
                        v.wait_ge(s_pe.h, M[("yred", i - 1, DS - 2 + s)])
                    ii = v.tensor_tensor_scan(ht2[:, hs:hs + TC], dat[:, hs:hs + TC],
                                              wt2[:, hs:hs + TC],
                                              carry[:, ct * DS + s:ct * DS + s + 1],
                                              ALU.mult, ALU.add)
                    M[("scan", i, s)] = s_dve.inc(ii)

                if debug and i == 0 and s in (0, 1):
                    @blk.gpsimd
                    def _(gp, s=s, hs=hs, sl=sl):
                        gp.wait_ge(s_dve.h, M[("scan", 0, s)])
                        s_sp.inc(gp.dma_start(out=dbg["scan0"][:, (s * 3) * TC:(s * 3 + 1) * TC],
                                              in_=dat[:, hs:hs + TC]), 16)
                        s_sp.inc(gp.dma_start(out=dbg["scan0"][:, (s * 3 + 1) * TC:(s * 3 + 2) * TC],
                                              in_=wt2[:, hs:hs + TC]), 16)
                        M[("dbgs", s)] = s_sp.inc(gp.dma_start(
                            out=dbg["scan0"][:, (s * 3 + 2) * TC:(s * 3 + 3) * TC],
                            in_=ht2[:, hs:hs + TC]), 16)

                    @blk.vector
                    def _(v, s=s):
                        v.wait_ge(s_sp.h, M[("dbgs", s)])

                    @blk.scalar
                    def _(a, s=s):
                        a.wait_ge(s_sp.h, M[("dbgs", s)])

                @blk.tensor
                def _(t, g=g, ct=ct, i=i, s=s, hs=hs):
                    t.wait_ge(s_dve.h, M[("scan", i, s)])
                    if s == 0 and ("y2", i - 1) in M:
                        t.wait_ge(s_dve.h, M[("y2", i - 1)])
                    for si in range(4):
                        mi_ = t.matmul(ps_y[:, si * 512:(si + 1) * 512], ident_b[:],
                                 ht2[:, hs + si * 512:hs + (si + 1) * 512],
                                 start=(s == 0), stop=(s == DS - 1))
                    s_pe.inc(mi_)
                    M[("yred", i, s)] = s_pe.n

            @blk.scalar
            def _(a, g=g, i=i, sl=sl):
                a.wait_ge(s_sp.h, M[("ld2", i)])
                if ("ygsp", i - 1) in M:
                    a.wait_ge(s_sp.h, M[("ygsp", i - 1)])
                a.activation(out=gst[:, sl:sl + TC], in_=zt2[:, sl:sl + TC], func=AF.Silu)
                afence(a)
                M[("gs", i)] = s_act.n

            @blk.vector
            def _(v, g=g, ct=ct, i=i, sl=sl):
                v.wait_ge(s_pe.h, M[("yred", i, DS - 1)])
                v.tensor_copy(carry[:, ct * DS + DS - 1:ct * DS + DS],
                              ht2[:, ((DS - 1) % 2) * TC + TC - 1:((DS - 1) % 2) * TC + TC])
                ii = v.scalar_tensor_tensor(y2t[:, sl:sl + TC], ut2[:, sl:sl + TC],
                                            d_sb[:, ct:ct + 1], ps_y[:, 0:TC],
                                            ALU.mult, ALU.add)
                M[("y2", i)] = s_dve.inc(ii)
                v.wait_ge(s_act.h, M[("gs", i)])
                if ("ygsp", i - 1) in M:
                    v.wait_ge(s_sp.h, M[("ygsp", i - 1)])
                ii = v.tensor_mul(ygt2[:, sl:sl + TC], y2t[:, sl:sl + TC], gst[:, sl:sl + TC])
                M[("yg", i)] = s_dve.inc(ii)

            @blk.gpsimd
            def _(gp, g=g, ct=ct, i=i, sl=sl):
                gp.wait_ge(s_dve.h, M[("yg", i)])
                M[("ygsp", i)] = s_sp.inc(gp.dma_start(
                    out=yg_in[g * (DI // 2) + ct * 128:g * (DI // 2) + (ct + 1) * 128, :],
                    in_=ygt2[:, sl:sl + TC]), 16)

        @blk.gpsimd
        def _(gp, g=g):
            gp.wait_ge(s_sp.h, M[("ygsp", g * CTS + CTS - 1)])
            s_cc.inc(gp.collective_compute(
                "AllGather", ALU.bypass, PAIRS,
                [yg_in[g * (DI // 2):(g + 1) * (DI // 2), :].opt()],
                [ygath[g * DI:(g + 1) * DI, :].opt()]))
            M[("agy", g)] = s_cc.n

    barrier()
    p2.close()
    if stop_after == "p2":
        emit_stop()
        st.close()
        return nc

    TCH = 1024
    p3 = ExitStack()
    ygld = sbuf("ygld", [128, CTF * TCH], BF16, p3)
    owstr = sbuf("owstr", [128, 2 * 16 * 128], BF16, p3)
    xres = sbuf("xres", [128, 2 * TCH], F32, p3)
    r2t = sbuf("r2t", [128, DT * TCH], F32, p3)
    sq3 = sbuf("sq3", [128, 2 * TCH], F32, p3)
    stat3 = sbuf("stat3", [1, 4 * TCH], F32, p3)
    mur3 = sbuf("mur3", [128, TCH], F32, p3)
    rsr3 = sbuf("rsr3", [128, TCH], F32, p3)
    x1ft = sbuf("x1ft", [128, 2 * TCH], F32, p3)
    x1bt = sbuf("x1bt", [128, 2 * TCH], BF16, p3)
    ps_o = [psum(f"ps_o{i}", [128, 512], stk=p3) for i in range(4)]
    ps_st3 = psum("ps_st3", [128, 1536], stk=p3)
    opcnt = [0]
    ops_evac = {}

    for lj in range(4):
        @blk.gpsimd
        def _(gp, lj=lj):
            gp.wait_ge(s_cc.h, M[("agy", NSC - 1)])
            for kt in range(CTF):
                if lj >= 1:
                    gp.wait_ge(s_pe.h, M[("opmmd", lj * DT - 1)])
                M[("ygld", lj, kt)] = s_sp.inc(gp.indirect_dma_start(
                    out=ygld[:, kt * TCH:(kt + 1) * TCH], out_offset=None,
                    in_=ygath[:, :],
                    in_offset=IndirectOffsetOnAxis(
                        ap=ygidx_sb[:, (lj // 2) * CTF + kt:(lj // 2) * CTF + kt + 1], axis=0),
                    element_offset=(lj % 2) * TCH), 16)

        for mt in range(DT):
            wb = (mt % 2) * 16 * 128
            gidx = lj * DT + mt

            @blk.sync
            def _(sp, lj=lj, mt=mt, wb=wb, gidx=gidx):
                if gidx >= 2:
                    sp.wait_ge(s_pe.h, M[("opmmd", gidx - 2)])
                sw = s_ow[gidx % 2]
                M[("ow", gidx)] = (sw.h, sw.inc(sp.dma_start(
                    out=owstr[:, wb:wb + 16 * 128],
                    in_=outw_b[:, mt * 2048:(mt + 1) * 2048]), 16))
                if gidx >= 2:
                    sp.wait_ge(s_dve.h, M[("r2", gidx - 2)])
                M[("xres", gidx)] = s_x.inc(sp.dma_start(
                    out=xres[:, (gidx % 2) * TCH:(gidx % 2) * TCH + TCH],
                    in_=x_pad[:, lj * DT * TFP + mt * TFP + 3:lj * DT * TFP + mt * TFP + 3 + TCH]), 16)

            @blk.tensor
            def _(t, lj=lj, mt=mt, wb=wb, gidx=gidx):
                t.wait_ge(*M[("ow", gidx)])
                t.wait_ge(s_sp.h, M[("ygld", lj, CTF - 1)])
                for si in range(2):
                    pi = opcnt[0] % 4
                    opcnt[0] += 1
                    if pi in ops_evac:
                        t.wait_ge(s_dve.h, ops_evac[pi])
                    for kt in range(CTF):
                        mi_ = t.matmul(ps_o[pi][:, 0:512],
                                 owstr[:, wb + kt * 128:wb + (kt + 1) * 128],
                                 ygld[:, kt * TCH + si * 512:kt * TCH + (si + 1) * 512],
                                 start=(kt == 0), stop=(kt == CTF - 1))
                    s_pe.inc(mi_)
                    M[("opmm", lj, mt, si)] = (s_pe.n, pi)
                M[("opmmd", gidx)] = s_pe.n

            @blk.vector
            def _(v, lj=lj, mt=mt, gidx=gidx):
                v.wait_ge(s_x.h, M[("xres", gidx)])
                for si in range(2):
                    cnt, pi = M[("opmm", lj, mt, si)]
                    v.wait_ge(s_pe.h, cnt)
                    ii = v.scalar_tensor_tensor(
                        r2t[:, mt * TCH + si * 512:mt * TCH + (si + 1) * 512],
                        xres[:, (gidx % 2) * TCH + si * 512:(gidx % 2) * TCH + (si + 1) * 512],
                        2.0, ps_o[pi][:, 0:512], ALU.mult, ALU.add)
                    ops_evac[pi] = s_dve.inc(ii)
                M[("r2", gidx)] = s_dve.n

        emit_ln_stats(("ln2", lj), TCH, lambda dt: r2t[:, dt * TCH:(dt + 1) * TCH],
                      ps_st3, sq3, stat3, mur3, rsr3,
                      gate_act=(s_dve.h, M[("r2", lj * DT + DT - 1)]))

        for dt in range(DT):
            @blk.vector
            def _(v, lj=lj, dt=dt):
                if dt == 0:
                    v.wait_ge(s_dve.h, M[(("ln2", lj), "rsev")])
                if ("x1sp", lj, dt - 1) in M:
                    v.wait_ge(s_sp.h, M[("x1sp", lj, dt - 1)])
                elif lj >= 1 and dt == 0:
                    v.wait_ge(s_sp.h, M[("x1sp", lj - 1, DT - 1)])
                xo = (dt % 2) * TCH
                v.tensor_sub(sq3[:, 0:TCH], r2t[:, dt * TCH:(dt + 1) * TCH], mur3[:])
                ii = v.tensor_mul(x1ft[:, xo:xo + TCH], sq3[:, 0:TCH], rsr3[:])
                s_dve.inc(ii)
                ii = v.tensor_copy(x1bt[:, xo:xo + TCH], x1ft[:, xo:xo + TCH])
                s_dve.inc(ii)
                ii = v.reduce_sum(pool_cols[:, lj * DT + dt:lj * DT + dt + 1],
                                  x1ft[:, xo:xo + TCH], axis=mybir.AxisListType.X)
                s_dve.inc(ii)
                M[("x1", lj, dt)] = s_dve.n

            @blk.gpsimd
            def _(gp, lj=lj, dt=dt):
                gp.wait_ge(s_dve.h, M[("x1", lj, dt)])
                xo = (dt % 2) * TCH
                s_sp.inc(gp.dma_start(
                    out=x1f_scr[dt * 128:(dt + 1) * 128, lj * TCH:(lj + 1) * TCH],
                    in_=x1ft[:, xo:xo + TCH]), 16)
                M[("x1sp", lj, dt)] = s_sp.inc(gp.dma_start(
                    out=x1b_scr[dt * 128:(dt + 1) * 128, lj * TCH:(lj + 1) * TCH],
                    in_=x1bt[:, xo:xo + TCH]), 16)

    barrier()
    p3.close()
    if stop_after == "p25":
        emit_stop()
        st.close()
        return nc

    p4 = ExitStack()
    pooled_sb = sbuf("pooled_sb", [128, DT], F32, p4)
    lg_sb = sbuf("lg_sb", [1, 8], F32, p4)
    m8_sb = sbuf("m8_sb", [1, 8], F32, p4)
    i8_sb = sbuf("i8_sb", [1, 8], U32, p4)
    i8f_sb = sbuf("i8f_sb", [1, 8], F32, p4)
    wpad = sbuf("wpad", [1, 3 * 128], F32, p4)
    ebc = sbuf("ebc", [128, 2], F32, p4)
    wbc = sbuf("wbc", [128, 2], F32, p4)
    ebi = sbuf("ebi", [2, 2], U32, p4)
    w1idx = sbuf("w1idx", [128, 16], U32, p4)
    w2idx = sbuf("w2idx", [128, 64], U32, p4)
    idxf = sbuf("idxf", [128, 80], F32, p4)
    ps_r = psum("ps_r", [128, 512], stk=p4)

    @blk.vector
    def _(v):
        v.wait_ge(s_dve.h, M[("x1", 3, DT - 1)])
        v.tensor_add(pool_cols[:, 0:DT], pool_cols[:, 0:DT], pool_cols[:, DT:2 * DT])
        v.tensor_add(pool_cols[:, 2 * DT:3 * DT], pool_cols[:, 2 * DT:3 * DT],
                     pool_cols[:, 3 * DT:4 * DT])
        vfence(v)
        v.tensor_add(pool_cols[:, 0:DT], pool_cols[:, 0:DT], pool_cols[:, 2 * DT:3 * DT])
        vfence(v)
        v.tensor_scalar_mul(pooled_sb[:], pool_cols[:, 0:DT], 1.0 / float(T))
        vfence(v)
        M["poolloc"] = s_dve.n

    @blk.gpsimd
    def _(gp):
        gp.wait_ge(s_dve.h, M["poolloc"])
        s_sp.inc(gp.dma_start(out=pool_in[:], in_=pooled_sb[:]), 16)
        gp.wait_ge(s_sp.h, s_sp.n)
        s_cc.inc(gp.collective_compute("AllReduce", ALU.add, PAIRS,
                                       [pool_in.opt()], [pool_out.opt()]))
        gp.wait_ge(s_cc.h, s_cc.n)
        M["poolcc"] = s_sp.inc(gp.dma_start(out=pooled_sb[:], in_=pool_out[:]), 16)

    @blk.tensor
    def _(t):
        t.wait_ge(s_sp.h, M["poolcc"])
        for kt in range(DT):
            mi_ = t.matmul(ps_r[0:1, 0:NE], pooled_sb[:, kt:kt + 1], wr_sb[:, kt * NE:(kt + 1) * NE],
                     start=(kt == 0), stop=(kt == DT - 1))
        s_pe.inc(mi_)
        M["lg"] = s_pe.n

    @blk.vector
    def _(v):
        v.wait_ge(s_pe.h, M["lg"])
        v.tensor_copy(lg_sb[:], ps_r[0:1, 0:NE])
        vfence(v)
        v.max(out=m8_sb[:], in_=lg_sb[:])
        vfence(v)
        v.max_index(out=i8_sb[:], in_max=m8_sb[:], in_values=lg_sb[:])
        vfence(v)
        v.tensor_copy(i8f_sb[:], i8_sb[:])
        vfence(v)
        v.tensor_sub(wpad[0:1, 0:128], m8_sb[0:1, 1:2].to_broadcast([1, 128]),
                     m8_sb[0:1, 0:1].to_broadcast([1, 128]))
        vfence(v)
        M["wdiff"] = s_dve.n

    @blk.scalar
    def _(a):
        a.wait_ge(s_dve.h, M["wdiff"])
        a.activation(out=wpad[0:1, 128:256], in_=wpad[0:1, 0:128], func=AF.Exp)
        afence(a)
        M["wexp"] = s_act.n

    @blk.vector
    def _(v):
        v.wait_ge(s_act.h, M["wexp"])
        v.tensor_scalar_add(wpad[0:1, 256:384], wpad[0:1, 128:256], 1.0)
        vfence(v)
        v.reciprocal(wpad[0:1, 0:128], wpad[0:1, 256:384])
        vfence(v)
        v.tensor_scalar(wpad[0:1, 128:256], wpad[0:1, 0:128], -1.0, 1.0,
                        ALU.mult, ALU.add)
        vfence(v)
        M["wgts"] = s_dve.n

    @blk.tensor
    def _(t):
        t.wait_ge(s_dve.h, M["wgts"])
        t.matmul(ps_r[:, 0:2], ones_row_f[:], i8f_sb[0:1, 0:2], start=True, stop=True)
        t.matmul(ps_r[:, 4:5], ones_row_f[:], wpad[0:1, 0:1], start=True, stop=True)
        mi_ = t.matmul(ps_r[:, 5:6], ones_row_f[:], wpad[0:1, 128:129], start=True, stop=True)
        s_pe.inc(mi_)
        M["ebcmm"] = s_pe.n

    @blk.vector
    def _(v):
        v.wait_ge(s_pe.h, M["ebcmm"])
        v.tensor_copy(ebc[:], ps_r[:, 0:2])
        v.tensor_copy(wbc[:], ps_r[:, 4:6])
        vfence(v)
        ii = None
        for slot in range(2):
            for kt in range(DT):
                ii = v.scalar_tensor_tensor(idxf[:, slot * 8 + kt:slot * 8 + kt + 1],
                                            ebc[:, slot:slot + 1], 1024.0,
                                            iota_sb[:, kt:kt + 1], ALU.mult, ALU.add)
        for slot in range(2):
            for kt2 in range(32):
                ii = v.scalar_tensor_tensor(idxf[:, 16 + slot * 32 + kt2:16 + slot * 32 + kt2 + 1],
                                            ebc[:, slot:slot + 1], 4096.0,
                                            iota_sb[:, kt2:kt2 + 1], ALU.mult, ALU.add)
        vfence(v)
        v.tensor_copy(w1idx[:], idxf[:, 0:16])
        v.tensor_copy(w2idx[:], idxf[:, 16:80])
        v.tensor_copy(ebi[:], ebc[0:2, 0:2])
        vfence(v)
        M["idx"] = s_dve.n

    if debug:
        @blk.gpsimd
        def _(gp):
            gp.wait_ge(s_dve.h, M["idx"])
            s_sp.inc(gp.dma_start(out=dbg["rout"][0:1, 0:8], in_=m8_sb[:]), 16)
            s_sp.inc(gp.dma_start(out=dbg["rout"][0:1, 8:16], in_=i8f_sb[:]), 16)
            s_sp.inc(gp.dma_start(out=dbg["yg"][:], in_=yg_in[:]), 16)
            s_sp.inc(gp.dma_start(out=dbg["x1"][:], in_=x1f_scr[:]), 16)

    barrier()
    if stop_after == "p3":
        p4.close()
        emit_stop()
        st.close()
        return nc

    w1sb = sbuf("w1sb", [128, DT * FF], BF16, p4)
    w2sb = sbuf("w2sb", [128, 32 * DM], BF16, p4)
    b1rb = sbuf("b1rb", [2, FF], BF16, p4)
    b2rb = sbuf("b2rb", [2, DM], BF16, p4)
    x1ld = sbuf("x1ld", [128, 2 * DT * TFN], BF16, p4)
    hb4 = sbuf("hb4", [128, FTK * TFN], BF16, p4)
    y0t = sbuf("y0t", [128, 2 * TFN], F32, p4)
    y0ld = sbuf("y0ld", [128, 2 * TFN], F32, p4)
    x1fld = sbuf("x1fld", [128, 2 * TFN], F32, p4)
    r3t = sbuf("r3t", [128, DT * TFN], F32, p4)
    sq4 = sbuf("sq4", [128, 2 * TFN], F32, p4)
    sq5 = sbuf("sq5", [128, 2 * TFN], F32, p4)
    sq6 = sbuf("sq6", [128, TFN], F32, p4)
    stat5 = sbuf("stat5", [1, 4 * TFN], F32, p4)
    mur4 = sbuf("mur4", [128, TFN], F32, p4)
    rsr4 = sbuf("rsr4", [128, TFN], F32, p4)
    outt = sbuf("outt", [128, 2 * TFN], F32, p4)
    ps_f = [psum(f"ps_f{i}", [128, 512], stk=p4) for i in range(4)]
    ps_st5 = psum("ps_st5", [128, 1536], stk=p4)

    for slot in range(2):
        @blk.gpsimd
        def _(gp, slot=slot):
            gp.wait_ge(s_dve.h, M["idx"])
            if slot == 1:
                gp.wait_ge(s_pe.h, M[("mm1d", 0, NFN - 1)])
                gp.wait_ge(s_pe.h, M[("mm2", 0, NFN - 1, DT - 1)])
            for kt in range(DT):
                M[("w1g", slot, kt)] = s_sp.inc(gp.indirect_dma_start(
                    out=w1sb[:, kt * FF:(kt + 1) * FF], out_offset=None,
                    in_=w1[:, :],
                    in_offset=IndirectOffsetOnAxis(
                        ap=w1idx[:, slot * 8 + kt:slot * 8 + kt + 1], axis=0)), 16)
            for kt2 in range(32):
                M[("w2g", slot, kt2)] = s_sp.inc(gp.indirect_dma_start(
                    out=w2sb[:, kt2 * DM:(kt2 + 1) * DM], out_offset=None,
                    in_=w2[:, :],
                    in_offset=IndirectOffsetOnAxis(
                        ap=w2idx[:, slot * 32 + kt2:slot * 32 + kt2 + 1], axis=0)), 16)
            M[("b1g", slot)] = s_sp.inc(gp.indirect_dma_start(
                out=b1rb[:], out_offset=None, in_=b1[:, :],
                in_offset=IndirectOffsetOnAxis(ap=ebi[0:2, slot:slot + 1], axis=0)), 16)
            M[("b2g", slot)] = s_sp.inc(gp.indirect_dma_start(
                out=b2rb[:], out_offset=None, in_=b2[:, :],
                in_offset=IndirectOffsetOnAxis(ap=ebi[0:2, slot:slot + 1], axis=0)), 16)

        for fc in range(NFN):
            gidx = slot * NFN + fc

            @blk.sync
            def _(sp, slot=slot, fc=fc, gidx=gidx):
                sp.wait_ge(s_sp.h, M[("x1sp", 3, DT - 1)])
                if gidx >= 2:
                    sp.wait_ge(s_pe.h, M[("mm1d", (gidx - 2) // NFN, (gidx - 2) % NFN)])
                M[("x1ld", gidx)] = s_x.inc(sp.dma_start(
                    out=x1ld[:, (gidx % 2) * DT * TFN:(gidx % 2) * DT * TFN + DT * TFN].rearrange(
                        "p (a t) -> p a t", a=DT),
                    in_=x1b_scr[:, fc * TFN:(fc + 1) * TFN].rearrange(
                        "(a p) t -> p a t", p=128)), 16)

            for ft in range(FTK):
                @blk.tensor
                def _(t, slot=slot, fc=fc, gidx=gidx, ft=ft):
                    if ft == 0:
                        t.wait_ge(s_x.h, M[("x1ld", gidx)])
                        t.wait_ge(s_sp.h, M[("b2g", slot)])
                    xo = (gidx % 2) * DT * TFN
                    pi = ft % 2
                    if ("gelu", slot, fc, ft - 2) in M:
                        t.wait_ge(s_act.h, M[("gelu", slot, fc, ft - 2)])
                    elif ("gelu", slot, fc - 1, FTK - 2 + ft) in M:
                        t.wait_ge(s_act.h, M[("gelu", slot, fc - 1, FTK - 2 + ft)])
                    elif slot == 1 and ("gelu", 0, NFN - 1, FTK - 2 + ft) in M:
                        t.wait_ge(s_act.h, M[("gelu", 0, NFN - 1, FTK - 2 + ft)])
                    for kt in range(DT):
                        t.matmul(ps_f[pi][:, 0:TFN],
                                 w1sb[:, kt * FF + ft * 128:kt * FF + (ft + 1) * 128],
                                 x1ld[:, xo + kt * TFN:xo + (kt + 1) * TFN],
                                 start=(kt == 0), stop=False)
                    mi_ = t.matmul(ps_f[pi][:, 0:TFN], b1rb[0:1, ft * 128:(ft + 1) * 128],
                             ones_tfn_b[:], start=False, stop=True)
                    s_pe.inc(mi_)
                    M[("mm1", slot, fc, ft)] = (s_pe.n, pi)
                    if ft == FTK - 1:
                        M[("mm1d", slot, fc)] = s_pe.n

                @blk.scalar
                def _(a, slot=slot, fc=fc, ft=ft):
                    cnt, pi = M[("mm1", slot, fc, ft)]
                    a.wait_ge(s_pe.h, cnt)
                    ii = a.activation(out=hb4[:, ft * TFN:(ft + 1) * TFN],
                                      in_=ps_f[pi][:, 0:TFN], func=AF.Gelu)
                    s_act.inc(ii)
                    if ft == FTK - 1:
                        afence(a)
                    M[("gelu", slot, fc, ft)] = s_act.n

            for mt in range(DT):
                @blk.tensor
                def _(t, slot=slot, fc=fc, mt=mt):
                    if mt == 0:
                        t.wait_ge(s_act.h, M[("gelu", slot, fc, FTK - 1)])
                    pi = 2 + (mt % 2)
                    if ("ev3", slot, fc, mt - 2) in M:
                        t.wait_ge(s_dve.h, M[("ev3", slot, fc, mt - 2)])
                    elif ("ev3", slot, fc - 1, DT - 2 + mt) in M:
                        t.wait_ge(s_dve.h, M[("ev3", slot, fc - 1, DT - 2 + mt)])
                    elif slot == 1 and ("ev3", 0, NFN - 1, DT - 2 + mt) in M:
                        t.wait_ge(s_dve.h, M[("ev3", 0, NFN - 1, DT - 2 + mt)])
                    for ft in range(FTK):
                        t.matmul(ps_f[pi][:, 0:TFN],
                                 w2sb[:, ft * DM + mt * 128:ft * DM + (mt + 1) * 128],
                                 hb4[:, ft * TFN:(ft + 1) * TFN],
                                 start=(ft == 0), stop=False)
                    mi_ = t.matmul(ps_f[pi][:, 0:TFN], b2rb[0:1, mt * 128:(mt + 1) * 128],
                             ones_tfn_b[:], start=False, stop=True)
                    s_pe.inc(mi_)
                    M[("mm2", slot, fc, mt)] = s_pe.n

                if slot == 0:
                    @blk.vector
                    def _(v, fc=fc, mt=mt):
                        v.wait_ge(s_pe.h, M[("mm2", 0, fc, mt)])
                        if ("y0sp", fc, mt - 1) in M:
                            v.wait_ge(s_sp.h, M[("y0sp", fc, mt - 1)])
                        elif ("y0sp", fc - 1, DT - 1 + mt) in M:
                            v.wait_ge(s_sp.h, M[("y0sp", fc - 1, DT - 1 + mt)])
                        pi = 2 + (mt % 2)
                        ii = v.tensor_copy(y0t[:, (mt % 2) * TFN:(mt % 2) * TFN + TFN],
                                           ps_f[pi][:, 0:TFN])
                        s_dve.inc(ii)
                        M[("ev3", 0, fc, mt)] = s_dve.n

                    @blk.gpsimd
                    def _(gp, fc=fc, mt=mt):
                        gp.wait_ge(s_dve.h, M[("ev3", 0, fc, mt)])
                        M[("y0sp", fc, mt)] = s_sp.inc(gp.dma_start(
                            out=y0_scr[mt * 128:(mt + 1) * 128, fc * TFN:(fc + 1) * TFN],
                            in_=y0t[:, (mt % 2) * TFN:(mt % 2) * TFN + TFN]), 16)
                else:
                    @blk.sync
                    def _(sp, fc=fc, mt=mt):
                        sp.wait_ge(s_sp.h, M[("y0sp", fc, mt)])
                        if ("r3", fc, mt - 2) in M:
                            sp.wait_ge(s_dve.h, M[("r3", fc, mt - 2)])
                        elif ("r3", fc - 1, DT - 2 + mt) in M:
                            sp.wait_ge(s_dve.h, M[("r3", fc - 1, DT - 2 + mt)])
                        M[("y0l2", fc, mt)] = s_x.inc(sp.dma_start(
                            out=y0ld[:, (mt % 2) * TFN:(mt % 2) * TFN + TFN],
                            in_=y0_scr[mt * 128:(mt + 1) * 128, fc * TFN:(fc + 1) * TFN]), 16)
                        M[("x1fl", fc, mt)] = s_x.inc(sp.dma_start(
                            out=x1fld[:, (mt % 2) * TFN:(mt % 2) * TFN + TFN],
                            in_=x1f_scr[mt * 128:(mt + 1) * 128, fc * TFN:(fc + 1) * TFN]), 16)

                    @blk.vector
                    def _(v, fc=fc, mt=mt):
                        v.wait_ge(s_pe.h, M[("mm2", 1, fc, mt)])
                        v.wait_ge(s_x.h, M[("x1fl", fc, mt)])
                        pi = 2 + (mt % 2)
                        mo = (mt % 2) * TFN
                        v.tensor_scalar_mul(sq5[:, 0:TFN], ps_f[pi][:, 0:TFN], wbc[:, 1:2])
                        v.scalar_tensor_tensor(sq5[:, TFN:2 * TFN], y0ld[:, mo:mo + TFN],
                                               wbc[:, 0:1], sq5[:, 0:TFN], ALU.mult, ALU.add)
                        ii = v.tensor_add(r3t[:, mt * TFN:(mt + 1) * TFN],
                                          sq5[:, TFN:2 * TFN], x1fld[:, mo:mo + TFN])
                        s_dve.inc(ii)
                        M[("ev3", 1, fc, mt)] = s_dve.n
                        M[("r3", fc, mt)] = s_dve.n

            if slot == 1:
                emit_ln_stats(("ln3", fc), TFN, lambda dt: r3t[:, dt * TFN:(dt + 1) * TFN],
                              ps_st5, sq4, stat5, mur4, rsr4,
                              gate_act=(s_dve.h, M[("r3", fc, DT - 1)]))

                for dt in range(DT):
                    @blk.vector
                    def _(v, fc=fc, dt=dt):
                        if dt == 0:
                            v.wait_ge(s_dve.h, M[(("ln3", fc), "rsev")])
                        if ("osp", fc, dt - 1) in M:
                            v.wait_ge(s_sp.h, M[("osp", fc, dt - 1)])
                        elif ("osp", fc - 1, DT - 1 + dt) in M:
                            v.wait_ge(s_sp.h, M[("osp", fc - 1, DT - 1 + dt)])
                        v.tensor_sub(sq6[:, 0:TFN], r3t[:, dt * TFN:(dt + 1) * TFN], mur4[:])
                        ii = v.tensor_mul(outt[:, (dt % 2) * TFN:(dt % 2) * TFN + TFN],
                                          sq6[:, 0:TFN], rsr4[:])
                        s_dve.inc(ii)
                        M[("oev", fc, dt)] = s_dve.n

                    @blk.gpsimd
                    def _(gp, fc=fc, dt=dt):
                        gp.wait_ge(s_dve.h, M[("oev", fc, dt)])
                        M[("osp", fc, dt)] = s_sp.inc(gp.dma_start(
                            out=out_fm[dt * 128:(dt + 1) * 128, fc * TFN:(fc + 1) * TFN],
                            in_=outt[:, (dt % 2) * TFN:(dt % 2) * TFN + TFN]), 16)

    @blk.gpsimd
    def _(gp):
        gp.wait_ge(s_sp.h, s_sp.n)

    barrier()
    p4.close()
    st.close()
    return nc


def _host_inputs(inp):
    x = np.asarray(inp["L0_attn_x"])
    maps = []
    ident = np.eye(128, dtype=np.float32)
    iota = (np.arange(33, dtype=np.float32)[None, :] * 128
            + np.arange(128, dtype=np.float32)[:, None]).astype(np.float32)
    def swz(w, mb):
        K, Mw = w.shape
        a = K // 128
        return np.ascontiguousarray(
            w.reshape(a, 128, Mw // mb, mb).transpose(1, 2, 0, 3).reshape(128, -1))

    shared = {
        "inw": swz(np.asarray(inp["L0_attn_in_proj_w"]), 512),
        "dtw": swz(np.asarray(inp["L0_attn_dt_proj_w"]), 128),
        "outw": swz(np.asarray(inp["L0_attn_out_proj_w"]), 128),
        "xpw": np.ascontiguousarray(np.asarray(inp["L0_attn_x_proj_w"])[:, DS:]),
        "dtb": np.asarray(inp["L0_attn_dt_proj_b"]),
        "convw": np.ascontiguousarray(np.asarray(inp["L0_attn_conv_w"])[:, 0, :]),
        "convb": np.asarray(inp["L0_attn_conv_b"]),
        "lng": np.asarray(inp["L0_attn_ln_gamma"]),
        "lnb": np.asarray(inp["L0_attn_ln_beta"]),
        "wr": np.asarray(inp["L0_ffn_Wr"]),
        "w1": np.asarray(inp["L0_ffn_W1"]).reshape(NE * DM, FF),
        "w2": np.asarray(inp["L0_ffn_W2"]).reshape(NE * FF, DM),
        "b1": np.asarray(inp["L0_ffn_b1"]),
        "b2": np.asarray(inp["L0_ffn_b2"]),
        "ident_in": ident,
        "iota_in": iota,
    }
    for k in range(8):
        b, par = k // 2, k % 2
        lo = par * TH
        xp0 = np.zeros((DM, TH + 3), np.float32)
        xp0[:, 3:] = x[b, lo:lo + TH, :].T
        if par == 1:
            xp0[:, 0:3] = x[b, lo - 3:lo, :].T
        xp = np.stack([xp0[:, j * TF:j * TF + TFP] for j in range(NFC)], axis=0)
        xp = np.ascontiguousarray(
            xp.reshape(NFC, DT, 128, TFP).transpose(2, 0, 1, 3).reshape(128, -1))
        ygidx = np.zeros((128, 2 * CTF), np.uint32)
        for lj in range(2):
            g = par * 2 + lj
            for kt in range(CTF):
                ygidx[:, lj * CTF + kt] = g * DI + kt * 128 + np.arange(128)
        scidx = np.zeros((128, NSC * 2 * CTS), np.uint32)
        for g in range(NSC):
            for jsub in range(2):
                j = (g % 2) * 2 + jsub
                for ct in range(CTS):
                    scidx[:, (g * 2 + jsub) * CTS + ct] = (j * 2 * DI + (g // 2) * DI
                                                           + par * (DI // 2) + ct * 128 + np.arange(128))
        m = dict(shared)
        m.update({
            "x_pad": xp,
            "alog_h": np.ascontiguousarray(np.asarray(inp["L0_attn_A_log"])[par * 1024:(par + 1) * 1024]),
            "d_h": np.ascontiguousarray(np.asarray(inp["L0_attn_D_param"])[par * 1024:(par + 1) * 1024]),
            "ygidx_in": ygidx,
            "scidx_in": scidx,
            "hmask": np.full((128, 1), 1.0 if par == 1 else 0.0, np.float32),
        })
        maps.append(m)
    return maps


def kernel(**inputs):
    nc = build_program(debug=False)
    in_maps = _host_inputs(inputs)
    res = run_bass_kernel_spmd(nc, in_maps, core_ids=list(range(8)))
    out = np.zeros((NB, T, DM), np.float32)
    for k in range(8):
        b, par = k // 2, k % 2
        out[b, par * TH:(par + 1) * TH, :] = res.results[k]["out_fm"].T
    return out


if __name__ == "__main__":
    nc = build_program(debug=False)
    n = sum(1 for _ in nc.all_instructions())
    print("built OK,", n, "instructions")


# version 09
# speedup vs baseline: 1.2531x; 1.0059x over previous
from contextlib import ExitStack
import numpy as np
import concourse.bass as bass
import concourse.mybir as mybir
from concourse.bass import IndirectOffsetOnAxis
from concourse.bass_utils import run_bass_kernel_spmd

AF = mybir.ActivationFunctionType
ALU = mybir.AluOpType
F32 = mybir.dt.float32
BF16 = mybir.dt.bfloat16
U32 = mybir.dt.uint32

DM, DI, DS, NB, T = 1024, 2048, 16, 4, 8192
TH = T // 2
TF = 1024
TFP = TF + 3
NFC = TH // TF
TC = 2048
NSC = T // TC
CTF = DI // 128
CTS = DI // 2 // 128
DT = DM // 128
NE, FF = 8, 4096
FTK = FF // 128
TFN = 256
NFN = TH // TFN
LNEPS = 1e-5
PAIRS = [[0, 1], [2, 3], [4, 5], [6, 7]]


def build_program(debug=False, detect_races=False, stop_after=None):
    nc = bass.Bass("TRN2", target_bir_lowering=False, debug=False,
                   detect_race_conditions=detect_races)

    def din(name, shape, dt=F32):
        return nc.dram_tensor(name, shape, dt, kind="ExternalInput").ap()

    def scr(name, shape, dt=F32):
        return nc.dram_tensor(name, shape, dt, kind="Internal").ap()

    x_pad = din("x_pad", [128, NFC * DT * TFP])
    inw = din("inw", [128, 8 * 8 * 512])
    dtw = din("dtw", [128, CTF * CTF * 128])
    outw = din("outw", [128, DT * CTF * 128])
    xpw = din("xpw", [DI, DS])
    dtb = din("dtb", [DI])
    convw = din("convw", [DI, 4])
    convb = din("convb", [DI])
    lng = din("lng", [DM])
    lnb = din("lnb", [DM])
    alog_h = din("alog_h", [DI // 2, DS])
    d_h = din("d_h", [DI // 2])
    wr = din("wr", [DM, NE])
    w1 = din("w1", [NE * DM, FF])
    w2 = din("w2", [NE * FF, DM])
    b1 = din("b1", [NE, FF])
    b2 = din("b2", [NE, DM])
    ident_in = din("ident_in", [128, 128])
    iota_in = din("iota_in", [128, 33])
    ygidx_in = din("ygidx_in", [128, 2 * CTF], U32)
    scidx_in = din("scidx_in", [128, NSC * 2 * CTS], U32)
    hmask = din("hmask", [128, 1])
    out_fm = nc.dram_tensor("out_fm", [DM, TH], F32, kind="ExternalOutput").ap()

    inw_b = scr("inw_b", [128, 8 * 8 * 512], BF16)
    dtw_b = scr("dtw_b", [128, CTF * CTF * 128], BF16)
    outw_b = scr("outw_b", [128, DT * CTF * 128], BF16)
    dx_in = scr("dx_in", [NFC * DI, TF])
    ux_in = scr("ux_in", [NFC * DI, TF], BF16)
    zx_in = scr("zx_in", [NFC * DI, TF], BF16)
    bmx_in = scr("bmx_in", [NFC * DS, TF])
    dx_out = scr("dx_out", [NFC * 2 * DI, TF])
    ux_out = scr("ux_out", [NFC * 2 * DI, TF], BF16)
    zx_out = scr("zx_out", [NFC * 2 * DI, TF], BF16)
    bmx_out = scr("bmx_out", [NFC * 2 * DS, TF])
    yg_in = scr("yg_in", [NSC * (DI // 2), TC], BF16)
    ygath = scr("ygath", [NSC * DI, TC], BF16)
    pool_in = scr("pool_in", [128, DT])
    pool_out = scr("pool_out", [128, DT])
    y0_scr = scr("y0_scr", [DM, TH])
    x1b_scr = scr("x1b_scr", [DM, TH], BF16)
    x1f_scr = scr("x1f_scr", [DM, TH])
    dbg = {}
    if debug:
        dbg["delta"] = nc.dram_tensor("dbg_delta", [NFC * DI, TF], F32, kind="ExternalOutput").ap()
        dbg["u"] = nc.dram_tensor("dbg_u", [NFC * DI, TF], BF16, kind="ExternalOutput").ap()
        dbg["bm"] = nc.dram_tensor("dbg_bm", [NFC * DS, TF], F32, kind="ExternalOutput").ap()
        dbg["yg"] = nc.dram_tensor("dbg_yg", [NSC * (DI // 2), TC], BF16, kind="ExternalOutput").ap()
        dbg["x1"] = nc.dram_tensor("dbg_x1", [DM, TH], F32, kind="ExternalOutput").ap()
        dbg["rout"] = nc.dram_tensor("dbg_rout", [1, 16], F32, kind="ExternalOutput").ap()
        dbg["scan0"] = nc.dram_tensor("dbg_scan0", [128, 8 * TC], F32, kind="ExternalOutput").ap()
        dbg["bmr"] = nc.dram_tensor("dbg_bmr", [128, TC + 64], F32, kind="ExternalOutput").ap()

    st = ExitStack()

    def sbuf(name, shape, dt, stk=None):
        return (stk or st).enter_context(nc.sbuf_tensor(name, shape, dt))

    def psum(name, shape, dt=F32, stk=None):
        return (stk or st).enter_context(nc.psum_tensor(name, shape, dt))

    class Sem:
        def __init__(self, name):
            self.h = st.enter_context(nc.semaphore(name))
            self.n = 0

        def inc(self, instr, k=1):
            instr.then_inc(self.h, k)
            self.n += k
            return self.n

    a_sb = sbuf("a_sb", [128, CTS * DS], F32)
    d_sb = sbuf("d_sb", [128, CTS], F32)
    cw_sb = sbuf("cw_sb", [128, CTF * 4], F32)
    cb_sb = sbuf("cb_sb", [128, CTF], F32)
    dtb_sb = sbuf("dtb_sb", [128, CTF], F32)
    lng_sb = sbuf("lng_sb", [128, DT], F32)
    lnb_sb = sbuf("lnb_sb", [128, DT], F32)
    wr_sb = sbuf("wr_sb", [128, DT * NE], F32)
    xpw_sb = sbuf("xpw_sb", [128, CTF * DS], BF16)
    ident_b = sbuf("ident_b", [128, 128], BF16)
    ident_f = sbuf("ident_f", [128, 128], F32)
    ones_row_b = sbuf("ones_row_b", [1, 128], BF16)
    ones_row_f = sbuf("ones_row_f", [1, 128], F32)
    ones_col_f = sbuf("ones_col_f", [128, 1], F32)
    ones_tfn_b = sbuf("ones_tfn_b", [1, TFN], BF16)
    iota_sb = sbuf("iota_sb", [128, 33], F32)
    ygidx_sb = sbuf("ygidx_sb", [128, 2 * CTF], U32)
    scidx_sb = sbuf("scidx_sb", [128, NSC * 2 * CTS], U32)
    hmask_sb = sbuf("hmask_sb", [128, 1], F32)
    pool_cols = sbuf("pool_cols", [128, 4 * DT], F32)
    epsc = sbuf("epsc", [1, 2], F32)
    jnk_v = sbuf("jnk_v", [128, 2], F32)
    jnk_a = sbuf("jnk_a", [128, 2], F32)
    jnk_g = sbuf("jnk_g", [128, 512], F32)

    s_x = Sem("s_x")
    s_iw = [Sem("s_iw0"), Sem("s_iw1")]
    s_dw = [Sem("s_dw0"), Sem("s_dw1")]
    s_ow = [Sem("s_ow0"), Sem("s_ow1")]
    s_sp = Sem("s_sp")
    s_cc = Sem("s_cc")
    s_pe = Sem("s_pe")
    s_act = Sem("s_act")
    s_dve = Sem("s_dve")
    s_bar = Sem("s_bar")

    blk = st.enter_context(nc.Block())

    def vfence(v):
        ii = v.tensor_copy(jnk_v[0:1, 1:2], jnk_v[0:1, 0:1])
        s_dve.inc(ii)
        return ii

    def afence(a):
        ii = a.activation(out=jnk_a[0:1, 1:2], in_=jnk_a[0:1, 0:1], func=AF.Copy)
        s_act.inc(ii)
        return ii

    M = {}
    ln_last_rsev = {}

    def barrier():
        s_bar.n += 4
        tgt = s_bar.n
        pe_tgt = s_pe.n
        for dec in (blk.sync, blk.scalar, blk.vector, blk.gpsimd):
            @dec
            def _(e):
                e.sem_inc(s_bar.h, 1)
                e.wait_ge(s_bar.h, tgt)
                e.wait_ge(s_pe.h, pe_tgt)

        @blk.tensor
        def _(t):
            t.wait_ge(s_bar.h, tgt)


    class StopEmission(Exception):
        pass

    def maybe_stop(tag):
        if stop_after == tag:
            raise StopEmission()

    def emit_stop():
        @blk.gpsimd
        def _(gp):
            for dt in range(DT):
                for q in range(TH // 512):
                    s_sp.inc(gp.dma_start(out=out_fm[dt * 128:(dt + 1) * 128, q * 512:(q + 1) * 512],
                                          in_=jnk_g[:]), 16)
            gp.wait_ge(s_sp.h, s_sp.n)
        barrier()

    @blk.sync
    def _(sp):
        with nc.allow_non_contiguous_dma("tiny constants"):
            for ct in range(CTS):
                s_x.inc(sp.dma_start(out=a_sb[:, ct * DS:(ct + 1) * DS],
                                     in_=alog_h[ct * 128:(ct + 1) * 128, :]), 16)
            s_x.inc(sp.dma_start(out=d_sb[:], in_=d_h.rearrange("(a p) -> p a", p=128)), 16)
            for ct in range(CTF):
                s_x.inc(sp.dma_start(out=cw_sb[:, ct * 4:(ct + 1) * 4],
                                     in_=convw[ct * 128:(ct + 1) * 128, :]), 16)
            s_x.inc(sp.dma_start(out=cb_sb[:], in_=convb.rearrange("(a p) -> p a", p=128)), 16)
            s_x.inc(sp.dma_start(out=dtb_sb[:], in_=dtb.rearrange("(a p) -> p a", p=128)), 16)
            s_x.inc(sp.dma_start(out=lng_sb[:], in_=lng.rearrange("(a p) -> p a", p=128)), 16)
            s_x.inc(sp.dma_start(out=lnb_sb[:], in_=lnb.rearrange("(a p) -> p a", p=128)), 16)
            for kt in range(DT):
                s_x.inc(sp.dma_start(out=wr_sb[:, kt * NE:(kt + 1) * NE],
                                     in_=wr[kt * 128:(kt + 1) * 128, :]), 16)
            s_x.inc(sp.dma_start(out=ident_f[:], in_=ident_in[:]), 16)
            s_x.inc(sp.dma_start(out=iota_sb[:], in_=iota_in[:]), 16)
            s_x.inc(sp.dma_start(out=ygidx_sb[:], in_=ygidx_in[:]), 16)
            s_x.inc(sp.dma_start(out=scidx_sb[:], in_=scidx_in[:]), 16)
            s_x.inc(sp.dma_start(out=hmask_sb[:], in_=hmask[:]), 16)
    p0_x = s_x.n

    @blk.gpsimd
    def _(g):
        g.memset(jnk_g[:], 0.0)
        for i in range(4):
            s_sp.inc(g.dma_start(out=inw_b[:, i * 8192:(i + 1) * 8192],
                                 in_=inw[:, i * 8192:(i + 1) * 8192]), 16)
        for i in range(4):
            s_sp.inc(g.dma_start(out=dtw_b[:, i * 8192:(i + 1) * 8192],
                                 in_=dtw[:, i * 8192:(i + 1) * 8192]), 16)
        for i in range(2):
            s_sp.inc(g.dma_start(out=outw_b[:, i * 8192:(i + 1) * 8192],
                                 in_=outw[:, i * 8192:(i + 1) * 8192]), 16)
        for kt in range(CTF):
            s_sp.inc(g.dma_start(out=xpw_sb[:, kt * DS:(kt + 1) * DS],
                                 in_=xpw[kt * 128:(kt + 1) * 128, :]), 16)
        g.wait_ge(s_sp.h, s_sp.n)

    @blk.scalar
    def _(a):
        s_act.inc(a.memzero(jnk_a[:]))
        a.wait_ge(s_x.h, p0_x)
        a.activation(out=a_sb[:], in_=a_sb[:], func=AF.Exp)
        afence(a)

    @blk.vector
    def _(v):
        s_dve.inc(v.memset(jnk_v[:], 0.0))
        v.wait_ge(s_act.h, s_act.n)
        s_dve.inc(v.tensor_scalar_mul(a_sb[:], a_sb[:], -1.0))
        s_dve.inc(v.tensor_copy(ident_b[:], ident_f[:]))
        s_dve.inc(v.memset(ones_row_b[:], 1.0))
        s_dve.inc(v.memset(ones_row_f[:], 1.0))
        s_dve.inc(v.memset(ones_col_f[:], 1.0))
        s_dve.inc(v.memset(ones_tfn_b[:], 1.0))
        s_dve.inc(v.memset(epsc[0:1, 0:1], LNEPS))
        s_dve.inc(v.memset(epsc[0:1, 1:2], -0.5))


    barrier()
    if stop_after == "p0":
        emit_stop()
        st.close()
        return nc

    def emit_ln_stats(tag, W, tiles, ps_stat, sq2, stat4, murep, rsrep,
                      gate_act=None):
        slabs = [(i, min(i + 512, W)) for i in range(0, W, 512)]
        key = id(ps_stat)
        prev_rsev = ln_last_rsev.get(key)

        def sq_slot(dt):
            return sq2[:, (dt % 2) * W:(dt % 2) * W + W]

        for dt in range(DT):
            @blk.scalar
            def _(a, dt=dt):
                if dt == 0 and gate_act is not None:
                    a.wait_ge(*gate_act)
                if dt <= 1 and prev_rsev is not None:
                    a.wait_ge(s_dve.h, prev_rsev)
                if (tag, "st", dt - 2) in M:
                    a.wait_ge(s_pe.h, M[(tag, "st", dt - 2)])
                a.activation(out=sq_slot(dt), in_=tiles(dt), func=AF.Square)
                afence(a)
                M[(tag, "sq", dt)] = s_act.n

            @blk.tensor
            def _(t, dt=dt):
                if dt == 0 and prev_rsev is not None:
                    t.wait_ge(s_dve.h, prev_rsev)
                t.wait_ge(s_act.h, M[(tag, "sq", dt)])
                for (lo, hi) in slabs:
                    t.matmul(ps_stat[0:1, lo:hi], ones_col_f[:], tiles(dt)[:, lo:hi],
                             start=(dt == 0), stop=(dt == DT - 1))
                for (lo, hi) in slabs:
                    mi_ = t.matmul(ps_stat[32:33, lo:hi], ones_col_f[:], sq_slot(dt)[:, lo:hi],
                             start=(dt == 0), stop=(dt == DT - 1))
                s_pe.inc(mi_)
                M[(tag, "st", dt)] = s_pe.n

        @blk.vector
        def _(v):
            v.wait_ge(s_pe.h, M[(tag, "st", DT - 1)])
            v.tensor_scalar_mul(stat4[0:1, 0:W], ps_stat[0:1, 0:W], 1.0 / DM)
            v.tensor_scalar_mul(stat4[0:1, W:2 * W], ps_stat[32:33, 0:W], 1.0 / DM)
            vfence(v)
            M[(tag, "muev")] = s_dve.n

        @blk.scalar
        def _(a):
            a.wait_ge(s_dve.h, M[(tag, "muev")])
            a.activation(out=stat4[0:1, 2 * W:3 * W], in_=stat4[0:1, 0:W],
                         func=AF.Square)
            afence(a)
            M[(tag, "mu2")] = s_act.n

        @blk.vector
        def _(v):
            v.wait_ge(s_act.h, M[(tag, "mu2")])
            v.tensor_sub(stat4[0:1, W:2 * W], stat4[0:1, W:2 * W],
                         stat4[0:1, 2 * W:3 * W])
            vfence(v)
            M[(tag, "var")] = s_dve.n

        @blk.scalar
        def _(a):
            a.wait_ge(s_dve.h, M[(tag, "var")])
            a.activation(out=stat4[0:1, 2 * W:3 * W], in_=stat4[0:1, W:2 * W],
                         func=AF.Ln, bias=epsc[0:1, 0:1])
            afence(a)
            a.activation(out=stat4[0:1, 3 * W:4 * W], in_=stat4[0:1, 2 * W:3 * W],
                         func=AF.Exp, scale=epsc[0:1, 1:2])
            afence(a)
            M[(tag, "rstd")] = s_act.n

        @blk.tensor
        def _(t):
            t.wait_ge(s_act.h, M[(tag, "rstd")])
            for (lo, hi) in slabs:
                mi_ = t.matmul(ps_stat[:, lo:hi], ones_row_f[:], stat4[0:1, lo:hi],
                         start=True, stop=True)
            s_pe.inc(mi_)
            M[(tag, "mubc")] = s_pe.n

        @blk.vector
        def _(v):
            v.wait_ge(s_pe.h, M[(tag, "mubc")])
            s_dve.inc(v.tensor_copy(murep[:, 0:W], ps_stat[:, 0:W]))
            M[(tag, "muget")] = s_dve.n

        @blk.tensor
        def _(t):
            t.wait_ge(s_dve.h, M[(tag, "muget")])
            for (lo, hi) in slabs:
                mi_ = t.matmul(ps_stat[:, lo:hi], ones_row_f[:], stat4[0:1, 3 * W + lo:3 * W + hi],
                         start=True, stop=True)
            s_pe.inc(mi_)
            M[(tag, "rsbc")] = s_pe.n

        @blk.vector
        def _(v):
            v.wait_ge(s_pe.h, M[(tag, "rsbc")])
            s_dve.inc(v.tensor_copy(rsrep[:, 0:W], ps_stat[:, 0:W]))
            M[(tag, "rsev")] = s_dve.n
            ln_last_rsev[key] = s_dve.n

    p1 = ExitStack()
    xt = sbuf("xt", [128, DT * TFP], F32, p1)
    sq2 = sbuf("sq2", [128, 2 * TFP], F32, p1)
    stat4 = sbuf("stat4", [1, 4 * TFP], F32, p1)
    murep = sbuf("murep", [128, TFP], F32, p1)
    rsrep = sbuf("rsrep", [128, TFP], F32, p1)
    xnb = sbuf("xnb", [128, DT * TFP], BF16, p1)
    wstr = sbuf("wstr", [128, 2 * 8 * 512], BF16, p1)
    uraw = sbuf("uraw", [128, 2 * TFP], F32, p1)
    ucv = sbuf("ucv", [128, 2 * TF], F32, p1)
    ub = sbuf("ub", [128, CTF * TF], BF16, p1)
    zb = sbuf("zb", [128, 2 * TF], BF16, p1)
    dwstr = sbuf("dwstr", [128, 2 * 16 * 128], BF16, p1)
    e1t = sbuf("e1t", [128, 2 * TF], F32, p1)
    delt = sbuf("delt", [128, 2 * TF], F32, p1)
    bmev = sbuf("bmev", [16, TF], F32, p1)
    ps_pool = [psum(f"ps_p{i}", [128, 512], stk=p1) for i in range(4)]
    ps_stat = psum("ps_stat", [128, 1536], stk=p1)
    ps_bm = psum("ps_bm", [16, 512], stk=p1)
    ps_evac = {}
    SLABS3 = [(0, 512), (512, 1024), (1024, TFP)]
    USLABS = [(3, 515), (515, 1027)]
    pcnt = [0]

    def ps_next():
        i = pcnt[0] % 4
        pcnt[0] += 1
        return i

    import contextlib as _ctl
    with _ctl.suppress(StopEmission):
      for j in range(NFC):
        xo = 0

        def XT(dt, xo=xo):
            return xt[:, xo + dt * TFP:xo + (dt + 1) * TFP]

        @blk.sync
        def _(sp, j=j, xo=xo):
            if j >= 1:
                sp.wait_ge(s_dve.h, M[("xn", j - 1)])
            M[("x", j)] = s_x.inc(sp.dma_start(
                out=xt[:, xo:xo + DT * TFP],
                in_=x_pad[:, j * DT * TFP:(j + 1) * DT * TFP]), 16)

        @blk.scalar
        def _(a, j=j):
            a.wait_ge(s_x.h, M[("x", j)])
            if j >= 1:
                a.wait_ge(s_dve.h, M[("xn", j - 1)])
        emit_ln_stats(("ln1", j), TFP, XT, ps_stat, sq2, stat4, murep, rsrep)
        maybe_stop(f"p1ln{j}")

        @blk.vector
        def _(v, j=j):
            v.wait_ge(s_dve.h, M[(("ln1", j), "rsev")])
            if j >= 1:
                v.wait_ge(s_pe.h, M[("inproj", j - 1, 7)])
            ii = None
            for dt in range(DT):
                v.tensor_sub(sq2[:, 0:TFP], XT(dt), murep[:, 0:TFP])
                v.scalar_tensor_tensor(sq2[:, TFP:2 * TFP], sq2[:, 0:TFP],
                                       lng_sb[:, dt:dt + 1], rsrep[:, 0:TFP], ALU.mult, ALU.mult)
                ii = v.tensor_scalar_add(xnb[:, dt * TFP:(dt + 1) * TFP],
                                         sq2[:, TFP:2 * TFP], lnb_sb[:, dt:dt + 1])
            if j == 0:
                for dt in range(DT):
                    v.tensor_scalar_mul(xnb[:, dt * TFP:dt * TFP + 3],
                                        xnb[:, dt * TFP:dt * TFP + 3], hmask_sb[:, 0:1])
                vfence(v)
                M[("xn", j)] = s_dve.n
            else:
                s_dve.inc(ii)
                M[("xn", j)] = s_dve.n

        maybe_stop(f"p1xn{j}")
        for grp in range(8):
            wb = (grp % 2) * 8 * 512

            @blk.sync
            def _(sp, j=j, grp=grp, wb=wb):
                gidx = j * 8 + grp
                if gidx >= 2:
                    sp.wait_ge(s_pe.h, M[("inproj", (gidx - 2) // 8, (gidx - 2) % 8)])
                sw = s_iw[gidx % 2]
                M[("inw", gidx)] = (sw.h, sw.inc(sp.dma_start(
                    out=wstr[:, wb:wb + 8 * 512],
                    in_=inw_b[:, grp * 4096:(grp + 1) * 4096]), 16))

            for mi in range(4):
                mt = grp * 4 + mi
                slabs_mt = SLABS3 if mt < CTF else USLABS
                for (lo, hi) in slabs_mt:
                    pi = ps_next()

                    @blk.tensor
                    def _(t, j=j, grp=grp, wb=wb, mi=mi, mt=mt, lo=lo, hi=hi, pi=pi):
                        gidx = j * 8 + grp
                        if mi == 0 and lo == SLABS3[0][0] or (mi == 0 and lo == slabs_mt[0][0]):
                            t.wait_ge(*M[("inw", gidx)])
                            t.wait_ge(s_dve.h, M[("xn", j)])
                        if pi in ps_evac:
                            t.wait_ge(*ps_evac[pi])
                        for kt in range(DT):
                            mi_ = t.matmul(ps_pool[pi][:, 0:hi - lo],
                                     wstr[:, wb + kt * 512 + mi * 128:wb + kt * 512 + mi * 128 + 128],
                                     xnb[:, kt * TFP + lo:kt * TFP + hi],
                                     start=(kt == 0), stop=(kt == DT - 1))
                        s_pe.inc(mi_)
                        M[("mm", j, mt, lo)] = (s_pe.n, pi)
                        if mi == 3 and (lo, hi) == slabs_mt[-1]:
                            M[("inproj", j, grp)] = s_pe.n

                    @blk.vector
                    def _(v, j=j, mt=mt, lo=lo, hi=hi):
                        cnt, pi2 = M[("mm", j, mt, lo)]
                        if mt < CTF:
                            ct = mt
                            if lo == 0:
                                if ct >= 2:
                                    v.wait_ge(s_act.h, M[("silu", j, ct - 2)])
                                elif j >= 1:
                                    v.wait_ge(s_act.h, M[("silu", j - 1, CTF - 2 + ct)])
                            dst = uraw[:, (ct % 2) * TFP + lo:(ct % 2) * TFP + hi]
                        else:
                            ct = mt - CTF
                            if lo == 3:
                                if ("spz", j, ct - 1) in M:
                                    v.wait_ge(s_sp.h, M[("spz", j, ct - 1)])
                                elif j >= 1 and ct == 0:
                                    v.wait_ge(s_sp.h, M[("spz", j - 1, CTF - 1)])
                            dst = zb[:, (ct % 2) * TF + lo - 3:(ct % 2) * TF + hi - 3]
                        v.wait_ge(s_pe.h, cnt)
                        ii = v.tensor_copy(dst, ps_pool[pi2][:, 0:hi - lo])
                        ps_evac[pi2] = (s_dve.h, s_dve.inc(ii))
                        if mt >= CTF and (lo, hi) == USLABS[-1]:
                            M[("zev", j, ct)] = s_dve.n

                if mt < CTF:
                    ct = mt

                    @blk.vector
                    def _(v, j=j, ct=ct, mt=mt):
                        ur = uraw[:, (ct % 2) * TFP:(ct % 2) * TFP + TFP]
                        uc = ucv[:, (ct % 2) * TF:(ct % 2) * TF + TF]
                        v.tensor_scalar_mul(uc, ur[:, 0:TF], cw_sb[:, ct * 4:ct * 4 + 1])
                        v.scalar_tensor_tensor(uc, ur[:, 1:1 + TF], cw_sb[:, ct * 4 + 1:ct * 4 + 2],
                                               uc, ALU.mult, ALU.add)
                        v.scalar_tensor_tensor(uc, ur[:, 2:2 + TF], cw_sb[:, ct * 4 + 2:ct * 4 + 3],
                                               uc, ALU.mult, ALU.add)
                        ii = v.scalar_tensor_tensor(uc, ur[:, 3:3 + TF], cw_sb[:, ct * 4 + 3:ct * 4 + 4],
                                                    uc, ALU.mult, ALU.add)
                        s_dve.inc(ii)
                        M[("conv", j, ct)] = s_dve.n

                    @blk.scalar
                    def _(a, j=j, ct=ct):
                        if j >= 1 and ct == 0:
                            a.wait_ge(s_pe.h, M[("dtmm", j - 1, CTF - 1)])
                            a.wait_ge(s_pe.h, M[("bm", j - 1, 1)])
                            a.wait_ge(s_sp.h, M[("spu", j - 1, CTF - 1)])
                        a.wait_ge(s_dve.h, M[("conv", j, ct)])
                        a.activation(out=ub[:, ct * TF:(ct + 1) * TF],
                                     in_=ucv[:, (ct % 2) * TF:(ct % 2) * TF + TF],
                                     func=AF.Silu, bias=cb_sb[:, ct:ct + 1])
                        afence(a)
                        M[("silu", j, ct)] = s_act.n

                    @blk.gpsimd
                    def _(g, j=j, ct=ct):
                        g.wait_ge(s_act.h, M[("silu", j, ct)])
                        M[("spu", j, ct)] = s_sp.inc(g.dma_start(
                            out=ux_in[j * DI + ct * 128:j * DI + (ct + 1) * 128, :],
                            in_=ub[:, ct * TF:(ct + 1) * TF]), 16)
                else:
                    ct = mt - CTF

                    @blk.gpsimd
                    def _(g, j=j, ct=ct):
                        g.wait_ge(s_dve.h, M[("zev", j, ct)])
                        M[("spz", j, ct)] = s_sp.inc(g.dma_start(
                            out=zx_in[j * DI + ct * 128:j * DI + (ct + 1) * 128, :],
                            in_=zb[:, (ct % 2) * TF:(ct % 2) * TF + TF]), 16)

        maybe_stop(f"p1ip{j}")
        for si, (lo, hi) in enumerate(USLABS):
            @blk.tensor
            def _(t, j=j, si=si, lo=lo, hi=hi):
                if si == 1:
                    t.wait_ge(s_dve.h, M[("bmev_s", j, 0)])
                elif j >= 1:
                    t.wait_ge(s_dve.h, M[("bmev_s", j - 1, 1)])
                for kt in range(CTF):
                    if si == 0:
                        t.wait_ge(s_act.h, M[("silu", j, kt)])
                    mi_ = t.matmul(ps_bm[:, 0:512], xpw_sb[:, kt * DS:(kt + 1) * DS],
                             ub[:, kt * TF + lo - 3:kt * TF + hi - 3],
                             start=(kt == 0), stop=(kt == CTF - 1))
                s_pe.inc(mi_)
                M[("bm", j, si)] = s_pe.n

            @blk.vector
            def _(v, j=j, si=si, lo=lo, hi=hi):
                if j >= 1 and si == 0:
                    v.wait_ge(s_sp.h, M[("spbm", j - 1)])
                v.wait_ge(s_pe.h, M[("bm", j, si)])
                ii = v.tensor_copy(bmev[:, lo - 3:hi - 3], ps_bm[:, 0:512])
                M[("bmev_s", j, si)] = s_dve.inc(ii)

        @blk.gpsimd
        def _(g, j=j):
            g.wait_ge(s_dve.h, M[("bmev_s", j, 1)])
            M[("spbm", j)] = s_sp.inc(g.dma_start(
                out=bmx_in[j * DS:(j + 1) * DS, :], in_=bmev[:]), 16)

        maybe_stop(f"p1bm{j}")
        for mt in range(CTF):
            wb = (mt % 2) * 16 * 128

            @blk.sync
            def _(sp, j=j, mt=mt, wb=wb):
                gidx = j * CTF + mt
                if gidx >= 2:
                    sp.wait_ge(s_pe.h, M[("dtmm", (gidx - 2) // CTF, (gidx - 2) % CTF)])
                sw = s_dw[gidx % 2]
                M[("dtw", gidx)] = (sw.h, sw.inc(sp.dma_start(
                    out=dwstr[:, wb:wb + 16 * 128],
                    in_=dtw_b[:, mt * 2048:(mt + 1) * 2048]), 16))

            @blk.tensor
            def _(t, j=j, mt=mt, wb=wb):
                t.wait_ge(*M[("dtw", j * CTF + mt)])
                for si, (lo, hi) in enumerate(USLABS):
                    pi = ps_next()
                    if pi in ps_evac:
                        t.wait_ge(*ps_evac[pi])
                    for kt in range(CTF):
                        if si == 0 and mt == 0:
                            t.wait_ge(s_act.h, M[("silu", j, kt)])
                        mi_ = t.matmul(ps_pool[pi][:, 0:512],
                                 dwstr[:, wb + kt * 128:wb + (kt + 1) * 128],
                                 ub[:, kt * TF + lo - 3:kt * TF + hi - 3],
                                 start=(kt == 0), stop=(kt == CTF - 1))
                    s_pe.inc(mi_)
                    M[("dtps", j, mt, si)] = (s_pe.n, pi)
                M[("dtmm", j, mt)] = s_pe.n

            @blk.scalar
            def _(a, j=j, mt=mt):
                dl = delt[:, (mt % 2) * TF:(mt % 2) * TF + TF]
                e1 = e1t[:, (mt % 2) * TF:(mt % 2) * TF + TF]
                if ("spdt", j, mt - 1) in M:
                    a.wait_ge(s_sp.h, M[("spdt", j, mt - 1)])
                elif j >= 1 and mt < 1:
                    a.wait_ge(s_sp.h, M[("spdt", j - 1, CTF - 1)])
                for si, (lo, hi) in enumerate(USLABS):
                    cnt, pi = M[("dtps", j, mt, si)]
                    a.wait_ge(s_pe.h, cnt)
                    ii = a.activation(out=e1[:, lo - 3:hi - 3], in_=ps_pool[pi][:, 0:512],
                                      func=AF.Exp, bias=dtb_sb[:, mt:mt + 1])
                    ps_evac[pi] = (s_act.h, s_act.inc(ii))
                a.activation(out=dl, in_=e1, func=AF.Ln, bias=1.0)
                afence(a)
                M[("dtact", j, mt)] = s_act.n

            @blk.gpsimd
            def _(g, j=j, mt=mt):
                g.wait_ge(s_act.h, M[("dtact", j, mt)])
                M[("spdt", j, mt)] = s_sp.inc(g.dma_start(
                    out=dx_in[j * DI + mt * 128:j * DI + (mt + 1) * 128, :],
                    in_=delt[:, (mt % 2) * TF:(mt % 2) * TF + TF]), 16)

        maybe_stop(f"p1dt{j}")

        @blk.gpsimd
        def _(g, j=j):
            g.wait_ge(s_sp.h, s_sp.n)
            for (ti, to, R) in ((dx_in, dx_out, DI), (ux_in, ux_out, DI),
                                (zx_in, zx_out, DI), (bmx_in, bmx_out, DS)):
                s_cc.inc(g.collective_compute(
                    "AllGather", ALU.bypass, PAIRS,
                    [ti[j * R:(j + 1) * R, :].opt()],
                    [to[j * 2 * R:(j + 1) * 2 * R, :].opt()]))
            M[("a2a", j)] = s_cc.n
        maybe_stop(f"p1cc{j}")

    barrier()
    p1.close()
    if stop_after is not None and stop_after.startswith("p1"):
        emit_stop()
        st.close()
        return nc

    if debug:
        @blk.gpsimd
        def _(g):
            s_sp.inc(g.dma_start(out=dbg["delta"][:], in_=dx_in[:]), 16)
            s_sp.inc(g.dma_start(out=dbg["u"][:], in_=ux_in[:]), 16)
            s_sp.inc(g.dma_start(out=dbg["bm"][:], in_=bmx_in[:]), 16)
        barrier()

    p2 = ExitStack()
    dlt2 = sbuf("dlt2", [128, 2 * TC], F32, p2)
    ut2 = sbuf("ut2", [128, 2 * TC], BF16, p2)
    zt2 = sbuf("zt2", [128, 2 * TC], BF16, p2)
    udl = sbuf("udl", [128, TC], BF16, p2)
    bm2b = sbuf("bm2b", [1, 2 * TC], BF16, p2)
    bmrep = sbuf("bmrep", [128, DS * TC], BF16, p2)
    dat = sbuf("dat", [128, 2 * TC], F32, p2)
    wt2 = sbuf("wt2", [128, 2 * TC], BF16, p2)
    ht2 = sbuf("ht2", [128, 2 * TC], BF16, p2)
    carry = sbuf("carry", [128, CTS * DS], F32, p2)
    y2t = sbuf("y2t", [128, 2 * TC], BF16, p2)
    gst = sbuf("gst", [128, 2 * TC], BF16, p2)
    ygt2 = sbuf("ygt2", [128, 2 * TC], BF16, p2)
    ps_y = psum("ps_y", [128, TC], stk=p2)
    ps_e = psum("ps_e", [128, 1024], stk=p2)

    @blk.vector
    def _(v):
        s_dve.inc(v.memset(carry[:], 0.0))

    for g in range(NSC):
        rh = g // 2
        c0 = (g % 2) * TC

        for s in range(DS):
            @blk.gpsimd
            def _(gp, g=g, rh=rh, c0=c0, s=s):
                if g == 0 and s == 0:
                    gp.wait_ge(s_cc.h, M[("a2a", NFC - 1)])
                if ("bmbc", g, (s - 2) * 4 + 3) in M:
                    gp.wait_ge(s_pe.h, M[("bmbc", g, (s - 2) * 4 + 3)])
                elif ("bmbc", g - 1, (DS - 2 + s) * 4 + 3) in M:
                    gp.wait_ge(s_pe.h, M[("bmbc", g - 1, (DS - 2 + s) * 4 + 3)])
                if g >= 1:
                    gp.wait_ge(s_dve.h, M[("wld", (g - 1) * CTS + CTS - 1, s)])
                for jsub in range(2):
                    j = (g % 2) * 2 + jsub
                    r = j * 2 * DS + rh * DS + s
                    M[("bm2", g, s)] = s_sp.inc(gp.dma_start(
                        out=bm2b[0:1, (s % 2) * TC + jsub * TF:(s % 2) * TC + (jsub + 1) * TF],
                        in_=bmx_out[r:r + 1, :]), 16)

            for si in range(4):
                @blk.tensor
                def _(t, g=g, s=s, si=si):
                    if si == 0:
                        t.wait_ge(s_sp.h, M[("bm2", g, s)])
                    idx = s * 4 + si
                    half = idx % 2
                    if (("bmrd", g, idx - 2)) in M:
                        t.wait_ge(s_act.h, M[("bmrd", g, idx - 2)])
                    elif ("bmrd", g - 1, DS * 4 - 2 + idx) in M:
                        t.wait_ge(s_act.h, M[("bmrd", g - 1, DS * 4 - 2 + idx)])
                    mi_ = t.matmul(ps_e[:, half * 512:half * 512 + 512], ones_row_b[:],
                             bm2b[0:1, (s % 2) * TC + si * 512:(s % 2) * TC + (si + 1) * 512],
                             start=True, stop=True)
                    s_pe.inc(mi_)
                    M[("bmbc", g, idx)] = s_pe.n

                @blk.scalar
                def _(a, g=g, s=s, si=si):
                    idx = s * 4 + si
                    half = idx % 2
                    a.wait_ge(s_pe.h, M[("bmbc", g, idx)])
                    ii = a.activation(out=bmrep[:, s * TC + si * 512:s * TC + (si + 1) * 512],
                                      in_=ps_e[:, half * 512:half * 512 + 512], func=AF.Copy)
                    s_act.inc(ii)
                    if si == 3:
                        afence(a)
                    M[("bmrd", g, idx)] = s_act.n

            if debug and g == 0 and s == 0:
                @blk.gpsimd
                def _(gp):
                    gp.wait_ge(s_act.h, M[("bmrd", 0, 3)])
                    s_sp.inc(gp.dma_start(out=dbg["bmr"][:, 0:TC], in_=bmrep[:, 0:TC]), 16)
                    M[("dbgbm",)] = s_sp.inc(gp.dma_start(out=dbg["bmr"][0:1, TC:TC + 64],
                                                          in_=bm2b[0:1, 448:512]), 16)

                @blk.tensor
                def _(t):
                    t.wait_ge(s_sp.h, M[("dbgbm",)])

                @blk.scalar
                def _(a):
                    a.wait_ge(s_sp.h, M[("dbgbm",)])

        for ct in range(CTS):
            i = g * CTS + ct
            sl = (i % 2) * TC

            @blk.gpsimd
            def _(gp, g=g, ct=ct, i=i, sl=sl, rh=rh, c0=c0):
                gp.wait_ge(s_cc.h, M[("a2a", NFC - 1)])
                if i >= 2:
                    gp.wait_ge(s_act.h, M[("dA", i - 2, DS - 1)])
                    gp.wait_ge(s_dve.h, M[("yg", i - 2)])
                for jsub in range(2):
                    icol = (g * 2 + jsub) * CTS + ct
                    iox = IndirectOffsetOnAxis(ap=scidx_sb[:, icol:icol + 1], axis=0)
                    s_sp.inc(gp.indirect_dma_start(
                        out=dlt2[:, sl + jsub * TF:sl + (jsub + 1) * TF], out_offset=None,
                        in_=dx_out[:, :], in_offset=iox), 16)
                    s_sp.inc(gp.indirect_dma_start(
                        out=ut2[:, sl + jsub * TF:sl + (jsub + 1) * TF], out_offset=None,
                        in_=ux_out[:, :], in_offset=iox), 16)
                    M[("ld2", i)] = s_sp.inc(gp.indirect_dma_start(
                        out=zt2[:, sl + jsub * TF:sl + (jsub + 1) * TF], out_offset=None,
                        in_=zx_out[:, :], in_offset=iox), 16)

            @blk.vector
            def _(v, g=g, ct=ct, i=i, sl=sl):
                v.wait_ge(s_sp.h, M[("ld2", i)])
                ii = v.tensor_mul(udl[:], ut2[:, sl:sl + TC], dlt2[:, sl:sl + TC])
                s_dve.inc(ii)
                M[("udl", i)] = s_dve.n

            if debug and i == 0:
                @blk.gpsimd
                def _(gp, sl=sl):
                    gp.wait_ge(s_dve.h, M[("udl", 0)])
                    s_sp.inc(gp.dma_start(out=dbg["scan0"][:, 6 * TC:7 * TC], in_=udl[:]), 16)
                    M[("dbgu",)] = s_sp.inc(gp.dma_start(
                        out=dbg["scan0"][:, 7 * TC:8 * TC], in_=dlt2[:, sl:sl + TC]), 16)

                @blk.vector
                def _(v):
                    v.wait_ge(s_sp.h, M[("dbgu",)])

            for s in range(DS):
                hs = (s % 2) * TC

                @blk.scalar
                def _(a, g=g, ct=ct, i=i, s=s, sl=sl, hs=hs):
                    if s == 0:
                        a.wait_ge(s_sp.h, M[("ld2", i)])
                    if ("scan", i, s - 2) in M:
                        a.wait_ge(s_dve.h, M[("scan", i, s - 2)])
                    elif ("scan", i - 1, DS - 2 + s) in M:
                        a.wait_ge(s_dve.h, M[("scan", i - 1, DS - 2 + s)])
                    a.activation(out=dat[:, hs:hs + TC], in_=dlt2[:, sl:sl + TC],
                                 func=AF.Exp, scale=a_sb[:, ct * DS + s:ct * DS + s + 1])
                    afence(a)
                    M[("dA", i, s)] = s_act.n

                @blk.vector
                def _(v, g=g, ct=ct, i=i, s=s, sl=sl, hs=hs):
                    v.wait_ge(s_dve.h, M[("udl", i)])
                    v.wait_ge(s_act.h, M[("bmrd", g, 4 * s + 3)])
                    if ("scan", i, s - 2) in M:
                        v.wait_ge(s_dve.h, M[("scan", i, s - 2)])
                    ii = v.tensor_mul(wt2[:, hs:hs + TC], udl[:],
                                      bmrep[:, s * TC:(s + 1) * TC])
                    M[("wld", i, s)] = s_dve.inc(ii)
                    if s >= 1:
                        v.tensor_copy(carry[:, ct * DS + s - 1:ct * DS + s],
                                      ht2[:, ((s - 1) % 2) * TC + TC - 1:((s - 1) % 2) * TC + TC])
                    v.wait_ge(s_act.h, M[("dA", i, s)])
                    if ("yred", i, s - 2) in M:
                        v.wait_ge(s_pe.h, M[("yred", i, s - 2)])
                    elif ("yred", i - 1, DS - 2 + s) in M:
                        v.wait_ge(s_pe.h, M[("yred", i - 1, DS - 2 + s)])
                    ii = v.tensor_tensor_scan(ht2[:, hs:hs + TC], dat[:, hs:hs + TC],
                                              wt2[:, hs:hs + TC],
                                              carry[:, ct * DS + s:ct * DS + s + 1],
                                              ALU.mult, ALU.add)
                    M[("scan", i, s)] = s_dve.inc(ii)

                if debug and i == 0 and s in (0, 1):
                    @blk.gpsimd
                    def _(gp, s=s, hs=hs, sl=sl):
                        gp.wait_ge(s_dve.h, M[("scan", 0, s)])
                        s_sp.inc(gp.dma_start(out=dbg["scan0"][:, (s * 3) * TC:(s * 3 + 1) * TC],
                                              in_=dat[:, hs:hs + TC]), 16)
                        s_sp.inc(gp.dma_start(out=dbg["scan0"][:, (s * 3 + 1) * TC:(s * 3 + 2) * TC],
                                              in_=wt2[:, hs:hs + TC]), 16)
                        M[("dbgs", s)] = s_sp.inc(gp.dma_start(
                            out=dbg["scan0"][:, (s * 3 + 2) * TC:(s * 3 + 3) * TC],
                            in_=ht2[:, hs:hs + TC]), 16)

                    @blk.vector
                    def _(v, s=s):
                        v.wait_ge(s_sp.h, M[("dbgs", s)])

                    @blk.scalar
                    def _(a, s=s):
                        a.wait_ge(s_sp.h, M[("dbgs", s)])

                @blk.tensor
                def _(t, g=g, ct=ct, i=i, s=s, hs=hs):
                    t.wait_ge(s_dve.h, M[("scan", i, s)])
                    if s == 0 and ("y2", i - 1) in M:
                        t.wait_ge(s_dve.h, M[("y2", i - 1)])
                    for si in range(4):
                        mi_ = t.matmul(ps_y[:, si * 512:(si + 1) * 512], ident_b[:],
                                 ht2[:, hs + si * 512:hs + (si + 1) * 512],
                                 start=(s == 0), stop=(s == DS - 1))
                    s_pe.inc(mi_)
                    M[("yred", i, s)] = s_pe.n

            @blk.scalar
            def _(a, g=g, i=i, sl=sl):
                a.wait_ge(s_sp.h, M[("ld2", i)])
                if ("ygsp", i - 1) in M:
                    a.wait_ge(s_sp.h, M[("ygsp", i - 1)])
                a.activation(out=gst[:, sl:sl + TC], in_=zt2[:, sl:sl + TC], func=AF.Silu)
                afence(a)
                M[("gs", i)] = s_act.n

            @blk.vector
            def _(v, g=g, ct=ct, i=i, sl=sl):
                v.wait_ge(s_pe.h, M[("yred", i, DS - 1)])
                v.tensor_copy(carry[:, ct * DS + DS - 1:ct * DS + DS],
                              ht2[:, ((DS - 1) % 2) * TC + TC - 1:((DS - 1) % 2) * TC + TC])
                ii = v.scalar_tensor_tensor(y2t[:, sl:sl + TC], ut2[:, sl:sl + TC],
                                            d_sb[:, ct:ct + 1], ps_y[:, 0:TC],
                                            ALU.mult, ALU.add)
                M[("y2", i)] = s_dve.inc(ii)
                v.wait_ge(s_act.h, M[("gs", i)])
                if ("ygsp", i - 1) in M:
                    v.wait_ge(s_sp.h, M[("ygsp", i - 1)])
                ii = v.tensor_mul(ygt2[:, sl:sl + TC], y2t[:, sl:sl + TC], gst[:, sl:sl + TC])
                M[("yg", i)] = s_dve.inc(ii)

            @blk.gpsimd
            def _(gp, g=g, ct=ct, i=i, sl=sl):
                gp.wait_ge(s_dve.h, M[("yg", i)])
                M[("ygsp", i)] = s_sp.inc(gp.dma_start(
                    out=yg_in[g * (DI // 2) + ct * 128:g * (DI // 2) + (ct + 1) * 128, :],
                    in_=ygt2[:, sl:sl + TC]), 16)

        @blk.gpsimd
        def _(gp, g=g):
            gp.wait_ge(s_sp.h, M[("ygsp", g * CTS + CTS - 1)])
            s_cc.inc(gp.collective_compute(
                "AllGather", ALU.bypass, PAIRS,
                [yg_in[g * (DI // 2):(g + 1) * (DI // 2), :].opt()],
                [ygath[g * DI:(g + 1) * DI, :].opt()]))
            M[("agy", g)] = s_cc.n

    barrier()
    p2.close()
    if stop_after == "p2":
        emit_stop()
        st.close()
        return nc

    TCH = 1024
    p3 = ExitStack()
    ygld = sbuf("ygld", [128, CTF * TCH], BF16, p3)
    owstr = sbuf("owstr", [128, 2 * 16 * 128], BF16, p3)
    xres = sbuf("xres", [128, 2 * TCH], F32, p3)
    r2t = sbuf("r2t", [128, DT * TCH], F32, p3)
    sq3 = sbuf("sq3", [128, 2 * TCH], F32, p3)
    stat3 = sbuf("stat3", [1, 4 * TCH], F32, p3)
    mur3 = sbuf("mur3", [128, TCH], F32, p3)
    rsr3 = sbuf("rsr3", [128, TCH], F32, p3)
    x1ft = sbuf("x1ft", [128, 2 * TCH], F32, p3)
    x1bt = sbuf("x1bt", [128, 2 * TCH], BF16, p3)
    ps_o = [psum(f"ps_o{i}", [128, 512], stk=p3) for i in range(4)]
    ps_st3 = psum("ps_st3", [128, 1536], stk=p3)
    opcnt = [0]
    ops_evac = {}

    for lj in range(4):
        @blk.gpsimd
        def _(gp, lj=lj):
            gp.wait_ge(s_cc.h, M[("agy", NSC - 1)])
            for kt in range(CTF):
                if lj >= 1:
                    gp.wait_ge(s_pe.h, M[("opmmd", lj * DT - 1)])
                M[("ygld", lj, kt)] = s_sp.inc(gp.indirect_dma_start(
                    out=ygld[:, kt * TCH:(kt + 1) * TCH], out_offset=None,
                    in_=ygath[:, :],
                    in_offset=IndirectOffsetOnAxis(
                        ap=ygidx_sb[:, (lj // 2) * CTF + kt:(lj // 2) * CTF + kt + 1], axis=0),
                    element_offset=(lj % 2) * TCH), 16)

        for mt in range(DT):
            wb = (mt % 2) * 16 * 128
            gidx = lj * DT + mt

            @blk.sync
            def _(sp, lj=lj, mt=mt, wb=wb, gidx=gidx):
                if gidx >= 2:
                    sp.wait_ge(s_pe.h, M[("opmmd", gidx - 2)])
                sw = s_ow[gidx % 2]
                M[("ow", gidx)] = (sw.h, sw.inc(sp.dma_start(
                    out=owstr[:, wb:wb + 16 * 128],
                    in_=outw_b[:, mt * 2048:(mt + 1) * 2048]), 16))
                if gidx >= 2:
                    sp.wait_ge(s_dve.h, M[("r2", gidx - 2)])
                M[("xres", gidx)] = s_x.inc(sp.dma_start(
                    out=xres[:, (gidx % 2) * TCH:(gidx % 2) * TCH + TCH],
                    in_=x_pad[:, lj * DT * TFP + mt * TFP + 3:lj * DT * TFP + mt * TFP + 3 + TCH]), 16)

            @blk.tensor
            def _(t, lj=lj, mt=mt, wb=wb, gidx=gidx):
                t.wait_ge(*M[("ow", gidx)])
                t.wait_ge(s_sp.h, M[("ygld", lj, CTF - 1)])
                for si in range(2):
                    pi = opcnt[0] % 4
                    opcnt[0] += 1
                    if pi in ops_evac:
                        t.wait_ge(s_dve.h, ops_evac[pi])
                    for kt in range(CTF):
                        mi_ = t.matmul(ps_o[pi][:, 0:512],
                                 owstr[:, wb + kt * 128:wb + (kt + 1) * 128],
                                 ygld[:, kt * TCH + si * 512:kt * TCH + (si + 1) * 512],
                                 start=(kt == 0), stop=(kt == CTF - 1))
                    s_pe.inc(mi_)
                    M[("opmm", lj, mt, si)] = (s_pe.n, pi)
                M[("opmmd", gidx)] = s_pe.n

            @blk.vector
            def _(v, lj=lj, mt=mt, gidx=gidx):
                v.wait_ge(s_x.h, M[("xres", gidx)])
                for si in range(2):
                    cnt, pi = M[("opmm", lj, mt, si)]
                    v.wait_ge(s_pe.h, cnt)
                    ii = v.scalar_tensor_tensor(
                        r2t[:, mt * TCH + si * 512:mt * TCH + (si + 1) * 512],
                        xres[:, (gidx % 2) * TCH + si * 512:(gidx % 2) * TCH + (si + 1) * 512],
                        2.0, ps_o[pi][:, 0:512], ALU.mult, ALU.add)
                    ops_evac[pi] = s_dve.inc(ii)
                M[("r2", gidx)] = s_dve.n

        emit_ln_stats(("ln2", lj), TCH, lambda dt: r2t[:, dt * TCH:(dt + 1) * TCH],
                      ps_st3, sq3, stat3, mur3, rsr3,
                      gate_act=(s_dve.h, M[("r2", lj * DT + DT - 1)]))

        for dt in range(DT):
            @blk.vector
            def _(v, lj=lj, dt=dt):
                if dt == 0:
                    v.wait_ge(s_dve.h, M[(("ln2", lj), "rsev")])
                if ("x1sp", lj, dt - 1) in M:
                    v.wait_ge(s_sp.h, M[("x1sp", lj, dt - 1)])
                elif lj >= 1 and dt == 0:
                    v.wait_ge(s_sp.h, M[("x1sp", lj - 1, DT - 1)])
                xo = (dt % 2) * TCH
                v.tensor_sub(sq3[:, 0:TCH], r2t[:, dt * TCH:(dt + 1) * TCH], mur3[:])
                ii = v.tensor_mul(x1ft[:, xo:xo + TCH], sq3[:, 0:TCH], rsr3[:])
                s_dve.inc(ii)
                ii = v.tensor_copy(x1bt[:, xo:xo + TCH], x1ft[:, xo:xo + TCH])
                s_dve.inc(ii)
                ii = v.reduce_sum(pool_cols[:, lj * DT + dt:lj * DT + dt + 1],
                                  x1ft[:, xo:xo + TCH], axis=mybir.AxisListType.X)
                s_dve.inc(ii)
                M[("x1", lj, dt)] = s_dve.n

            @blk.gpsimd
            def _(gp, lj=lj, dt=dt):
                gp.wait_ge(s_dve.h, M[("x1", lj, dt)])
                xo = (dt % 2) * TCH
                s_sp.inc(gp.dma_start(
                    out=x1f_scr[dt * 128:(dt + 1) * 128, lj * TCH:(lj + 1) * TCH],
                    in_=x1ft[:, xo:xo + TCH]), 16)
                M[("x1sp", lj, dt)] = s_sp.inc(gp.dma_start(
                    out=x1b_scr[dt * 128:(dt + 1) * 128, lj * TCH:(lj + 1) * TCH],
                    in_=x1bt[:, xo:xo + TCH]), 16)

    barrier()
    p3.close()
    if stop_after == "p25":
        emit_stop()
        st.close()
        return nc

    p4 = ExitStack()
    pooled_sb = sbuf("pooled_sb", [128, DT], F32, p4)
    lg_sb = sbuf("lg_sb", [1, 8], F32, p4)
    m8_sb = sbuf("m8_sb", [1, 8], F32, p4)
    i8_sb = sbuf("i8_sb", [1, 8], U32, p4)
    i8f_sb = sbuf("i8f_sb", [1, 8], F32, p4)
    wpad = sbuf("wpad", [1, 3 * 128], F32, p4)
    ebc = sbuf("ebc", [128, 2], F32, p4)
    wbc = sbuf("wbc", [128, 2], F32, p4)
    ebi = sbuf("ebi", [2, 2], U32, p4)
    w1idx = sbuf("w1idx", [128, 16], U32, p4)
    w2idx = sbuf("w2idx", [128, 64], U32, p4)
    idxf = sbuf("idxf", [128, 80], F32, p4)
    ps_r = psum("ps_r", [128, 512], stk=p4)

    @blk.vector
    def _(v):
        v.wait_ge(s_dve.h, M[("x1", 3, DT - 1)])
        v.tensor_add(pool_cols[:, 0:DT], pool_cols[:, 0:DT], pool_cols[:, DT:2 * DT])
        v.tensor_add(pool_cols[:, 2 * DT:3 * DT], pool_cols[:, 2 * DT:3 * DT],
                     pool_cols[:, 3 * DT:4 * DT])
        vfence(v)
        v.tensor_add(pool_cols[:, 0:DT], pool_cols[:, 0:DT], pool_cols[:, 2 * DT:3 * DT])
        vfence(v)
        v.tensor_scalar_mul(pooled_sb[:], pool_cols[:, 0:DT], 1.0 / float(T))
        vfence(v)
        M["poolloc"] = s_dve.n

    @blk.gpsimd
    def _(gp):
        gp.wait_ge(s_dve.h, M["poolloc"])
        s_sp.inc(gp.dma_start(out=pool_in[:], in_=pooled_sb[:]), 16)
        gp.wait_ge(s_sp.h, s_sp.n)
        s_cc.inc(gp.collective_compute("AllReduce", ALU.add, PAIRS,
                                       [pool_in.opt()], [pool_out.opt()]))
        gp.wait_ge(s_cc.h, s_cc.n)
        M["poolcc"] = s_sp.inc(gp.dma_start(out=pooled_sb[:], in_=pool_out[:]), 16)

    @blk.tensor
    def _(t):
        t.wait_ge(s_sp.h, M["poolcc"])
        for kt in range(DT):
            mi_ = t.matmul(ps_r[0:1, 0:NE], pooled_sb[:, kt:kt + 1], wr_sb[:, kt * NE:(kt + 1) * NE],
                     start=(kt == 0), stop=(kt == DT - 1))
        s_pe.inc(mi_)
        M["lg"] = s_pe.n

    @blk.vector
    def _(v):
        v.wait_ge(s_pe.h, M["lg"])
        v.tensor_copy(lg_sb[:], ps_r[0:1, 0:NE])
        vfence(v)
        v.max(out=m8_sb[:], in_=lg_sb[:])
        vfence(v)
        v.max_index(out=i8_sb[:], in_max=m8_sb[:], in_values=lg_sb[:])
        vfence(v)
        v.tensor_copy(i8f_sb[:], i8_sb[:])
        vfence(v)
        v.tensor_sub(wpad[0:1, 0:128], m8_sb[0:1, 1:2].to_broadcast([1, 128]),
                     m8_sb[0:1, 0:1].to_broadcast([1, 128]))
        vfence(v)
        M["wdiff"] = s_dve.n

    @blk.scalar
    def _(a):
        a.wait_ge(s_dve.h, M["wdiff"])
        a.activation(out=wpad[0:1, 128:256], in_=wpad[0:1, 0:128], func=AF.Exp)
        afence(a)
        M["wexp"] = s_act.n

    @blk.vector
    def _(v):
        v.wait_ge(s_act.h, M["wexp"])
        v.tensor_scalar_add(wpad[0:1, 256:384], wpad[0:1, 128:256], 1.0)
        vfence(v)
        v.reciprocal(wpad[0:1, 0:128], wpad[0:1, 256:384])
        vfence(v)
        v.tensor_scalar(wpad[0:1, 128:256], wpad[0:1, 0:128], -1.0, 1.0,
                        ALU.mult, ALU.add)
        vfence(v)
        M["wgts"] = s_dve.n

    @blk.tensor
    def _(t):
        t.wait_ge(s_dve.h, M["wgts"])
        t.matmul(ps_r[:, 0:2], ones_row_f[:], i8f_sb[0:1, 0:2], start=True, stop=True)
        t.matmul(ps_r[:, 4:5], ones_row_f[:], wpad[0:1, 0:1], start=True, stop=True)
        mi_ = t.matmul(ps_r[:, 5:6], ones_row_f[:], wpad[0:1, 128:129], start=True, stop=True)
        s_pe.inc(mi_)
        M["ebcmm"] = s_pe.n

    @blk.vector
    def _(v):
        v.wait_ge(s_pe.h, M["ebcmm"])
        v.tensor_copy(ebc[:], ps_r[:, 0:2])
        v.tensor_copy(wbc[:], ps_r[:, 4:6])
        vfence(v)
        ii = None
        for slot in range(2):
            for kt in range(DT):
                ii = v.scalar_tensor_tensor(idxf[:, slot * 8 + kt:slot * 8 + kt + 1],
                                            ebc[:, slot:slot + 1], 1024.0,
                                            iota_sb[:, kt:kt + 1], ALU.mult, ALU.add)
        for slot in range(2):
            for kt2 in range(32):
                ii = v.scalar_tensor_tensor(idxf[:, 16 + slot * 32 + kt2:16 + slot * 32 + kt2 + 1],
                                            ebc[:, slot:slot + 1], 4096.0,
                                            iota_sb[:, kt2:kt2 + 1], ALU.mult, ALU.add)
        vfence(v)
        v.tensor_copy(w1idx[:], idxf[:, 0:16])
        v.tensor_copy(w2idx[:], idxf[:, 16:80])
        v.tensor_copy(ebi[:], ebc[0:2, 0:2])
        vfence(v)
        M["idx"] = s_dve.n

    if debug:
        @blk.gpsimd
        def _(gp):
            gp.wait_ge(s_dve.h, M["idx"])
            s_sp.inc(gp.dma_start(out=dbg["rout"][0:1, 0:8], in_=m8_sb[:]), 16)
            s_sp.inc(gp.dma_start(out=dbg["rout"][0:1, 8:16], in_=i8f_sb[:]), 16)
            s_sp.inc(gp.dma_start(out=dbg["yg"][:], in_=yg_in[:]), 16)
            s_sp.inc(gp.dma_start(out=dbg["x1"][:], in_=x1f_scr[:]), 16)

    barrier()
    if stop_after == "p3":
        p4.close()
        emit_stop()
        st.close()
        return nc

    w1sb = sbuf("w1sb", [128, DT * FF], BF16, p4)
    w2sb = sbuf("w2sb", [128, 32 * DM], BF16, p4)
    b1rb = sbuf("b1rb", [2, FF], BF16, p4)
    b2rb = sbuf("b2rb", [2, DM], BF16, p4)
    x1ld = sbuf("x1ld", [128, 2 * DT * TFN], BF16, p4)
    hb4 = sbuf("hb4", [128, FTK * TFN], BF16, p4)
    y0t = sbuf("y0t", [128, 2 * TFN], F32, p4)
    y0ld = sbuf("y0ld", [128, 2 * TFN], F32, p4)
    x1fld = sbuf("x1fld", [128, 2 * TFN], F32, p4)
    r3t = sbuf("r3t", [128, 2 * DT * TFN], F32, p4)
    sq4 = sbuf("sq4", [128, 2 * TFN], F32, p4)
    sq5 = sbuf("sq5", [128, 2 * TFN], F32, p4)
    sq6 = sbuf("sq6", [128, TFN], F32, p4)
    stat5 = sbuf("stat5", [1, 4 * TFN], F32, p4)
    mur4 = sbuf("mur4", [128, TFN], F32, p4)
    rsr4 = sbuf("rsr4", [128, TFN], F32, p4)
    outt = sbuf("outt", [128, 2 * TFN], F32, p4)
    ps_f = [psum(f"ps_f{i}", [128, 512], stk=p4) for i in range(4)]
    ps_st5 = psum("ps_st5", [128, 1536], stk=p4)

    for slot in range(2):
        @blk.gpsimd
        def _(gp, slot=slot):
            gp.wait_ge(s_dve.h, M["idx"])
            if slot == 1:
                gp.wait_ge(s_pe.h, M[("mm1d", 0, NFN - 1)])
            for kt in range(DT):
                M[("w1g", slot, kt)] = s_sp.inc(gp.indirect_dma_start(
                    out=w1sb[:, kt * FF:(kt + 1) * FF], out_offset=None,
                    in_=w1[:, :],
                    in_offset=IndirectOffsetOnAxis(
                        ap=w1idx[:, slot * 8 + kt:slot * 8 + kt + 1], axis=0)), 16)
            if slot == 1:
                gp.wait_ge(s_pe.h, M[("mm2", 0, NFN - 1, DT - 1)])
            for kt2 in range(32):
                M[("w2g", slot, kt2)] = s_sp.inc(gp.indirect_dma_start(
                    out=w2sb[:, kt2 * DM:(kt2 + 1) * DM], out_offset=None,
                    in_=w2[:, :],
                    in_offset=IndirectOffsetOnAxis(
                        ap=w2idx[:, slot * 32 + kt2:slot * 32 + kt2 + 1], axis=0)), 16)
            M[("b1g", slot)] = s_sp.inc(gp.indirect_dma_start(
                out=b1rb[:], out_offset=None, in_=b1[:, :],
                in_offset=IndirectOffsetOnAxis(ap=ebi[0:2, slot:slot + 1], axis=0)), 16)
            M[("b2g", slot)] = s_sp.inc(gp.indirect_dma_start(
                out=b2rb[:], out_offset=None, in_=b2[:, :],
                in_offset=IndirectOffsetOnAxis(ap=ebi[0:2, slot:slot + 1], axis=0)), 16)

        for fc in range(NFN):
            gidx = slot * NFN + fc

            @blk.sync
            def _(sp, slot=slot, fc=fc, gidx=gidx):
                sp.wait_ge(s_sp.h, M[("x1sp", 3, DT - 1)])
                if gidx >= 2:
                    sp.wait_ge(s_pe.h, M[("mm1d", (gidx - 2) // NFN, (gidx - 2) % NFN)])
                M[("x1ld", gidx)] = s_x.inc(sp.dma_start(
                    out=x1ld[:, (gidx % 2) * DT * TFN:(gidx % 2) * DT * TFN + DT * TFN].rearrange(
                        "p (a t) -> p a t", a=DT),
                    in_=x1b_scr[:, fc * TFN:(fc + 1) * TFN].rearrange(
                        "(a p) t -> p a t", p=128)), 16)

            for ft in range(FTK):
                @blk.tensor
                def _(t, slot=slot, fc=fc, gidx=gidx, ft=ft):
                    if ft == 0:
                        t.wait_ge(s_x.h, M[("x1ld", gidx)])
                        t.wait_ge(s_sp.h, M[("b2g", slot)])
                    xo = (gidx % 2) * DT * TFN
                    pi = ft % 2
                    if ("gelu", slot, fc, ft - 2) in M:
                        t.wait_ge(s_act.h, M[("gelu", slot, fc, ft - 2)])
                    elif ("gelu", slot, fc - 1, FTK - 2 + ft) in M:
                        t.wait_ge(s_act.h, M[("gelu", slot, fc - 1, FTK - 2 + ft)])
                    elif slot == 1 and ("gelu", 0, NFN - 1, FTK - 2 + ft) in M:
                        t.wait_ge(s_act.h, M[("gelu", 0, NFN - 1, FTK - 2 + ft)])
                    for kt in range(DT):
                        t.matmul(ps_f[pi][:, 0:TFN],
                                 w1sb[:, kt * FF + ft * 128:kt * FF + (ft + 1) * 128],
                                 x1ld[:, xo + kt * TFN:xo + (kt + 1) * TFN],
                                 start=(kt == 0), stop=False)
                    mi_ = t.matmul(ps_f[pi][:, 0:TFN], b1rb[0:1, ft * 128:(ft + 1) * 128],
                             ones_tfn_b[:], start=False, stop=True)
                    s_pe.inc(mi_)
                    M[("mm1", slot, fc, ft)] = (s_pe.n, pi)
                    if ft == FTK - 1:
                        M[("mm1d", slot, fc)] = s_pe.n

                @blk.scalar
                def _(a, slot=slot, fc=fc, ft=ft):
                    cnt, pi = M[("mm1", slot, fc, ft)]
                    a.wait_ge(s_pe.h, cnt)
                    ii = a.activation(out=hb4[:, ft * TFN:(ft + 1) * TFN],
                                      in_=ps_f[pi][:, 0:TFN], func=AF.Gelu)
                    s_act.inc(ii)
                    if ft == FTK - 1:
                        afence(a)
                    M[("gelu", slot, fc, ft)] = s_act.n

            for mt in range(DT):
                @blk.tensor
                def _(t, slot=slot, fc=fc, mt=mt):
                    if mt == 0:
                        t.wait_ge(s_act.h, M[("gelu", slot, fc, FTK - 1)])
                    pi = 2 + (mt % 2)
                    if ("ev3", slot, fc, mt - 2) in M:
                        t.wait_ge(s_dve.h, M[("ev3", slot, fc, mt - 2)])
                    elif ("ev3", slot, fc - 1, DT - 2 + mt) in M:
                        t.wait_ge(s_dve.h, M[("ev3", slot, fc - 1, DT - 2 + mt)])
                    elif slot == 1 and ("ev3", 0, NFN - 1, DT - 2 + mt) in M:
                        t.wait_ge(s_dve.h, M[("ev3", 0, NFN - 1, DT - 2 + mt)])
                    for ft in range(FTK):
                        t.matmul(ps_f[pi][:, 0:TFN],
                                 w2sb[:, ft * DM + mt * 128:ft * DM + (mt + 1) * 128],
                                 hb4[:, ft * TFN:(ft + 1) * TFN],
                                 start=(ft == 0), stop=False)
                    mi_ = t.matmul(ps_f[pi][:, 0:TFN], b2rb[0:1, mt * 128:(mt + 1) * 128],
                             ones_tfn_b[:], start=False, stop=True)
                    s_pe.inc(mi_)
                    M[("mm2", slot, fc, mt)] = s_pe.n

                if slot == 0:
                    @blk.vector
                    def _(v, fc=fc, mt=mt):
                        v.wait_ge(s_pe.h, M[("mm2", 0, fc, mt)])
                        if ("y0sp", fc, mt - 1) in M:
                            v.wait_ge(s_sp.h, M[("y0sp", fc, mt - 1)])
                        elif ("y0sp", fc - 1, DT - 1 + mt) in M:
                            v.wait_ge(s_sp.h, M[("y0sp", fc - 1, DT - 1 + mt)])
                        pi = 2 + (mt % 2)
                        ii = v.tensor_copy(y0t[:, (mt % 2) * TFN:(mt % 2) * TFN + TFN],
                                           ps_f[pi][:, 0:TFN])
                        s_dve.inc(ii)
                        M[("ev3", 0, fc, mt)] = s_dve.n

                    @blk.gpsimd
                    def _(gp, fc=fc, mt=mt):
                        gp.wait_ge(s_dve.h, M[("ev3", 0, fc, mt)])
                        M[("y0sp", fc, mt)] = s_sp.inc(gp.dma_start(
                            out=y0_scr[mt * 128:(mt + 1) * 128, fc * TFN:(fc + 1) * TFN],
                            in_=y0t[:, (mt % 2) * TFN:(mt % 2) * TFN + TFN]), 16)
                else:
                    @blk.sync
                    def _(sp, fc=fc, mt=mt):
                        sp.wait_ge(s_sp.h, M[("y0sp", fc, mt)])
                        if ("r3", fc, mt - 2) in M:
                            sp.wait_ge(s_dve.h, M[("r3", fc, mt - 2)])
                        elif ("r3", fc - 1, DT - 2 + mt) in M:
                            sp.wait_ge(s_dve.h, M[("r3", fc - 1, DT - 2 + mt)])
                        M[("y0l2", fc, mt)] = s_x.inc(sp.dma_start(
                            out=y0ld[:, (mt % 2) * TFN:(mt % 2) * TFN + TFN],
                            in_=y0_scr[mt * 128:(mt + 1) * 128, fc * TFN:(fc + 1) * TFN]), 16)
                        M[("x1fl", fc, mt)] = s_x.inc(sp.dma_start(
                            out=x1fld[:, (mt % 2) * TFN:(mt % 2) * TFN + TFN],
                            in_=x1f_scr[mt * 128:(mt + 1) * 128, fc * TFN:(fc + 1) * TFN]), 16)

                    @blk.vector
                    def _(v, fc=fc, mt=mt):
                        v.wait_ge(s_pe.h, M[("mm2", 1, fc, mt)])
                        v.wait_ge(s_x.h, M[("x1fl", fc, mt)])
                        pi = 2 + (mt % 2)
                        mo = (mt % 2) * TFN
                        v.tensor_scalar_mul(sq5[:, 0:TFN], ps_f[pi][:, 0:TFN], wbc[:, 1:2])
                        v.scalar_tensor_tensor(sq5[:, TFN:2 * TFN], y0ld[:, mo:mo + TFN],
                                               wbc[:, 0:1], sq5[:, 0:TFN], ALU.mult, ALU.add)
                        ii = v.tensor_add(r3t[:, (fc % 2) * DT * TFN + mt * TFN:
                                               (fc % 2) * DT * TFN + (mt + 1) * TFN],
                                          sq5[:, TFN:2 * TFN], x1fld[:, mo:mo + TFN])
                        s_dve.inc(ii)
                        M[("ev3", 1, fc, mt)] = s_dve.n
                        M[("r3", fc, mt)] = s_dve.n

            def emit_ln3(fc):
                ro = (fc % 2) * DT * TFN
                emit_ln_stats(("ln3", fc), TFN,
                              lambda dt: r3t[:, ro + dt * TFN:ro + (dt + 1) * TFN],
                              ps_st5, sq4, stat5, mur4, rsr4,
                              gate_act=(s_dve.h, M[("r3", fc, DT - 1)]))

                for dt in range(DT):
                    @blk.vector
                    def _(v, fc=fc, dt=dt, ro=ro):
                        if dt == 0:
                            v.wait_ge(s_dve.h, M[(("ln3", fc), "rsev")])
                        if ("osp", fc, dt - 1) in M:
                            v.wait_ge(s_sp.h, M[("osp", fc, dt - 1)])
                        elif ("osp", fc - 1, DT - 1 + dt) in M:
                            v.wait_ge(s_sp.h, M[("osp", fc - 1, DT - 1 + dt)])
                        v.tensor_sub(sq6[:, 0:TFN], r3t[:, ro + dt * TFN:ro + (dt + 1) * TFN],
                                     mur4[:])
                        ii = v.tensor_mul(outt[:, (dt % 2) * TFN:(dt % 2) * TFN + TFN],
                                          sq6[:, 0:TFN], rsr4[:])
                        s_dve.inc(ii)
                        M[("oev", fc, dt)] = s_dve.n

                    @blk.gpsimd
                    def _(gp, fc=fc, dt=dt):
                        gp.wait_ge(s_dve.h, M[("oev", fc, dt)])
                        M[("osp", fc, dt)] = s_sp.inc(gp.dma_start(
                            out=out_fm[dt * 128:(dt + 1) * 128, fc * TFN:(fc + 1) * TFN],
                            in_=outt[:, (dt % 2) * TFN:(dt % 2) * TFN + TFN]), 16)

            if slot == 1 and fc >= 1:
                emit_ln3(fc - 1)

        if slot == 1:
            emit_ln3(NFN - 1)

    @blk.gpsimd
    def _(gp):
        gp.wait_ge(s_sp.h, s_sp.n)

    barrier()
    p4.close()
    st.close()
    return nc


def _host_inputs(inp):
    x = np.asarray(inp["L0_attn_x"])
    maps = []
    ident = np.eye(128, dtype=np.float32)
    iota = (np.arange(33, dtype=np.float32)[None, :] * 128
            + np.arange(128, dtype=np.float32)[:, None]).astype(np.float32)
    def swz(w, mb):
        K, Mw = w.shape
        a = K // 128
        return np.ascontiguousarray(
            w.reshape(a, 128, Mw // mb, mb).transpose(1, 2, 0, 3).reshape(128, -1))

    shared = {
        "inw": swz(np.asarray(inp["L0_attn_in_proj_w"]), 512),
        "dtw": swz(np.asarray(inp["L0_attn_dt_proj_w"]), 128),
        "outw": swz(np.asarray(inp["L0_attn_out_proj_w"]), 128),
        "xpw": np.ascontiguousarray(np.asarray(inp["L0_attn_x_proj_w"])[:, DS:]),
        "dtb": np.asarray(inp["L0_attn_dt_proj_b"]),
        "convw": np.ascontiguousarray(np.asarray(inp["L0_attn_conv_w"])[:, 0, :]),
        "convb": np.asarray(inp["L0_attn_conv_b"]),
        "lng": np.asarray(inp["L0_attn_ln_gamma"]),
        "lnb": np.asarray(inp["L0_attn_ln_beta"]),
        "wr": np.asarray(inp["L0_ffn_Wr"]),
        "w1": np.asarray(inp["L0_ffn_W1"]).reshape(NE * DM, FF),
        "w2": np.asarray(inp["L0_ffn_W2"]).reshape(NE * FF, DM),
        "b1": np.asarray(inp["L0_ffn_b1"]),
        "b2": np.asarray(inp["L0_ffn_b2"]),
        "ident_in": ident,
        "iota_in": iota,
    }
    for k in range(8):
        b, par = k // 2, k % 2
        lo = par * TH
        xp0 = np.zeros((DM, TH + 3), np.float32)
        xp0[:, 3:] = x[b, lo:lo + TH, :].T
        if par == 1:
            xp0[:, 0:3] = x[b, lo - 3:lo, :].T
        xp = np.stack([xp0[:, j * TF:j * TF + TFP] for j in range(NFC)], axis=0)
        xp = np.ascontiguousarray(
            xp.reshape(NFC, DT, 128, TFP).transpose(2, 0, 1, 3).reshape(128, -1))
        ygidx = np.zeros((128, 2 * CTF), np.uint32)
        for lj in range(2):
            g = par * 2 + lj
            for kt in range(CTF):
                ygidx[:, lj * CTF + kt] = g * DI + kt * 128 + np.arange(128)
        scidx = np.zeros((128, NSC * 2 * CTS), np.uint32)
        for g in range(NSC):
            for jsub in range(2):
                j = (g % 2) * 2 + jsub
                for ct in range(CTS):
                    scidx[:, (g * 2 + jsub) * CTS + ct] = (j * 2 * DI + (g // 2) * DI
                                                           + par * (DI // 2) + ct * 128 + np.arange(128))
        m = dict(shared)
        m.update({
            "x_pad": xp,
            "alog_h": np.ascontiguousarray(np.asarray(inp["L0_attn_A_log"])[par * 1024:(par + 1) * 1024]),
            "d_h": np.ascontiguousarray(np.asarray(inp["L0_attn_D_param"])[par * 1024:(par + 1) * 1024]),
            "ygidx_in": ygidx,
            "scidx_in": scidx,
            "hmask": np.full((128, 1), 1.0 if par == 1 else 0.0, np.float32),
        })
        maps.append(m)
    return maps


def kernel(**inputs):
    nc = build_program(debug=False)
    in_maps = _host_inputs(inputs)
    res = run_bass_kernel_spmd(nc, in_maps, core_ids=list(range(8)))
    out = np.zeros((NB, T, DM), np.float32)
    for k in range(8):
        b, par = k // 2, k % 2
        out[b, par * TH:(par + 1) * TH, :] = res.results[k]["out_fm"].T
    return out


if __name__ == "__main__":
    nc = build_program(debug=False)
    n = sum(1 for _ in nc.all_instructions())
    print("built OK,", n, "instructions")
